# version 34
# baseline (speedup 1.0000x reference)
import contextlib
import numpy as np
import ml_dtypes
import concourse.bass as bass
import concourse.mybir as mybir
from concourse.bass_utils import run_bass_kernel_spmd

F32 = mybir.dt.float32
BF16 = mybir.dt.bfloat16
AF = mybir.ActivationFunctionType
ALU = mybir.AluOpType
AX = mybir.AxisListType

P = 128
SG = 1024
NSG = 2
TCORE = SG * NSG
D = 1024
DCH = D // P
H = 4096
HCH = H // P
HEADS, HS, K9 = 16, 64, 9
NPAIR = HEADS // 2
NTIL = SG // P
LN_EPS = 1e-5
ABS_EPS = 1e-6

ENGS = ("sync", "tensor", "vector", "scalar", "gpsimd")


class Sched:
    def __init__(self):
        self.ops = {e: [] for e in ENGS}
        self.cnt = {e: 0 for e in ENGS}
        self.sems = None

    def emit(self, eng, fn, deps=None, inc=True, at=None):
        waits = []
        if deps:
            for d in deps:
                if d == eng:
                    continue
                c = self.cnt[d]
                if c > 0:
                    waits.append((d, c))
        if at:
            for d, c in at.items():
                if d != eng and c > 0:
                    waits.append((d, c))
        my_inc = None
        if inc:
            self.cnt[eng] += 1
            my_inc = self.cnt[eng]
        sems = lambda: self.sems

        def op(h):
            for (d, c) in waits:
                h.wait_ge(sems()[d], c)
            last = fn(h)
            if my_inc is not None:
                if last is None:
                    last = h.nop()
                last.then_inc(sems()[eng], 1)
        self.ops[eng].append(op)

    def barrier(self, engs=("tensor", "vector", "scalar")):
        snap = dict(self.cnt)
        sems = lambda: self.sems
        for e in engs:
            waits = [(d, snap[d]) for d in engs if d != e and snap[d] > 0]

            def op(h, waits=waits):
                for (d, c) in waits:
                    h.wait_ge(sems()[d], c)
            self.ops[e].append(op)


def build_program(debug_taps=False, skip_ffn=False, paranoid=(), oldrs=False, w0_serial=False, w0_trunc=None):
    nc = bass.Bass("TRN2", target_bir_lowering=False, debug=False)

    xin = nc.dram_tensor("xin", [TCORE + 8, D], F32, kind="ExternalInput").ap()
    wfc = nc.dram_tensor("wfc", [H, D], F32, kind="ExternalInput").ap()
    wproj = nc.dram_tensor("wproj", [D, H], F32, kind="ExternalInput").ap()
    lnw_in = nc.dram_tensor("lnw_d", [P, HCH], F32, kind="ExternalInput").ap()
    lnb_in = nc.dram_tensor("lnb_d", [P, HCH], F32, kind="ExternalInput").ap()
    cw_in = nc.dram_tensor("cwpack", [P, DCH * K9 * 18], F32, kind="ExternalInput").ap()
    identb_in = nc.dram_tensor("identb_d", [P, P], BF16, kind="ExternalInput").ap()
    identf_in = nc.dram_tensor("identf_d", [P, P], F32, kind="ExternalInput").ap()
    onesf_in = nc.dram_tensor("onesf_d", [P, P], F32, kind="ExternalInput").ap()
    selp_in = nc.dram_tensor("selp_d", [16, NPAIR * P], BF16, kind="ExternalInput").ap()
    out = nc.dram_tensor("out", [TCORE, D], F32, kind="ExternalOutput").ap()
    wfct_scr = nc.dram_tensor("wfct_scr", [16, P, 2048], BF16,
                              kind="Internal").ap()

    taps = {}

    def dbg(name, shape, dtype=F32):
        taps[name] = nc.dram_tensor("tap_" + name, shape, dtype,
                                    kind="ExternalOutput").ap()
        return taps[name]

    sch = Sched()
    ctx = contextlib.ExitStack()
    with ctx:
        sb = lambda name, shape, dt: ctx.enter_context(
            nc.sbuf_tensor(name, shape, dt)).ap()

        xall = sb("xall", [P, NTIL * D], F32)
        xt = [xall[:, i * D:(i + 1) * D] for i in range(NTIL)]
        xh = sb("xh", [8, D], F32)
        wpT_all = sb("wpT_all", [P, HCH * D], BF16)
        wpT = [wpT_all[:, i * D:(i + 1) * D] for i in range(HCH)]
        wsum = sb("wsum", [P, 64], F32)
        wfstg = sb("wfstg", [P, D], BF16)
        arena = sb("arena", [P, 32768], BF16)
        gbuf = sb("gbuf", [P, 2048], F32)
        zscr = sb("zscr", [P, H], BF16)
        zT = sb("zT", [P, HCH * P], BF16)
        x2ln_tm2 = sb("x2ln_tm2", [P, 2 * D], BF16)
        x2ln_tm_b = [x2ln_tm2[:, 0:D], x2ln_tm2[:, D:2 * D]]
        x2lnT2 = sb("x2lnT2", [P, 2 * DCH * P], BF16)
        x2lnT_b = [x2lnT2[:, 0:DCH * P], x2lnT2[:, DCH * P:2 * DCH * P]]
        stat = sb("stat", [P, 64], F32)
        accg = sb("accg", [P, 8], F32)
        lnw_t = sb("lnw_t", [P, HCH], F32)
        lnb_t = sb("lnb_t", [P, HCH], F32)
        lnb_b = sb("lnb_b", [P, HCH], BF16)
        cwb = sb("cwb", [P, DCH * K9 * 18], BF16)
        identb = sb("sb_identb", [P, P], BF16)
        identf = sb("sb_identf", [P, P], F32)
        onesf = sb("sb_onesf", [P, P], F32)
        onesb = sb("onesb", [1, P], BF16)
        selp = sb("sb_selp", [16, NPAIR * P], BF16)
        gam = sb("gam", [P, 4], F32)
        epsv = sb("epsv", [P, 1], F32)
        cvec = sb("cvec", [1, D], BF16)
        scol = sb("scol", [P, 1], BF16)
        srow = sb("srow", [1, P], BF16)

        wfT = [arena[:, c * 4096:(c + 1) * 4096] for c in range(DCH)]
        xlnT = [arena[:, c * 1040: c * 1040 + SG + 8] for c in range(DCH)]
        xln_tm = arena[:, 8320:9344]
        xln_hm = arena[0:8, 9344:10368]
        cklF = arena[:, 10368:16512].bitcast(F32)
        ckl_tm = [arena[:, 16512 + i * 768: 16512 + (i + 1) * 768].bitcast(F32)
                  for i in range(NTIL)]
        wsm_tm = arena[:, 22656:22800]
        wsmK = [arena[0:16, 22800 + k * SG: 22800 + (k + 1) * SG]
                for k in range(K9)]
        gmax = arena[:, 32016:32048].bitcast(F32)
        gsum = arena[:, 32048:32080].bitcast(F32)
        grec = arena[:, 32080:32112].bitcast(F32)
        prods = [arena[:, 10368 + j * 512: 10368 + (j + 1) * 512]
                 for j in range(5)]
        wsb = [arena[:, 10368 + (5 + m) * 512: 10368 + (6 + m) * 512]
               for m in range(4)]
        apair = arena[:, 14976:16000]
        wtile2b = [gbuf, zT.bitcast(F32),
                   wpT_all[:, 0:4096].bitcast(F32),
                   wpT_all[:, 4096:8192].bitcast(F32),
                   wpT_all[:, 8192:12288].bitcast(F32),
                   wpT_all[:, 12288:16384].bitcast(F32),
                   wpT_all[:, 16384:20480].bitcast(F32),
                   wpT_all[:, 20480:24576].bitcast(F32)]
        sgnA2 = zscr.bitcast(F32)
        wfstg2 = x2lnT2
        wqn = x2ln_tm2
        lnsq = wfstg.bitcast(F32)

        psv = lambda name, shape, dt: ctx.enter_context(
            nc.psum_tensor(name, shape, dt)).ap()
        ps_big = psv("ps_big", [P, 3072], F32)
        ps_tA = psv("ps_tA", [P, 512], F32)
        ps_tB = psv("ps_tB", [P, 512], F32)

        ps_fc = ps_big[:, 0:2048]
        ps_proj = ps_big[:, 2048:3072]
        ps_ckl = [ps_big[:, 0:512], ps_big[:, 512:1024], ps_big[:, 1024:1536]]
        ps_wb = [ps_big[:, 1536:2048], ps_big[:, 2048:2560]]
        ps_c = ps_big[0:1, 2048:3072]
        ps_bc = ps_big[:, 2560:2561]
        ps_tAb = ps_tA[:, 0:256].bitcast(BF16)
        ps_w45 = [ps_big[:, 2048:2560].bitcast(BF16),
                  ps_big[:, 2560:3072].bitcast(BF16)]
        ps_tAw = ps_tA.bitcast(BF16)
        ps_zt = [ps_big[:, b * 512:(b + 1) * 512].bitcast(BF16)
                 for b in range(4)]
        ps_tBb = ps_tB[:, 0:256].bitcast(BF16)
        ps_tm = ps_tB[:, 0:128]

        sems = {e: ctx.enter_context(nc.semaphore("s_" + e)) for e in ENGS}
        sch.sems = sems
        s_ld = ctx.enter_context(nc.semaphore("s_ld"))
        s_ld2 = ctx.enter_context(nc.semaphore("s_ld2"))
        s_lda = ctx.enter_context(nc.semaphore("s_lda"))
        s_ldb = ctx.enter_context(nc.semaphore("s_ldb"))
        s_st = ctx.enter_context(nc.semaphore("s_st"))
        s_w0v = ctx.enter_context(nc.semaphore("s_w0v"))
        s_w0s = ctx.enter_context(nc.semaphore("s_w0s"))
        s_w0t = ctx.enter_context(nc.semaphore("s_w0t"))
        s_w0e = ctx.enter_context(nc.semaphore("s_w0e"))
        s_ldc = ctx.enter_context(nc.semaphore("s_ldc"))
        s_ldd = ctx.enter_context(nc.semaphore("s_ldd"))
        s_lde = ctx.enter_context(nc.semaphore("s_lde"))
        s_ldf = ctx.enter_context(nc.semaphore("s_ldf"))
        s_ldg = ctx.enter_context(nc.semaphore("s_ldg"))
        s_ldh = ctx.enter_context(nc.semaphore("s_ldh"))
        ld_groups = {"main": [s_ld, 0], "small": [s_ld2, 0],
                     "wa": [s_lda, 0], "wb": [s_ldb, 0],
                     "wc": [s_ldc, 0], "wd": [s_ldd, 0],
                     "we": [s_lde, 0], "wf": [s_ldf, 0],
                     "wg": [s_ldg, 0], "wh": [s_ldh, 0]}
        WGRP = ("wa", "wb", "wc", "wd", "we", "wf", "wg", "wh")
        st_cnt = [0]

        def load(dst, src, deps=None, group="main"):
            g = ld_groups[group]
            g[1] += 1
            tgt = g[1] * 16
            sem = g[0]
            waits = [(d, sch.cnt[d]) for d in (deps or []) if sch.cnt[d] > 0]

            def op(h):
                for (d, c) in waits:
                    h.wait_ge(sems[d], c)
                h.dma_start(out=dst, in_=src).then_inc(sem, 16)
            sch.ops["sync"].append(op)
            return tgt

        def store(dst, src, deps=None):
            st_cnt[0] += 1
            tgt = st_cnt[0] * 16
            waits = [(d, sch.cnt[d]) for d in (deps or []) if sch.cnt[d] > 0]

            def op(h):
                for (d, c) in waits:
                    h.wait_ge(sems[d], c)
                h.dma_start(out=dst, in_=src).then_inc(s_st, 16)
            sch.ops["sync"].append(op)
            return tgt

        def wait_ld(eng, tgt, group="main"):
            sem = ld_groups[group][0]
            sch.ops[eng].append(lambda h: h.wait_ge(sem, tgt))

        def wait_ld2(eng, tgt):
            wait_ld(eng, tgt, group="small")

        def wait_st(eng, tgt):
            sch.ops[eng].append(lambda h: h.wait_ge(s_st, tgt))

        def dbg_store(dst, src):
            tgt = store(dst, src, deps=["vector", "scalar", "tensor"])
            for e in ("vector", "scalar", "tensor"):
                wait_st(e, tgt)

        E = sch.emit

        load(identb, identb_in[:, :])
        load(identf, identf_in[:, :])
        load(onesf, onesf_in[:, :])
        t_selp = load(selp, selp_in[:, :], group="small")
        load(lnw_t, lnw_in[:, :])
        load(lnb_t, lnb_in[:, :])
        cw_f = gbuf[:, 0:DCH * K9 * 18]
        t1 = load(cw_f, cw_in[:, :])
        wait_ld("vector", t1)
        E("vector", lambda v: v.tensor_copy(cwb, cw_f))
        E("vector", lambda v: v.tensor_copy(lnb_b, lnb_t))
        E("vector", lambda v: v.tensor_copy(onesb, onesf[0:1, :]))
        E("vector", lambda v: v.memset(epsv, LN_EPS))

        w0_steps = []
        w0_state = {"last_st": 0}
        gscr = zscr.bitcast(F32)[:, 0:1024]
        w0c = {"v": 0, "s": 0, "t": 0, "sgn": {}, "tt": {}, "tp": {},
               "ev": {0: 0, 1: 0}, "gamma_v0": 0, "gamma_v1": 0}

        def w0_chunks():
            out = []
            for i in range(16):
                src = wfc.rearrange("(a p) d -> p (a d)", p=P)[
                    :, 0:2048] if False else None
                out.append(("fc", i))
            for i in range(16):
                out.append(("pj", i))
            return out

        def w0_src(kind, i):
            if kind == "fc":
                return (wfc[256 * i: 256 * (i + 1), :]
                        .rearrange("(a p) d -> p a d", p=P))
            q0 = 2 * i
            rb = q0 // 4
            qa, qb = q0 % 4, (q0 + 1) % 4
            return (wproj[rb * P:(rb + 1) * P, qa * D:(qb + 1) * D]
                    .rearrange("p (a d) -> p a d", d=D))

        def w0_load(dst, src, grp, s_gate=0, v_gate=0, queue="sync"):
            g = ld_groups[grp]
            g[1] += 1
            tgt = g[1] * 16

            def op(h, tgt=tgt, s_gate=s_gate, v_gate=v_gate, g=g):
                if s_gate > 0:
                    h.wait_ge(s_w0s, s_gate)
                if v_gate > 0:
                    h.wait_ge(s_w0v, v_gate)
                h.dma_start(out=dst, in_=src).then_inc(g[0], 16)
            sch.ops[queue].append(op)
            return tgt

        FCB = (0, 1, 6, 7)
        PJB = (2, 3, 4, 5)

        def gamma_step(j, kind, i, bsel):
            def go():
                bmap = FCB if bsel == 0 else PJB
                bi = bmap[j % 4]
                buf = wtile2b[bi]
                grp = (WGRP[{0: 0, 1: 1, 6: 2, 7: 3}[bi]] if bsel == 0
                       else WGRP[4 + PJB.index(bi)])
                tgt = w0_load(buf.rearrange("p (a d) -> p a d", d=D),
                              w0_src(kind, i), grp,
                              s_gate=w0c["sgn"].get(("g", bsel, j - 4), 0),
                              queue=("gpsimd" if bsel else "sync"))
                w0c["s"] += 1
                w0c["sgn"][("g", bsel, j)] = w0c["s"]

                def op(h, tgt=tgt, j=j, grp=grp, bi=bi):
                    h.wait_ge(ld_groups[grp][0], tgt)
                    h.activation(wtile2b[bi][:, 0:1024],
                                 wtile2b[bi][:, 0:1024], AF.Abs,
                                 accum_out=wsum[:, 2 * j:2 * j + 1])
                    h.activation(wtile2b[bi][:, 1024:2048],
                                 wtile2b[bi][:, 1024:2048], AF.Abs,
                                 accum_out=wsum[:, 2 * j + 1:2 * j + 2]
                                 ).then_inc(s_w0s, 1)
                sch.ops["scalar"].append(op)
            return go

        def gamma_final(which, sgate_needed):
            def go():
                red = stat[:, 60 + which:61 + which]
                w0c["v"] += 0
                def opv(h, sg_=sgate_needed(), which=which, red=red):
                    h.wait_ge(s_w0s, sg_)
                    h.reduce_sum(red, wsum[:, 32 * which:32 * which + 32],
                                 axis=AX.X)
                    h.drain().then_inc(s_w0v, 1)
                w0c["v"] += 1
                vred = w0c["v"]
                sch.ops["vector"].append(opv)
                nelem = float(H * D)
                w0c["t"] += 1
                tmark = w0c["t"]

                evg = w0c["ev"][0]

                def opt(h, red=red, vred=vred, evg=evg):
                    h.wait_ge(s_w0v, vred)
                    if evg > 0:
                        h.wait_ge(s_w0s, evg)
                    h.matmul(ps_bc, onesf, red, start=True,
                             stop=True).then_inc(s_w0t, 1)
                sch.ops["tensor"].append(opt)
                w0c["v"] += 1

                def opv2(h, tmark=tmark, which=which):
                    h.wait_ge(s_w0t, tmark)
                    c0 = 2 * which
                    h.tensor_scalar(gam[:, c0:c0 + 1], ps_bc,
                                    0.5 / nelem, 0.5 * ABS_EPS,
                                    op0=ALU.mult, op1=ALU.add)
                    h.drain()
                    h.tensor_scalar(gam[:, c0 + 1:c0 + 2],
                                    gam[:, c0:c0 + 1], -1.0, None,
                                    op0=ALU.mult)
                    h.drain().then_inc(s_w0v, 1)
                sch.ops["vector"].append(opv2)
                w0c["gamma_v%d" % which] = w0c["v"]
            return go

        def tern_step(jj, kind, i):
            def go():
                j = jj
                if kind == "fc":
                    bi = FCB[jj % 4]
                    pv = jj - 4
                else:
                    bi = (0, 1)[jj % 2]
                    pv = jj - 4 if jj < 18 else jj - 2
                grp = WGRP[{0: 0, 1: 1, 6: 2, 7: 3}[bi]]
                buf = wtile2b[bi]
                sg_ = w0c["sgn"].get(("t", pv))
                if sg_ is None:
                    sg_ = w0c["sgn"][("g", 0, 12 + jj)]
                tgt = w0_load(buf.rearrange("p (a d) -> p a d", d=D),
                              w0_src(kind, i), grp, s_gate=sg_,
                              v_gate=w0c["tt"].get(pv, 0))
                c0 = 0 if kind == "fc" else 2
                gv = "gamma_v%d" % (0 if kind == "fc" else 1)
                w0c["s"] += 2
                w0c["sgn"][("t", jj)] = w0c["s"]
                vgate_sgnA = w0c["tt"].get(jj - 1, 0)

                def ops_act(h, tgt=tgt, j=j, c0=c0, grp=grp, vg=vgate_sgnA,
                            gv=gv, bi=bi):
                    h.wait_ge(ld_groups[grp][0], tgt)
                    h.wait_ge(s_w0v, max(vg, w0c[gv]))
                    h.activation(sgnA2, wtile2b[bi], AF.Sign,
                                 bias=gam[:, c0 + 1:c0 + 2],
                                 scale=1.0).then_inc(s_w0s, 1)
                    h.activation(wtile2b[bi], wtile2b[bi], AF.Sign,
                                 bias=gam[:, c0:c0 + 1],
                                 scale=1.0).then_inc(s_w0s, 1)
                sch.ops["scalar"].append(ops_act)
                if w0c.get("pend_st") is not None:
                    sch.ops["scalar"].append(w0c.pop("pend_st"))
                w0c["v"] += 1
                w0c["tt"][jj] = w0c["v"]
                smark = w0c["s"]
                tpgate = w0c["tp"].get(jj - 1, 0)

                def opv(h, j=j, smark=smark, tg=tpgate, bi=bi):
                    h.wait_ge(s_w0s, smark)
                    if tg > 0:
                        h.wait_ge(s_w0t, tg)
                    h.tensor_tensor(wqn[:, 0:2048], sgnA2, wtile2b[bi],
                                    op=ALU.add).then_inc(s_w0v, 1)
                sch.ops["vector"].append(opv)
                vmark = w0c["v"]
                w0c["t"] += 1
                evgate = w0c["ev"][0]

                def opt(h, vmark=vmark, evgate=evgate):
                    h.wait_ge(s_w0v, vmark)
                    if evgate > 0:
                        h.wait_ge(s_w0s, evgate)
                    for a in range(2):
                        for c in range(DCH):
                            ins = h.transpose(
                                ps_w45[a][:, c * P:(c + 1) * P],
                                wqn[:, a * 1024 + c * P:
                                    a * 1024 + (c + 1) * P], identb)
                    ins.then_inc(s_w0t, 1)
                sch.ops["tensor"].append(opt)
                tmark = w0c["t"]
                stg = w0c.get("last_fc_st", 0) if kind == "fc" else 0
                for a in range(2):
                    if kind == "fc":
                        edst = wfstg2[:, a * 1024:(a + 1) * 1024]
                        esrc = ps_w45[a]
                    else:
                        qq = (2 * i + a) % 4
                        hc0 = qq * 8
                        rb = (2 * i + a) // 4
                        edst = wpT_all.rearrange("p (h r) -> p h r", r=D)[
                            :, hc0:hc0 + 8, rb * P: rb * P + P]
                        esrc = ps_w45[a].rearrange("p (x b) -> p x b", b=P)
                    w0c["s"] += 1

                    def ope(h, d=edst, sr=esrc, tmark=tmark, stg=stg,
                            a=a):
                        h.wait_ge(s_w0t, tmark)
                        if a == 0 and stg > 0:
                            h.wait_ge(s_st, stg)
                        h.tensor_copy(d, sr).then_inc(s_w0s, 1)
                    sch.ops["vector"].append(ope)
                w0c["ev"][0] = w0c["s"]
                w0c["tp"][jj] = w0c["t"]
                if kind == "fc":
                    w0c["fcev"] = w0c.get("fcev", 0) + 1
                    fcev = w0c["fcev"]
                    sch.ops["vector"].append(
                        lambda h: h.nop().then_inc(s_w0e, 1))
                    st_cnt[0] += 1

                    def opst(h, i=i, fcev=fcev):
                        h.wait_ge(s_w0e, fcev)
                        h.dma_start(out=wfct_scr[i, :, :],
                                    in_=wfstg2).then_inc(s_st, 16)
                    w0c["pend_st"] = opst
                    w0c["last_fc_st"] = st_cnt[0] * 16
                    w0_state["last_st"] = st_cnt[0] * 16
            return go

        for j in range(16):
            w0_steps.append(gamma_step(j, "fc", j, 0))
        w0_steps.append(gamma_final(0, lambda: w0c["sgn"][("g", 0, 15)]))
        for j in range(16):
            w0_steps.append(gamma_step(16 + j, "pj", j, 2))
            w0_steps.append(tern_step(j, "fc", j))
        w0_steps.append(gamma_final(1, lambda: w0c["sgn"][("g", 2, 31)]))
        for j in range(16):
            w0_steps.append(tern_step(16 + j, "pj", j))

        def w0_post():
            if w0c.get("pend_st") is not None:
                sch.ops["scalar"].append(w0c.pop("pend_st"))
            sv, ss, st_ = w0c["v"], w0c["s"], w0c["t"]
            for e, sem, c in (("tensor", s_w0s, ss), ("vector", s_w0s, ss),
                              ("scalar", s_w0v, sv)):
                sch.ops[e].append(lambda h, sem=sem, c=c: h.wait_ge(sem, c))
            for w in range(2):
                for hc in range(HCH):
                    E("tensor", lambda t, w=w, hc=hc: t.matmul(
                        ps_c[:, w * 512:(w + 1) * 512], lnb_b[:, hc:hc + 1],
                        wpT[hc][:, w * 512:(w + 1) * 512],
                        start=(hc == 0), stop=(hc == HCH - 1)),
                        deps=(["vector", "scalar"] if hc == 0 else None))
            E("vector", lambda v: v.tensor_scalar(
                cvec, ps_c, 0.5, None, op0=ALU.mult), deps=["tensor"])
            for hc in range(HCH):
                E("vector", lambda v, hc=hc: v.tensor_scalar(
                    wpT[hc], wpT[hc], lnw_t[:, hc:hc + 1], None, op0=ALU.mult))

        def drip(n=4):
            for _ in range(n):
                if w0_steps:
                    w0_steps.pop(0)()

        def newton_rs(v_ap, rs_ap, seed, iters, tmp_base=56, rows=P):
            tmp = stat[0:rows, tmp_base:tmp_base + (v_ap.shape[1])]
            E("vector", lambda v: v.memset(rs_ap, seed))
            for _ in range(iters):
                E("vector", lambda v: v.drain())
                E("vector", lambda v: v.tensor_tensor(tmp, rs_ap, rs_ap,
                                                      op=ALU.mult))
                E("vector", lambda v: v.drain())
                E("vector", lambda v: v.tensor_tensor(tmp, tmp, v_ap,
                                                      op=ALU.mult))
                E("vector", lambda v: v.drain())
                E("vector", lambda v: v.tensor_scalar(
                    tmp, tmp, -0.5, 1.5, op0=ALU.mult, op1=ALU.add))
                E("vector", lambda v: v.drain())
                E("vector", lambda v: v.tensor_tensor(rs_ap, rs_ap, tmp,
                                                      op=ALU.mult))
            E("vector", lambda v: v.drain())

        def ln_collect(i, x_ap, deps=None):
            E("vector", lambda v, i=i, x_ap=x_ap: v.reduce_sum(
                stat[:, 16 + i:17 + i], x_ap, axis=AX.X), deps=deps)
            E("vector", lambda v: v.drain())
            E("scalar", lambda s, i=i, x_ap=x_ap: s.activation(
                lnsq, x_ap[:, 0:512], AF.Square,
                accum_out=stat[:, 24 + i:25 + i]), deps=["vector"])
            E("scalar", lambda s, i=i, x_ap=x_ap: s.activation(
                lnsq, x_ap[:, 512:1024], AF.Square,
                accum_out=stat[:, 32 + i:33 + i]))

        def ln_finish(mode, seed=0.6, iters=8, split_sums=False):
            n = NTIL
            mu = stat[:, 0:n]
            rs = stat[:, 8:8 + n]
            var = stat[:, 48:48 + n]
            scr = stat[:, 16:16 + n]
            E("vector", lambda v: v.drain())
            if split_sums:
                E("vector", lambda v: v.tensor_tensor(
                    stat[:, 16:16 + n], stat[:, 16:16 + n],
                    stat[:, 56:56 + n], op=ALU.add), deps=["scalar"])
                E("vector", lambda v: v.drain())
            E("vector", lambda v: v.tensor_scalar(
                mu, stat[:, 16:16 + n], 1.0 / D, None, op0=ALU.mult),
                deps=(None if split_sums else ["scalar"]))
            E("vector", lambda v: v.tensor_tensor(
                var, stat[:, 24:24 + n], stat[:, 32:32 + n], op=ALU.add))
            E("vector", lambda v: v.drain())
            E("vector", lambda v: v.tensor_scalar(
                var, var, 1.0 / D, None, op0=ALU.mult))
            E("vector", lambda v: v.tensor_tensor(scr, mu, mu, op=ALU.mult))
            E("vector", lambda v: v.drain())
            E("vector", lambda v: v.tensor_tensor(var, var, scr,
                                                  op=ALU.subtract))
            E("vector", lambda v: v.drain())
            if mode == "A":
                E("scalar", lambda s: s.activation(
                    scr, var, AF.Ln, bias=epsv, scale=1.0), deps=["vector"])
                E("scalar", lambda s: s.activation(
                    rs, scr, AF.Exp, scale=-0.5))
            else:
                E("vector", lambda v: v.tensor_scalar(
                    var, var, 1.0, LN_EPS, op0=ALU.mult, op1=ALU.add))
                newton_rs(var, rs, seed, iters, tmp_base=24)

        def xlnT_quad(c0, tok0):
            a0 = c0 * 1040 + 4 + tok0
            return (arena[:, a0: a0 + 4 * 1040]
                    .rearrange("p (c r) -> p c r", r=1040)[:, :, 0:P])

        if w0_trunc is not None:
            del w0_steps[w0_trunc:]
        if w0_serial:
            drip(len(w0_steps))
            if w0_trunc is None:
                w0_post()
        def keepwarm(sg):
            pass

        def halo_ln(mode, gate_t=None):
            E("vector", lambda v: v.reduce_sum(
                stat[0:8, 44:45], xh, axis=AX.X))
            E("scalar", lambda s: s.activation(
                lnsq[0:8, :], xh[0:8, 0:512], AF.Square,
                accum_out=stat[0:8, 45:46]), deps=["vector"])
            E("scalar", lambda s: s.activation(
                lnsq[0:8, :], xh[0:8, 512:1024], AF.Square,
                accum_out=stat[0:8, 46:47]))
            E("vector", lambda v: v.tensor_tensor(
                stat[0:8, 45:46], stat[0:8, 45:46], stat[0:8, 46:47],
                op=ALU.add), deps=["scalar"])
            E("vector", lambda v: v.drain())
            E("vector", lambda v: v.tensor_scalar(
                stat[0:8, 44:45], stat[0:8, 44:45], 1.0 / D, None,
                op0=ALU.mult))
            E("vector", lambda v: v.tensor_scalar(
                stat[0:8, 45:46], stat[0:8, 45:46], 1.0 / D, None,
                op0=ALU.mult))
            E("vector", lambda v: v.drain())
            E("vector", lambda v: v.tensor_tensor(
                stat[0:8, 46:47], stat[0:8, 44:45], stat[0:8, 44:45],
                op=ALU.mult))
            E("vector", lambda v: v.drain())
            E("vector", lambda v: v.tensor_tensor(
                stat[0:8, 45:46], stat[0:8, 45:46], stat[0:8, 46:47],
                op=ALU.subtract))
            E("vector", lambda v: v.drain())
            if mode == "act":
                E("scalar", lambda s: s.activation(
                    stat[0:8, 47:48], stat[0:8, 45:46], AF.Ln,
                    bias=epsv[0:8, :], scale=1.0), deps=["vector"])
                E("scalar", lambda s: s.activation(
                    stat[0:8, 45:46], stat[0:8, 47:48], AF.Exp, scale=-0.5))
                rs_c = 45
            else:
                E("vector", lambda v: v.tensor_scalar(
                    stat[0:8, 45:46], stat[0:8, 45:46], 1.0, LN_EPS,
                    op0=ALU.mult, op1=ALU.add))
                E("vector", lambda v: v.drain())
                newton_rs(stat[0:8, 45:46], stat[0:8, 47:48], 0.6, 6,
                          tmp_base=24, rows=8)
                rs_c = 47
            E("vector", lambda v, rs_c=rs_c: v.tensor_scalar(
                xln_hm, xh, stat[0:8, 44:45], stat[0:8, rs_c:rs_c + 1],
                op0=ALU.subtract, op1=ALU.mult),
                deps=(["scalar"] if mode == "act" else None),
                at=({"tensor": gate_t} if gate_t else None))

        def a_prep(gate_t=None, dodrip=False):
            ee = "vector"
            for i in range(NTIL):
                E("vector", lambda v, i=i: v.tensor_scalar(
                    xln_tm, xt[i], stat[:, i:i + 1], stat[:, 8 + i:9 + i],
                    op0=ALU.subtract, op1=ALU.mult),
                    deps=(["scalar"] if i == 0 else None),
                    at=({"tensor": gate_t} if (gate_t and i == 0) else None))
                for c4 in range(2):
                    pst = ps_tAb if (c4 == 0) else ps_tBb
                    for cj in range(4):
                        c = c4 * 4 + cj
                        E("tensor", lambda t, c=c, cj=cj, p=pst: t.transpose(
                            p[:, cj * P:(cj + 1) * P],
                            xln_tm[:, c * P:(c + 1) * P], identb),
                            deps=((["vector"] if ee == "vector"
                                   else ["vector", "scalar"])
                                  if cj == 0 else None))
                    if ee == "vector":
                        E("vector", lambda v, c4=c4, i=i, p=pst:
                          v.tensor_copy(xlnT_quad(c4 * 4, i * P), p[:]),
                          deps=["tensor"])
                    else:
                        E("scalar", lambda sc, c4=c4, i=i, p=pst:
                          sc.copy(xlnT_quad(c4 * 4, i * P), p[:]),
                          deps=["tensor"])
                if dodrip:
                    drip(3)
            for c in range(DCH):
                pst = ps_tAb if (c % 2 == 0) else ps_tBb
                E("tensor", lambda t, c=c, p=pst: t.transpose(
                    p[:, 0:8], xln_hm[:, c * P:(c + 1) * P],
                    identb[0:8, 0:8]),
                    deps=(["vector"] if ee == "vector"
                          else ["vector", "scalar"]))
                if ee == "vector":
                    E("vector", lambda v, c=c, p=pst: v.tensor_copy(
                        xlnT[c][:, 0:4], p[:, 0:4]), deps=["tensor"])
                    E("vector", lambda v, c=c, p=pst: v.tensor_copy(
                        xlnT[c][:, SG + 4:SG + 8], p[:, 4:8]))
                else:
                    E("scalar", lambda sc, c=c, p=pst: sc.copy(
                        xlnT[c][:, 0:4], p[:, 0:4]), deps=["tensor"])
                    E("scalar", lambda sc, c=c, p=pst: sc.copy(
                        xlnT[c][:, SG + 4:SG + 8], p[:, 4:8]))
            E("vector", lambda v: v.memset(cklF, 0.0),
              at=({"tensor": gate_t} if gate_t else None))
            if dodrip:
                drip(3)

        ov = {}

        def ov_tile(j):
            g = ld_groups[WGRP[j]]
            g[1] += 1
            tgt_o = g[1] * 16
            vc = sch.cnt["vector"]

            def opst(h, j=j, g=g, vc=vc):
                if vc > 0:
                    h.wait_ge(sems["vector"], vc)
                h.dma_start(out=out[j * P:(j + 1) * P, :],
                            in_=xt[j]).then_inc(g[0], 16)
            sch.ops["sync"].append(opst)
            g[1] += 1
            tgt = g[1] * 16

            def opld(h, tgt_o=tgt_o, j=j, g=g):
                h.wait_ge(g[0], tgt_o)
                h.dma_start(out=xt[j],
                            in_=xin[SG + 4 + j * P: SG + 4 + (j + 1) * P, :]
                            ).then_inc(g[0], 16)
            sch.ops["sync"].append(opld)
            ov[("xt", j)] = tgt

        def ov_collect(j):
            wait_ld("vector", ov[("xt", j)], group=WGRP[j])
            ln_collect(j, xt[j])

        for sg in range(NSG):
            base = sg * SG
            if sg == 1:
                sch.barrier()
            if sg == 0:
                tloads = []
                for i in range(NTIL):
                    tloads.append(load(xt[i], xin[base + 4 + i * P:
                                                  base + 4 + (i + 1) * P, :],
                                       deps=["vector", "tensor", "scalar"]))
                th_l = load(xh[0:4, :], xin[base + 0: base + 4, :],
                            group="small")
                th_r = load(xh[4:8, :], xin[base + SG + 4: base + SG + 8, :],
                            group="small")

                wait_ld("vector", tloads[-1])
                for i in range(NTIL):
                    ln_collect(i, xt[i])
                    drip(3)
                    keepwarm(sg)
                ln_finish("A")
                drip(3)
                if debug_taps:
                    tls = dbg("ln1stat", [P, 64], F32)
                    dbg_store(tls[:, :], stat)
                wait_ld2("vector", th_l)
                wait_ld2("vector", th_r)
                halo_ln("act")
                a_prep(dodrip=True)
                sch.barrier()

            if debug_taps and sg == 0:
                tpx = dbg("xlnT0", [P, SG + 8], BF16)
                store(tpx[:, :], xlnT[0], deps=["vector"])

            for half in range(2):
                for g in range(3):
                    ps = ps_ckl[g]
                    ncc = 3 if g < 2 else 2
                    for cc in range(ncc):
                        c = 3 * g + cc
                        off = 32 * cc
                        for k in range(K9):
                            E("tensor", lambda t, c=c, k=k, off=off, ps=ps,
                              half=half: t.matmul(
                                  ps[off:off + 18, :],
                                  cwb[:, (c * K9 + k) * 18:
                                      (c * K9 + k) * 18 + 18],
                                  xlnT[c][:, half * 512 + k:
                                          half * 512 + k + 512],
                                  start=(k == 0), stop=(k == K9 - 1)),
                              deps=(["vector", "scalar"] if (cc == 0 and k == 0)
                                    else None))
                    for cc in range(ncc):
                        off = 32 * cc
                        E("scalar", lambda sc, ps=ps, off=off, g=g, half=half:
                          sc.copy(
                              cklF[off:off + 18,
                                   g * SG + half * 512: g * SG + half * 512 + 512],
                              ps[off:off + 18, :]),
                          deps=(["tensor"] if cc == 0 else None))
                    drip(3)
            sch.barrier()

            if debug_taps and sg == 0:
                tpk = dbg("cklF", [P, 3 * SG], F32)
                store(tpk[:, :], cklF, deps=["vector"])

            sm_ev = {}
            for i in range(NTIL):
                for g in range(3):
                    slot = (i * 3 + g) % 4
                    pst = ps_zt[slot][:, 0:256].bitcast(F32)
                    at_t = ({"vector": sm_ev[slot]} if slot in sm_ev
                            else None)
                    E("tensor", lambda t, i=i, g=g, pst=pst: t.transpose(
                        pst, cklF[:, g * SG + i * P: g * SG + (i + 1) * P],
                        identf), deps=(["vector"] if (i == 0 and g == 0)
                                       else None), at=at_t)
                    E("vector", lambda v, i=i, g=g, pst=pst: v.tensor_copy(
                        ckl_tm[i][:, g * P:(g + 1) * P], pst),
                        deps=["tensor"])
                    sm_ev[slot] = sch.cnt["vector"]
                    keepwarm(sg)
                for g in range(3):
                    ncc = 3 if g < 2 else 2
                    x4 = (ckl_tm[i][:, g * P: g * P + 32 * ncc]
                          .rearrange("p (cc r) -> p cc r", r=32)[:, :, 0:18]
                          .rearrange("p cc (hh j) -> p cc hh j", j=K9))
                    gm = gmax[:, 0:2 * ncc].rearrange("p (cc hh) -> p cc hh", hh=2)
                    E("vector", lambda v, x4=x4, gm=gm: v.tensor_reduce(
                        gm, x4, axis=AX.X, op=ALU.max))
                    E("vector", lambda v: v.drain())
                    E("vector", lambda v, x4=x4, gm=gm, ncc=ncc: v.tensor_tensor(
                        x4, x4, gm.unsqueeze(3).broadcast_to([P, ncc, 2, K9]),
                        op=ALU.subtract))
                E("scalar", lambda s, i=i: s.activation(
                    ckl_tm[i], ckl_tm[i], AF.Exp), deps=["vector"])
                for g in range(3):
                    ncc = 3 if g < 2 else 2
                    x4 = (ckl_tm[i][:, g * P: g * P + 32 * ncc]
                          .rearrange("p (cc r) -> p cc r", r=32)[:, :, 0:18]
                          .rearrange("p cc (hh j) -> p cc hh j", j=K9))
                    gs = gsum[:, 0:2 * ncc].rearrange("p (cc hh) -> p cc hh", hh=2)
                    gr = grec[:, 0:2 * ncc]
                    grv = gr.rearrange("p (cc hh) -> p cc hh", hh=2)
                    wout = (wsm_tm[:, 54 * g: 54 * g + 18 * ncc]
                            .rearrange("p (cc hh j) -> p cc hh j", hh=2, j=K9))
                    E("vector", lambda v, x4=x4, gs=gs: v.tensor_reduce(
                        gs, x4, axis=AX.X, op=ALU.add),
                        deps=(["scalar"] if g == 0 else None))
                    E("vector", lambda v: v.drain())
                    E("vector", lambda v, gr=gr, ncc=ncc: v.reciprocal(
                        gr, gsum[:, 0:2 * ncc]))
                    E("vector", lambda v: v.drain())
                    E("vector", lambda v, x4=x4, grv=grv, wout=wout, ncc=ncc:
                      v.tensor_tensor(
                          wout, x4,
                          grv.unsqueeze(3).broadcast_to([P, ncc, 2, K9]),
                          op=ALU.mult))
                    keepwarm(sg)
                wsm3 = wsm_tm.rearrange("p (h k) -> p h k", k=K9)
                for k in range(K9):
                    pst = ps_tAw if k < 8 else ps_tBb
                    off = (k % 8) * P
                    E("tensor", lambda t, k=k, wsm3=wsm3, p=pst, off=off:
                      t.transpose(p[0:16, off:off + P], wsm3[:, :, k], identb),
                      deps=(["vector"] if k in (0, 8) else None))
                wk8 = (arena[0:16, 22800: 22800 + 8 * SG]
                       .rearrange("p (k r) -> p k r", r=SG)
                       [:, :, i * P:(i + 1) * P])
                E("vector", lambda v, i=i, wk8=wk8: v.tensor_copy(
                    wk8, ps_tAw[0:16, :].rearrange("p (k r) -> p k r", r=P)),
                    deps=["tensor"])
                E("vector", lambda v, i=i: v.tensor_copy(
                    wsmK[8][:, i * P:(i + 1) * P], ps_tBb[0:16, 0:P]))
                drip(4)
            sch.barrier()

            if debug_taps and sg == 0:
                tpw = dbg("wsmK0", [16, SG], BF16)
                store(tpw[:, :], wsmK[0], deps=["vector"])

            wait_ld2("tensor", t_selp)
            ps_ap = [ps_big[:, j * 512:(j + 1) * 512] for j in range(4)]
            ap_mm = {}
            ap_cp = {}
            ap_mul = {}
            cq = []
            kidx = 0
            for pair in range(NPAIR):
                sp = selp[:, pair * P:(pair + 1) * P]
                for w in range(2):
                    for k in range(K9):
                        slot = kidx % 4
                        wslot = kidx % 4
                        at_mm = {}
                        if kidx >= 4:
                            at_mm["scalar"] = ap_cp[kidx - 4]
                        E("tensor", lambda t, k=k, w=w, sp=sp, slot=slot:
                          t.matmul(ps_ap[slot], sp,
                                   wsmK[k][:, w * 512:(w + 1) * 512],
                                   start=True, stop=True),
                          at=(at_mm or None),
                          deps=(["vector"] if kidx == 0 else None))
                        ap_mm[kidx] = sch.cnt["tensor"]
                        at_cp = {"tensor": ap_mm[kidx]}
                        if kidx >= 4:
                            at_cp["vector"] = ap_mul[kidx - 4]
                        E("scalar", lambda s, slot=slot, wslot=wslot: s.copy(
                            wsb[wslot], ps_ap[slot]), at=at_cp)
                        ap_cp[kidx] = sch.cnt["scalar"]
                        if cq and pair >= 4:
                            fn, at_q = cq.pop(0)
                            E("scalar", fn, at=at_q)
                        xsl = xlnT[pair][:, w * 512 + k: w * 512 + k + 512]
                        if k < 4:
                            E("vector", lambda v, xsl=xsl, k=k, wslot=wslot:
                              v.tensor_tensor(prods[k], xsl, wsb[wslot],
                                              op=ALU.mult),
                              at={"scalar": ap_cp[kidx]})
                        elif k < 8:
                            E("vector", lambda v, xsl=xsl, k=k, wslot=wslot:
                              v.tensor_tensor(prods[4], xsl, wsb[wslot],
                                              op=ALU.mult),
                              at={"scalar": ap_cp[kidx]})
                            E("vector", lambda v, k=k: v.tensor_tensor(
                                prods[k - 4], prods[k - 4], prods[4],
                                op=ALU.add))
                        else:
                            E("vector", lambda v, xsl=xsl, k=k, wslot=wslot:
                              v.tensor_tensor(prods[4], xsl, wsb[wslot],
                                              op=ALU.mult),
                              at={"scalar": ap_cp[kidx]})
                        ap_mul[kidx] = sch.cnt["vector"]
                        kidx += 1
                    E("vector", lambda v: v.tensor_tensor(
                        prods[0], prods[0], prods[1], op=ALU.add))
                    E("vector", lambda v: v.tensor_tensor(
                        prods[2], prods[2], prods[3], op=ALU.add))
                    E("vector", lambda v: v.tensor_tensor(
                        prods[0], prods[0], prods[2], op=ALU.add))
                    E("vector", lambda v, w=w: v.tensor_tensor(
                        apair[:, w * 512:(w + 1) * 512], prods[0], prods[4],
                        op=ALU.add))
                    drip(3)
                for i4 in range(2):
                    pst = ps_tAb if (i4 == 0) else ps_tBb
                    for j in range(4):
                        i = i4 * 4 + j
                        E("tensor", lambda t, i=i, j=j, p=pst: t.transpose(
                            p[:, j * P:(j + 1) * P],
                            apair[:, i * P:(i + 1) * P], identb),
                            deps=(["vector"] if j == 0 else None))
                    dstv = xall.rearrange("p (t r) -> p t r", r=D)[
                        :, i4 * 4:(i4 + 1) * 4, pair * P:(pair + 1) * P]
                    E("vector", lambda v, dstv=dstv, p=pst: v.tensor_tensor(
                        dstv, dstv, p[:].rearrange("p (t r) -> p t r", r=P),
                        op=ALU.add), deps=["tensor"])
                    if (pair == 5 and i4 == 1 and sg == 0
                            and not w0_serial and not skip_ffn
                            and not ov.get("w0done")):
                        drip(len(w0_steps))
                        w0_post()
                        ov["w0done"] = True
                    if pair == 3:
                        vg3 = sch.cnt["vector"]
                        cdst = x2lnT2.bitcast(F32)[:, 0:512]
                        for j in range(4):
                            i = i4 * 4 + j
                            at3 = {"vector": vg3} if (j == 0) else None
                            cq.append((lambda sc, i=i, cdst=cdst:
                                       sc.activation(
                                           cdst, xt[i][:, 0:512], AF.Copy,
                                           accum_out=stat[:, 16 + i:17 + i]),
                                       at3))
                            cq.append((lambda sc, i=i: sc.activation(
                                lnsq, xt[i][:, 0:512], AF.Square,
                                accum_out=stat[:, 24 + i:25 + i]), None))
                    if pair == NPAIR - 1:
                        vg7 = sch.cnt["vector"]
                        for j in range(4):
                            i = i4 * 4 + j
                            E("vector", lambda v, i=i: v.reduce_sum(
                                stat[:, 56 + i:57 + i], xt[i][:, 512:1024],
                                axis=AX.X))
                            E("scalar", lambda sc, i=i: sc.activation(
                                lnsq, xt[i][:, 512:1024], AF.Square,
                                accum_out=stat[:, 32 + i:33 + i]),
                                at=({"vector": vg7} if j == 0 else None))
                        E("vector", lambda v: v.drain())
            sch.barrier()

            if debug_taps and sg == 0:
                tpa = dbg("x1", [SG, D], F32)
                for i in range(NTIL):
                    store(tpa[i * P:(i + 1) * P, :], xt[i], deps=["vector"])

            if sg == 0:
                if not w0_serial and not ov.get("w0done"):
                    drip(len(w0_steps))
                    w0_post()
                    ov["w0done"] = True
                tgt_w = w0_state["last_st"]
                sch.ops["sync"].append(lambda h, t=tgt_w: h.wait_ge(s_st, t))
            if skip_ffn:
                for i in range(NTIL):
                    store(out[base + i * P: base + (i + 1) * P, :], xt[i],
                          deps=["vector"])
                continue
            sg_pages = []
            for q4 in range(4):
                grpq = WGRP[q4]
                for pg in range(4 * q4, 4 * q4 + 4):
                    tgtq = load(arena[:, pg * 2048:(pg + 1) * 2048],
                                wfct_scr[pg, :, :],
                                deps=(["tensor", "vector", "scalar"]
                                      if pg == 4 * q4 else None), group=grpq)
                sg_pages.append((grpq, tgtq))
            tw = None

            ln_finish("B", seed=0.6, iters=5, split_sums=True)
            if debug_taps and sg == 0:
                tls2 = dbg("ln2stat", [P, 64], F32)
                dbg_store(tls2[:, :], stat)

            prev = {}
            for i in range(NTIL):
                tm = x2ln_tm_b[i % 2]
                tT = x2lnT_b[i % 2]
                E("vector", lambda v, i=i, tm=tm: v.tensor_scalar(
                    tm, xt[i], stat[:, i:i + 1], stat[:, 8 + i:9 + i],
                    op0=ALU.subtract, op1=ALU.mult))
                va = sch.cnt["vector"]
                ev1 = []
                for c4 in range(2):
                    pst = ps_tAb if (c4 == 0) else ps_tBb
                    dep_at = {"vector": va if "t1ev" not in prev
                              else max(va, prev["t1ev"][c4])}
                    for cj in range(4):
                        c = c4 * 4 + cj
                        E("tensor", lambda t, c=c, cj=cj, p=pst, tm=tm:
                          t.transpose(p[:, cj * P:(cj + 1) * P],
                                      tm[:, c * P:(c + 1) * P], identb),
                          at=(dep_at if cj == 0 else None))
                    E("vector", lambda v, c4=c4, p=pst, tT=tT: v.tensor_copy(
                        tT[:, c4 * 512:(c4 + 1) * 512], p[:]),
                        deps=["tensor"])
                    ev1.append(sch.cnt["vector"])
                if i > 0:
                    E("vector", lambda v, j=i - 1: v.scalar_tensor_tensor(
                        xt[j], ps_proj, stat[:, 42:43], xt[j],
                        op0=ALU.mult, op1=ALU.add),
                        at={"tensor": prev["proj"], "scalar": prev["sq"]})
                    prev["evac"] = sch.cnt["vector"]
                    if sg == 0 and not skip_ffn:
                        ov_tile(i - 1)
                        if i >= 2:
                            ov_collect(i - 2)

                sqq_hist = prev.get("sqq_hist", [0, 0])
                for quart in range(4):
                    half, wq = quart // 2, (quart % 2) * 2
                    for c in range(DCH):
                        for w in (wq, wq + 1):
                            at_ = None
                            if c == 0 and w == wq:
                                at_ = {"vector": ev1[1],
                                       "scalar": sqq_hist[-2]}
                                if i == 0:
                                    grpq, tgtq = sg_pages[quart]
                                    wait_ld("tensor", tgtq, group=grpq)
                            j0 = half * 8 + 2 * w
                            rhsv = (arena.rearrange("p (j r) -> p j r", r=2048)
                                    [:, j0:j0 + 2, :]
                                    .rearrange("p j (a u) -> p j a u", u=1024)
                                    [:, :, :, c * P:(c + 1) * P])
                            E("tensor", lambda t, c=c, w=w, half=half, tT=tT,
                              rhsv=rhsv: t.matmul(
                                ps_fc[:, w * 512:(w + 1) * 512],
                                tT[:, c * P:(c + 1) * P], rhsv,
                                start=(c == 0), stop=(c == DCH - 1)),
                                at=at_)
                    gsl = gbuf[:, (quart % 2) * 1024:(quart % 2 + 1) * 1024]
                    at_g = {"tensor": sch.cnt["tensor"]}
                    if quart == 2:
                        at_g["vector"] = prev.get("gcp", 0)
                    E("scalar", lambda s, quart=quart, gsl=gsl, at_g=at_g:
                      s.activation(
                        gsl, ps_fc[:, (quart % 2) * 1024:
                                   (quart % 2 + 1) * 1024],
                        AF.Gelu, scale=0.5,
                        accum_out=accg[:, quart:quart + 1]), at=at_g)
                    if quart == 3:
                        prev["gq3"] = sch.cnt["scalar"]
                    E("scalar", lambda s, quart=quart, gsl=gsl: s.activation(
                        zscr[:, 0:2048][:, (quart % 2) * 1024:
                                        (quart % 2 + 1) * 1024], gsl,
                        AF.Square, accum_out=accg[:, 4 + quart:5 + quart]))
                    sqq_hist.append(sch.cnt["scalar"])
                    if quart == 1:
                        E("vector", lambda v: v.tensor_copy(
                            zscr[:, 2048:4096], gbuf), deps=["scalar"])
                        prev["gcp"] = sch.cnt["vector"]
                prev["sq"] = sch.cnt["scalar"]
                prev["sqq_hist"] = sqq_hist[-2:]
                if sg == 0 and i == NTIL - 1:
                    ov["fc_last"] = sch.cnt["tensor"]
                E("vector", lambda v: v.tensor_reduce(
                    stat[:, 40:41], accg[:, 0:4], axis=AX.X, op=ALU.add),
                    at={"scalar": prev["gq3"]})
                E("vector", lambda v: v.drain())
                E("vector", lambda v: v.tensor_scalar(
                    stat[:, 40:41], stat[:, 40:41], 1.0 / H, None,
                    op0=ALU.mult))
                E("vector", lambda v: v.drain())
                vzq = []
                for zq in range(4):
                    if zq < 2:
                        src_z = zscr[:, 2048 + zq * 1024: 3072 + zq * 1024]
                    else:
                        src_z = gbuf[:, (zq - 2) * 1024:(zq - 1) * 1024]
                    E("vector", lambda v, zq=zq, src_z=src_z: v.tensor_scalar(
                        zscr[:, zq * 1024:(zq + 1) * 1024], src_z,
                        stat[:, 40:41], 0.5,
                        op0=ALU.subtract, op1=ALU.mult))
                    vzq.append(sch.cnt["vector"])
                vz0, vz = vzq[1], vzq[3]
                E("vector", lambda v: v.tensor_reduce(
                    stat[:, 41:42], accg[:, 4:8], axis=AX.X, op=ALU.add),
                    at={"scalar": prev["sq"]})
                E("vector", lambda v: v.drain())
                E("vector", lambda v: v.tensor_scalar(
                    stat[:, 41:42], stat[:, 41:42], 1.0 / H, None,
                    op0=ALU.mult))
                E("vector", lambda v: v.tensor_tensor(
                    stat[:, 42:43], stat[:, 40:41], stat[:, 40:41],
                    op=ALU.mult))
                E("vector", lambda v: v.drain())
                E("vector", lambda v: v.tensor_tensor(
                    stat[:, 41:42], stat[:, 41:42], stat[:, 42:43],
                    op=ALU.subtract))
                E("vector", lambda v: v.drain())
                E("vector", lambda v: v.tensor_scalar(
                    stat[:, 41:42], stat[:, 41:42], 1.0, LN_EPS,
                    op0=ALU.mult, op1=ALU.add))
                E("vector", lambda v: v.drain())
                evac_cnt = []
                for b4 in range(8):
                    pst = ps_zt[b4 % 4][:, 0:512]
                    need_z = vzq[b4 // 2]
                    dep_at = {"vector": (need_z if b4 < 4
                                         else max(need_z, evac_cnt[b4 - 4])),
                              "scalar": prev["sq"]}
                    for j in range(4):
                        hc = b4 * 4 + j
                        E("tensor", lambda t, hc=hc, j=j, p=pst: t.transpose(
                            p[:, j * P:(j + 1) * P],
                            zscr[:, hc * P:(hc + 1) * P], identb),
                            at=(dep_at if j == 0 else None))
                    E("vector", lambda v, b4=b4, p=pst: v.tensor_copy(
                        zT[:, b4 * 512:(b4 + 1) * 512], p[:]), deps=["tensor"])
                    evac_cnt.append(sch.cnt["vector"])
                prev["t1ev"] = [0, ev1[1]]
                newton_rs(stat[:, 41:42], stat[:, 42:43], 0.064, 4)
                E("vector", lambda v: v.tensor_tensor(
                    scol, stat[:, 41:42], stat[:, 42:43], op=ALU.mult))
                at_p = {"vector": evac_cnt[-1]}
                if "evac" in prev:
                    at_p["vector"] = max(at_p["vector"], prev["evac"])
                for hc in range(HCH):
                    for w in range(2):
                        E("tensor", lambda t, hc=hc, w=w: t.matmul(
                            ps_proj[:, w * 512:(w + 1) * 512],
                            zT[:, hc * P:(hc + 1) * P],
                            wpT[hc][:, w * 512:(w + 1) * 512],
                            start=(hc == 0), stop=False),
                            at=(at_p if (hc == 0 and w == 0) else None))
                E("tensor", lambda t: t.transpose(
                    ps_tAb[0:1, 0:P], scol, identb), deps=["vector"])
                E("vector", lambda v: v.tensor_copy(
                    srow, ps_tAb[0:1, 0:P]), deps=["tensor"])
                prev["t1ev"][0] = sch.cnt["vector"]
                for w in range(2):
                    E("tensor", lambda t, w=w: t.matmul(
                        ps_proj[:, w * 512:(w + 1) * 512], srow,
                        cvec[:, w * 512:(w + 1) * 512], start=False,
                        stop=(w == 1)), deps=(["vector"] if w == 0 else None))
                prev["proj"] = sch.cnt["tensor"]
            E("vector", lambda v: v.scalar_tensor_tensor(
                xt[NTIL - 1], ps_proj, stat[:, 42:43], xt[NTIL - 1],
                op0=ALU.mult, op1=ALU.add),
                at={"tensor": prev["proj"]})
            if sg == 0:
                ov_tile(NTIL - 1)
                ov_collect(NTIL - 2)
                ov_collect(NTIL - 1)
                th_l = load(xh[0:4, :], xin[SG + 0: SG + 4, :],
                            group="small")
                th_r = load(xh[4:8, :], xin[2 * SG + 4: 2 * SG + 8, :],
                            group="small")
                ln_finish("B", seed=0.6, iters=5)
                wait_ld2("vector", th_l)
                wait_ld2("vector", th_r)
                halo_ln("newton", gate_t=ov["fc_last"])
                a_prep(gate_t=ov["fc_last"])
            else:
                sch.barrier()
                for i in range(NTIL):
                    store(out[base + i * P: base + (i + 1) * P, :], xt[i],
                          deps=["vector"])

        sch.ops["gpsimd"].append(lambda h: h.wait_ge(s_st, st_cnt[0] * 16))

        with nc.Block() as block:
            @block.sync
            def _(sync):
                for op in sch.ops["sync"]:
                    op(sync)

            @block.tensor
            def _(tensor):
                for op in sch.ops["tensor"]:
                    op(tensor)

            @block.vector
            def _(vector):
                for op in sch.ops["vector"]:
                    op(vector)

            @block.scalar
            def _(scalar):
                for op in sch.ops["scalar"]:
                    op(scalar)

            @block.gpsimd
            def _(gpsimd):
                for op in sch.ops["gpsimd"]:
                    op(gpsimd)

    return nc, taps



def _pack_inputs(L0_attn_query, L0_attn_conv_weight, L0_ffn_W_fc,
                 L0_ffn_ln_w, L0_ffn_ln_b, L0_ffn_W_proj):
    x = np.asarray(L0_attn_query, dtype=np.float32)
    cw = np.asarray(L0_attn_conv_weight, dtype=np.float32)
    wfc = np.ascontiguousarray(np.asarray(L0_ffn_W_fc, dtype=np.float32))
    wproj = np.ascontiguousarray(np.asarray(L0_ffn_W_proj, dtype=np.float32))
    lnw = np.asarray(L0_ffn_ln_w, dtype=np.float32)
    lnb = np.asarray(L0_ffn_ln_b, dtype=np.float32)

    B, S, _ = x.shape
    Q = S // 4
    cwp = np.zeros((P, DCH * K9 * 18), np.float32)
    for c in range(DCH):
        for k in range(K9):
            for hh in range(2):
                h = 2 * c + hh
                for j in range(K9):
                    cwp[hh * HS:(hh + 1) * HS, (c * K9 + k) * 18 + hh * K9 + j] = \
                        cw[h * K9 + j, :, k]
    lnw_d = np.ascontiguousarray(lnw.reshape(HCH, P).T)
    lnb_d = np.ascontiguousarray(lnb.reshape(HCH, P).T)
    identb = np.eye(P, dtype=ml_dtypes.bfloat16)
    identf = np.eye(P, dtype=np.float32)
    onesf = np.ones((P, P), np.float32)
    selp = np.zeros((16, NPAIR * P), ml_dtypes.bfloat16)
    for pair in range(NPAIR):
        selp[2 * pair, pair * P: pair * P + HS] = 1
        selp[2 * pair + 1, pair * P + HS: (pair + 1) * P] = 1

    in_maps = []
    for b in range(B):
        for q in range(4):
            lo, hi = q * Q, (q + 1) * Q
            sl = x[b, max(0, lo - 4):min(S, hi + 4), :]
            padl = np.zeros((4 - (lo - max(0, lo - 4)), D), np.float32)
            padr = np.zeros((4 - (min(S, hi + 4) - hi), D), np.float32)
            shard = np.ascontiguousarray(np.concatenate([padl, sl, padr], axis=0))
            in_maps.append({
                "xin": shard, "wfc": wfc, "wproj": wproj,
                "lnw_d": lnw_d, "lnb_d": lnb_d, "cwpack": cwp,
                "identb_d": identb, "identf_d": identf, "onesf_d": onesf,
                "selp_d": selp,
            })
    return in_maps


_cached = {}


def kernel(L0_attn_query, L0_attn_conv_weight, L0_ffn_W_fc,
           L0_ffn_ln_w, L0_ffn_ln_b, L0_ffn_W_proj):
    in_maps = _pack_inputs(L0_attn_query, L0_attn_conv_weight, L0_ffn_W_fc,
                           L0_ffn_ln_w, L0_ffn_ln_b, L0_ffn_W_proj)
    if "nc" not in _cached:
        _cached["nc"] = build_program(debug_taps=False)[0]
    res = run_bass_kernel_spmd(_cached["nc"], in_maps, core_ids=list(range(8)))
    B, S = 2, 8192
    Q = S // 4
    outp = np.empty((B, S, D), np.float32)
    ci = 0
    for b in range(B):
        for q in range(4):
            outp[b, q * Q:(q + 1) * Q, :] = res.results[ci]["out"]
            ci += 1
    return outp


# version 35
# speedup vs baseline: 1.0030x; 1.0030x over previous
import contextlib
import numpy as np
import ml_dtypes
import concourse.bass as bass
import concourse.mybir as mybir
from concourse.bass_utils import run_bass_kernel_spmd

F32 = mybir.dt.float32
BF16 = mybir.dt.bfloat16
AF = mybir.ActivationFunctionType
ALU = mybir.AluOpType
AX = mybir.AxisListType

P = 128
SG = 1024
NSG = 2
TCORE = SG * NSG
D = 1024
DCH = D // P
H = 4096
HCH = H // P
HEADS, HS, K9 = 16, 64, 9
NPAIR = HEADS // 2
NTIL = SG // P
LN_EPS = 1e-5
ABS_EPS = 1e-6

ENGS = ("sync", "tensor", "vector", "scalar", "gpsimd")


class Sched:
    def __init__(self):
        self.ops = {e: [] for e in ENGS}
        self.cnt = {e: 0 for e in ENGS}
        self.sems = None

    def emit(self, eng, fn, deps=None, inc=True, at=None):
        waits = []
        if deps:
            for d in deps:
                if d == eng:
                    continue
                c = self.cnt[d]
                if c > 0:
                    waits.append((d, c))
        if at:
            for d, c in at.items():
                if d != eng and c > 0:
                    waits.append((d, c))
        my_inc = None
        if inc:
            self.cnt[eng] += 1
            my_inc = self.cnt[eng]
        sems = lambda: self.sems

        def op(h):
            for (d, c) in waits:
                h.wait_ge(sems()[d], c)
            last = fn(h)
            if my_inc is not None:
                if last is None:
                    last = h.nop()
                last.then_inc(sems()[eng], 1)
        self.ops[eng].append(op)

    def barrier(self, engs=("tensor", "vector", "scalar")):
        snap = dict(self.cnt)
        sems = lambda: self.sems
        for e in engs:
            waits = [(d, snap[d]) for d in engs if d != e and snap[d] > 0]

            def op(h, waits=waits):
                for (d, c) in waits:
                    h.wait_ge(sems()[d], c)
            self.ops[e].append(op)


def build_program(debug_taps=False, skip_ffn=False, paranoid=(), oldrs=False, w0_serial=False, w0_trunc=None):
    nc = bass.Bass("TRN2", target_bir_lowering=False, debug=False)

    xin = nc.dram_tensor("xin", [TCORE + 8, D], F32, kind="ExternalInput").ap()
    wfc = nc.dram_tensor("wfc", [H, D], F32, kind="ExternalInput").ap()
    wproj = nc.dram_tensor("wproj", [D, H], F32, kind="ExternalInput").ap()
    lnw_in = nc.dram_tensor("lnw_d", [P, HCH], F32, kind="ExternalInput").ap()
    lnb_in = nc.dram_tensor("lnb_d", [P, HCH], F32, kind="ExternalInput").ap()
    cw_in = nc.dram_tensor("cwpack", [P, DCH * K9 * 18], F32, kind="ExternalInput").ap()
    identb_in = nc.dram_tensor("identb_d", [P, P], BF16, kind="ExternalInput").ap()
    identf_in = nc.dram_tensor("identf_d", [P, P], F32, kind="ExternalInput").ap()
    onesf_in = nc.dram_tensor("onesf_d", [P, P], F32, kind="ExternalInput").ap()
    selp_in = nc.dram_tensor("selp_d", [16, NPAIR * P], BF16, kind="ExternalInput").ap()
    out = nc.dram_tensor("out", [TCORE, D], F32, kind="ExternalOutput").ap()
    wfct_scr = nc.dram_tensor("wfct_scr", [16, P, 2048], BF16,
                              kind="Internal").ap()

    taps = {}

    def dbg(name, shape, dtype=F32):
        taps[name] = nc.dram_tensor("tap_" + name, shape, dtype,
                                    kind="ExternalOutput").ap()
        return taps[name]

    sch = Sched()
    ctx = contextlib.ExitStack()
    with ctx:
        sb = lambda name, shape, dt: ctx.enter_context(
            nc.sbuf_tensor(name, shape, dt)).ap()

        xall = sb("xall", [P, NTIL * D], F32)
        xt = [xall[:, i * D:(i + 1) * D] for i in range(NTIL)]
        xh = sb("xh", [8, D], F32)
        wpT_all = sb("wpT_all", [P, HCH * D], BF16)
        wpT = [wpT_all[:, i * D:(i + 1) * D] for i in range(HCH)]
        wsum = sb("wsum", [P, 64], F32)
        wfstg = sb("wfstg", [P, D], BF16)
        arena = sb("arena", [P, 32768], BF16)
        gbuf = sb("gbuf", [P, 2048], F32)
        zscr = sb("zscr", [P, H], BF16)
        zT = sb("zT", [P, HCH * P], BF16)
        x2ln_tm2 = sb("x2ln_tm2", [P, 2 * D], BF16)
        x2ln_tm_b = [x2ln_tm2[:, 0:D], x2ln_tm2[:, D:2 * D]]
        x2lnT2 = sb("x2lnT2", [P, 2 * DCH * P], BF16)
        x2lnT_b = [x2lnT2[:, 0:DCH * P], x2lnT2[:, DCH * P:2 * DCH * P]]
        stat = sb("stat", [P, 64], F32)
        accg = sb("accg", [P, 8], F32)
        lnw_t = sb("lnw_t", [P, HCH], F32)
        lnb_t = sb("lnb_t", [P, HCH], F32)
        lnb_b = sb("lnb_b", [P, HCH], BF16)
        cwb = sb("cwb", [P, DCH * K9 * 18], BF16)
        identb = sb("sb_identb", [P, P], BF16)
        identf = sb("sb_identf", [P, P], F32)
        onesf = sb("sb_onesf", [P, P], F32)
        onesb = sb("onesb", [1, P], BF16)
        selp = sb("sb_selp", [16, NPAIR * P], BF16)
        gam = sb("gam", [P, 4], F32)
        epsv = sb("epsv", [P, 1], F32)
        cvec = sb("cvec", [1, D], BF16)
        scol = sb("scol", [P, 1], BF16)
        srow = sb("srow", [1, P], BF16)

        wfT = [arena[:, c * 4096:(c + 1) * 4096] for c in range(DCH)]
        xlnT = [arena[:, c * 1040: c * 1040 + SG + 8] for c in range(DCH)]
        xln_tm = arena[:, 8320:9344]
        xln_hm = arena[0:8, 9344:10368]
        cklF = arena[:, 10368:16512].bitcast(F32)
        ckl_tm = [arena[:, 16512 + i * 768: 16512 + (i + 1) * 768].bitcast(F32)
                  for i in range(NTIL)]
        wsm_tm = arena[:, 22656:22800]
        wsmK = [arena[0:16, 22800 + k * SG: 22800 + (k + 1) * SG]
                for k in range(K9)]
        gmax = arena[:, 32016:32048].bitcast(F32)
        gsum = arena[:, 32048:32080].bitcast(F32)
        grec = arena[:, 32080:32112].bitcast(F32)
        prods = [arena[:, 10368 + j * 512: 10368 + (j + 1) * 512]
                 for j in range(5)]
        wsb = [arena[:, 10368 + (5 + m) * 512: 10368 + (6 + m) * 512]
               for m in range(4)]
        apair = arena[:, 14976:16000]
        wtile2b = [gbuf, zT.bitcast(F32),
                   wpT_all[:, 0:4096].bitcast(F32),
                   wpT_all[:, 4096:8192].bitcast(F32),
                   wpT_all[:, 8192:12288].bitcast(F32),
                   wpT_all[:, 12288:16384].bitcast(F32),
                   wpT_all[:, 16384:20480].bitcast(F32),
                   wpT_all[:, 20480:24576].bitcast(F32)]
        sgnA2 = zscr.bitcast(F32)
        wfstg2 = x2lnT2
        wqn = x2ln_tm2
        lnsq = wfstg.bitcast(F32)

        psv = lambda name, shape, dt: ctx.enter_context(
            nc.psum_tensor(name, shape, dt)).ap()
        ps_big = psv("ps_big", [P, 3072], F32)
        ps_tA = psv("ps_tA", [P, 512], F32)
        ps_tB = psv("ps_tB", [P, 512], F32)

        ps_fc = ps_big[:, 0:2048]
        ps_proj = ps_big[:, 2048:3072]
        ps_ckl = [ps_big[:, 0:512], ps_big[:, 512:1024], ps_big[:, 1024:1536]]
        ps_wb = [ps_big[:, 1536:2048], ps_big[:, 2048:2560]]
        ps_c = ps_big[0:1, 2048:3072]
        ps_bc = ps_big[:, 2560:2561]
        ps_tAb = ps_tA[:, 0:256].bitcast(BF16)
        ps_w45 = [ps_big[:, 2048:2560].bitcast(BF16),
                  ps_big[:, 2560:3072].bitcast(BF16)]
        ps_tAw = ps_tA.bitcast(BF16)
        ps_zt = [ps_big[:, b * 512:(b + 1) * 512].bitcast(BF16)
                 for b in range(4)]
        ps_tBb = ps_tB[:, 0:256].bitcast(BF16)
        ps_tm = ps_tB[:, 0:128]

        sems = {e: ctx.enter_context(nc.semaphore("s_" + e)) for e in ENGS}
        sch.sems = sems
        s_ld = ctx.enter_context(nc.semaphore("s_ld"))
        s_ld2 = ctx.enter_context(nc.semaphore("s_ld2"))
        s_lda = ctx.enter_context(nc.semaphore("s_lda"))
        s_ldb = ctx.enter_context(nc.semaphore("s_ldb"))
        s_st = ctx.enter_context(nc.semaphore("s_st"))
        s_w0v = ctx.enter_context(nc.semaphore("s_w0v"))
        s_w0s = ctx.enter_context(nc.semaphore("s_w0s"))
        s_w0t = ctx.enter_context(nc.semaphore("s_w0t"))
        s_w0e = ctx.enter_context(nc.semaphore("s_w0e"))
        s_ldc = ctx.enter_context(nc.semaphore("s_ldc"))
        s_ldd = ctx.enter_context(nc.semaphore("s_ldd"))
        s_lde = ctx.enter_context(nc.semaphore("s_lde"))
        s_ldf = ctx.enter_context(nc.semaphore("s_ldf"))
        s_ldg = ctx.enter_context(nc.semaphore("s_ldg"))
        s_ldh = ctx.enter_context(nc.semaphore("s_ldh"))
        ld_groups = {"main": [s_ld, 0], "small": [s_ld2, 0],
                     "wa": [s_lda, 0], "wb": [s_ldb, 0],
                     "wc": [s_ldc, 0], "wd": [s_ldd, 0],
                     "we": [s_lde, 0], "wf": [s_ldf, 0],
                     "wg": [s_ldg, 0], "wh": [s_ldh, 0]}
        WGRP = ("wa", "wb", "wc", "wd", "we", "wf", "wg", "wh")
        st_cnt = [0]

        def load(dst, src, deps=None, group="main"):
            g = ld_groups[group]
            g[1] += 1
            tgt = g[1] * 16
            sem = g[0]
            waits = [(d, sch.cnt[d]) for d in (deps or []) if sch.cnt[d] > 0]

            def op(h):
                for (d, c) in waits:
                    h.wait_ge(sems[d], c)
                h.dma_start(out=dst, in_=src).then_inc(sem, 16)
            sch.ops["sync"].append(op)
            return tgt

        def store(dst, src, deps=None):
            st_cnt[0] += 1
            tgt = st_cnt[0] * 16
            waits = [(d, sch.cnt[d]) for d in (deps or []) if sch.cnt[d] > 0]

            def op(h):
                for (d, c) in waits:
                    h.wait_ge(sems[d], c)
                h.dma_start(out=dst, in_=src).then_inc(s_st, 16)
            sch.ops["sync"].append(op)
            return tgt

        def wait_ld(eng, tgt, group="main"):
            sem = ld_groups[group][0]
            sch.ops[eng].append(lambda h: h.wait_ge(sem, tgt))

        def wait_ld2(eng, tgt):
            wait_ld(eng, tgt, group="small")

        def wait_st(eng, tgt):
            sch.ops[eng].append(lambda h: h.wait_ge(s_st, tgt))

        def dbg_store(dst, src):
            tgt = store(dst, src, deps=["vector", "scalar", "tensor"])
            for e in ("vector", "scalar", "tensor"):
                wait_st(e, tgt)

        E = sch.emit

        load(identb, identb_in[:, :])
        load(identf, identf_in[:, :])
        load(onesf, onesf_in[:, :])
        t_selp = load(selp, selp_in[:, :], group="small")
        load(lnw_t, lnw_in[:, :])
        load(lnb_t, lnb_in[:, :])
        cw_f = gbuf[:, 0:DCH * K9 * 18]
        t1 = load(cw_f, cw_in[:, :])
        wait_ld("vector", t1)
        E("vector", lambda v: v.tensor_copy(cwb, cw_f))
        E("vector", lambda v: v.tensor_copy(lnb_b, lnb_t))
        E("vector", lambda v: v.tensor_copy(onesb, onesf[0:1, :]))
        E("vector", lambda v: v.memset(epsv, LN_EPS))

        w0_steps = []
        w0_state = {"last_st": 0}
        gscr = zscr.bitcast(F32)[:, 0:1024]
        w0c = {"v": 0, "s": 0, "t": 0, "sgn": {}, "tt": {}, "tp": {},
               "ev": {0: 0, 1: 0}, "gamma_v0": 0, "gamma_v1": 0}

        def w0_chunks():
            out = []
            for i in range(16):
                src = wfc.rearrange("(a p) d -> p (a d)", p=P)[
                    :, 0:2048] if False else None
                out.append(("fc", i))
            for i in range(16):
                out.append(("pj", i))
            return out

        def w0_src(kind, i):
            if kind == "fc":
                return (wfc[256 * i: 256 * (i + 1), :]
                        .rearrange("(a p) d -> p a d", p=P))
            q0 = 2 * i
            rb = q0 // 4
            qa, qb = q0 % 4, (q0 + 1) % 4
            return (wproj[rb * P:(rb + 1) * P, qa * D:(qb + 1) * D]
                    .rearrange("p (a d) -> p a d", d=D))

        def w0_load(dst, src, grp, s_gate=0, v_gate=0, queue="sync"):
            g = ld_groups[grp]
            g[1] += 1
            tgt = g[1] * 16

            def op(h, tgt=tgt, s_gate=s_gate, v_gate=v_gate, g=g):
                if s_gate > 0:
                    h.wait_ge(s_w0s, s_gate)
                if v_gate > 0:
                    h.wait_ge(s_w0v, v_gate)
                h.dma_start(out=dst, in_=src).then_inc(g[0], 16)
            sch.ops[queue].append(op)
            return tgt

        FCB = (0, 1, 6, 7)
        PJB = (2, 3, 4, 5)

        def gamma_step(j, kind, i, bsel):
            def go():
                bmap = FCB if bsel == 0 else PJB
                bi = bmap[j % 4]
                buf = wtile2b[bi]
                grp = (WGRP[{0: 0, 1: 1, 6: 2, 7: 3}[bi]] if bsel == 0
                       else WGRP[4 + PJB.index(bi)])
                tgt = w0_load(buf.rearrange("p (a d) -> p a d", d=D),
                              w0_src(kind, i), grp,
                              s_gate=w0c["sgn"].get(("g", bsel, j - 4), 0),
                              queue=("gpsimd" if bsel else "sync"))
                w0c["s"] += 1
                w0c["sgn"][("g", bsel, j)] = w0c["s"]

                def op(h, tgt=tgt, j=j, grp=grp, bi=bi):
                    h.wait_ge(ld_groups[grp][0], tgt)
                    h.activation(wtile2b[bi][:, 0:1024],
                                 wtile2b[bi][:, 0:1024], AF.Abs,
                                 accum_out=wsum[:, 2 * j:2 * j + 1])
                    h.activation(wtile2b[bi][:, 1024:2048],
                                 wtile2b[bi][:, 1024:2048], AF.Abs,
                                 accum_out=wsum[:, 2 * j + 1:2 * j + 2]
                                 ).then_inc(s_w0s, 1)
                sch.ops["scalar"].append(op)
            return go

        def gamma_final(which, sgate_needed):
            def go():
                red = stat[:, 60 + which:61 + which]
                w0c["v"] += 0
                def opv(h, sg_=sgate_needed(), which=which, red=red):
                    h.wait_ge(s_w0s, sg_)
                    h.reduce_sum(red, wsum[:, 32 * which:32 * which + 32],
                                 axis=AX.X)
                    h.drain().then_inc(s_w0v, 1)
                w0c["v"] += 1
                vred = w0c["v"]
                sch.ops["vector"].append(opv)
                nelem = float(H * D)
                w0c["t"] += 1
                tmark = w0c["t"]

                evg = w0c["ev"][0]

                def opt(h, red=red, vred=vred, evg=evg):
                    h.wait_ge(s_w0v, vred)
                    if evg > 0:
                        h.wait_ge(s_w0s, evg)
                    h.matmul(ps_bc, onesf, red, start=True,
                             stop=True).then_inc(s_w0t, 1)
                sch.ops["tensor"].append(opt)
                w0c["v"] += 1

                def opv2(h, tmark=tmark, which=which):
                    h.wait_ge(s_w0t, tmark)
                    c0 = 2 * which
                    h.tensor_scalar(gam[:, c0:c0 + 1], ps_bc,
                                    0.5 / nelem, 0.5 * ABS_EPS,
                                    op0=ALU.mult, op1=ALU.add)
                    h.drain()
                    h.tensor_scalar(gam[:, c0 + 1:c0 + 2],
                                    gam[:, c0:c0 + 1], -1.0, None,
                                    op0=ALU.mult)
                    h.drain().then_inc(s_w0v, 1)
                sch.ops["vector"].append(opv2)
                w0c["gamma_v%d" % which] = w0c["v"]
            return go

        def tern_step(jj, kind, i):
            def go():
                j = jj
                if kind == "fc":
                    bi = FCB[jj % 4]
                    pv = jj - 4
                else:
                    bi = (0, 1)[jj % 2]
                    pv = jj - 4 if jj < 18 else jj - 2
                grp = WGRP[{0: 0, 1: 1, 6: 2, 7: 3}[bi]]
                buf = wtile2b[bi]
                sg_ = w0c["sgn"].get(("t", pv))
                if sg_ is None:
                    sg_ = w0c["sgn"][("g", 0, 12 + jj)]
                tgt = w0_load(buf.rearrange("p (a d) -> p a d", d=D),
                              w0_src(kind, i), grp, s_gate=sg_,
                              v_gate=w0c["tt"].get(pv, 0))
                c0 = 0 if kind == "fc" else 2
                gv = "gamma_v%d" % (0 if kind == "fc" else 1)
                w0c["s"] += 2
                w0c["sgn"][("t", jj)] = w0c["s"]
                vgate_sgnA = w0c["tt"].get(jj - 1, 0)

                def ops_act(h, tgt=tgt, j=j, c0=c0, grp=grp, vg=vgate_sgnA,
                            gv=gv, bi=bi):
                    h.wait_ge(ld_groups[grp][0], tgt)
                    h.wait_ge(s_w0v, max(vg, w0c[gv]))
                    h.activation(sgnA2, wtile2b[bi], AF.Sign,
                                 bias=gam[:, c0 + 1:c0 + 2],
                                 scale=1.0).then_inc(s_w0s, 1)
                    h.activation(wtile2b[bi], wtile2b[bi], AF.Sign,
                                 bias=gam[:, c0:c0 + 1],
                                 scale=1.0).then_inc(s_w0s, 1)
                sch.ops["scalar"].append(ops_act)
                if w0c.get("pend_st") is not None:
                    sch.ops["scalar"].append(w0c.pop("pend_st"))
                w0c["v"] += 1
                w0c["tt"][jj] = w0c["v"]
                smark = w0c["s"]
                tpgate = w0c["tp"].get(jj - 1, 0)

                def opv(h, j=j, smark=smark, tg=tpgate, bi=bi):
                    h.wait_ge(s_w0s, smark)
                    if tg > 0:
                        h.wait_ge(s_w0t, tg)
                    h.tensor_tensor(wqn[:, 0:2048], sgnA2, wtile2b[bi],
                                    op=ALU.add).then_inc(s_w0v, 1)
                sch.ops["vector"].append(opv)
                vmark = w0c["v"]
                w0c["t"] += 1
                evgate = w0c["ev"][0]

                def opt(h, vmark=vmark, evgate=evgate):
                    h.wait_ge(s_w0v, vmark)
                    if evgate > 0:
                        h.wait_ge(s_w0s, evgate)
                    for a in range(2):
                        for c in range(DCH):
                            ins = h.transpose(
                                ps_w45[a][:, c * P:(c + 1) * P],
                                wqn[:, a * 1024 + c * P:
                                    a * 1024 + (c + 1) * P], identb)
                    ins.then_inc(s_w0t, 1)
                sch.ops["tensor"].append(opt)
                tmark = w0c["t"]
                stg = w0c.get("last_fc_st", 0) if kind == "fc" else 0
                for a in range(2):
                    if kind == "fc":
                        edst = wfstg2[:, a * 1024:(a + 1) * 1024]
                        esrc = ps_w45[a]
                    else:
                        qq = (2 * i + a) % 4
                        hc0 = qq * 8
                        rb = (2 * i + a) // 4
                        edst = wpT_all.rearrange("p (h r) -> p h r", r=D)[
                            :, hc0:hc0 + 8, rb * P: rb * P + P]
                        esrc = ps_w45[a].rearrange("p (x b) -> p x b", b=P)
                    w0c["s"] += 1

                    def ope(h, d=edst, sr=esrc, tmark=tmark, stg=stg,
                            a=a):
                        h.wait_ge(s_w0t, tmark)
                        if a == 0 and stg > 0:
                            h.wait_ge(s_st, stg)
                        h.tensor_copy(d, sr).then_inc(s_w0s, 1)
                    sch.ops["vector"].append(ope)
                w0c["ev"][0] = w0c["s"]
                w0c["tp"][jj] = w0c["t"]
                if kind == "fc":
                    w0c["fcev"] = w0c.get("fcev", 0) + 1
                    fcev = w0c["fcev"]
                    sch.ops["vector"].append(
                        lambda h: h.nop().then_inc(s_w0e, 1))
                    st_cnt[0] += 1

                    def opst(h, i=i, fcev=fcev):
                        h.wait_ge(s_w0e, fcev)
                        h.dma_start(out=wfct_scr[i, :, :],
                                    in_=wfstg2).then_inc(s_st, 16)
                    w0c["pend_st"] = opst
                    w0c["last_fc_st"] = st_cnt[0] * 16
                    w0_state["last_st"] = st_cnt[0] * 16
            return go

        for j in range(16):
            w0_steps.append(gamma_step(j, "fc", j, 0))
        w0_steps.append(gamma_final(0, lambda: w0c["sgn"][("g", 0, 15)]))
        for j in range(16):
            w0_steps.append(gamma_step(16 + j, "pj", j, 2))
            w0_steps.append(tern_step(j, "fc", j))
        w0_steps.append(gamma_final(1, lambda: w0c["sgn"][("g", 2, 31)]))
        for j in range(16):
            w0_steps.append(tern_step(16 + j, "pj", j))

        def w0_post():
            if w0c.get("pend_st") is not None:
                sch.ops["scalar"].append(w0c.pop("pend_st"))
            sv, ss, st_ = w0c["v"], w0c["s"], w0c["t"]
            for e, sem, c in (("tensor", s_w0s, ss), ("vector", s_w0s, ss),
                              ("scalar", s_w0v, sv)):
                sch.ops[e].append(lambda h, sem=sem, c=c: h.wait_ge(sem, c))
            for w in range(2):
                for hc in range(HCH):
                    E("tensor", lambda t, w=w, hc=hc: t.matmul(
                        ps_c[:, w * 512:(w + 1) * 512], lnb_b[:, hc:hc + 1],
                        wpT[hc][:, w * 512:(w + 1) * 512],
                        start=(hc == 0), stop=(hc == HCH - 1)),
                        deps=(["vector", "scalar"] if hc == 0 else None))
            E("vector", lambda v: v.tensor_scalar(
                cvec, ps_c, 0.5, None, op0=ALU.mult), deps=["tensor"])
            for hc in range(HCH):
                E("vector", lambda v, hc=hc: v.tensor_scalar(
                    wpT[hc], wpT[hc], lnw_t[:, hc:hc + 1], None, op0=ALU.mult))

        def drip(n=4):
            for _ in range(n):
                if w0_steps:
                    w0_steps.pop(0)()

        def newton_rs(v_ap, rs_ap, seed, iters, tmp_base=56, rows=P):
            tmp = stat[0:rows, tmp_base:tmp_base + (v_ap.shape[1])]
            E("vector", lambda v: v.memset(rs_ap, seed))
            for _ in range(iters):
                E("vector", lambda v: v.drain())
                E("vector", lambda v: v.tensor_tensor(tmp, rs_ap, rs_ap,
                                                      op=ALU.mult))
                E("vector", lambda v: v.drain())
                E("vector", lambda v: v.tensor_tensor(tmp, tmp, v_ap,
                                                      op=ALU.mult))
                E("vector", lambda v: v.drain())
                E("vector", lambda v: v.tensor_scalar(
                    tmp, tmp, -0.5, 1.5, op0=ALU.mult, op1=ALU.add))
                E("vector", lambda v: v.drain())
                E("vector", lambda v: v.tensor_tensor(rs_ap, rs_ap, tmp,
                                                      op=ALU.mult))
            E("vector", lambda v: v.drain())

        def ln_collect(i, x_ap, deps=None):
            E("vector", lambda v, i=i, x_ap=x_ap: v.reduce_sum(
                stat[:, 16 + i:17 + i], x_ap, axis=AX.X), deps=deps)
            E("vector", lambda v: v.drain())
            E("scalar", lambda s, i=i, x_ap=x_ap: s.activation(
                lnsq, x_ap[:, 0:512], AF.Square,
                accum_out=stat[:, 24 + i:25 + i]), deps=["vector"])
            E("scalar", lambda s, i=i, x_ap=x_ap: s.activation(
                lnsq, x_ap[:, 512:1024], AF.Square,
                accum_out=stat[:, 32 + i:33 + i]))

        def ln_finish(mode, seed=0.6, iters=8, split_sums=False):
            n = NTIL
            mu = stat[:, 0:n]
            rs = stat[:, 8:8 + n]
            var = stat[:, 48:48 + n]
            scr = stat[:, 16:16 + n]
            E("vector", lambda v: v.drain())
            if split_sums:
                E("vector", lambda v: v.tensor_tensor(
                    stat[:, 16:16 + n], stat[:, 16:16 + n],
                    stat[:, 56:56 + n], op=ALU.add), deps=["scalar"])
                E("vector", lambda v: v.drain())
            E("vector", lambda v: v.tensor_scalar(
                mu, stat[:, 16:16 + n], 1.0 / D, None, op0=ALU.mult),
                deps=(None if split_sums else ["scalar"]))
            E("vector", lambda v: v.tensor_tensor(
                var, stat[:, 24:24 + n], stat[:, 32:32 + n], op=ALU.add))
            E("vector", lambda v: v.drain())
            E("vector", lambda v: v.tensor_scalar(
                var, var, 1.0 / D, None, op0=ALU.mult))
            E("vector", lambda v: v.tensor_tensor(scr, mu, mu, op=ALU.mult))
            E("vector", lambda v: v.drain())
            E("vector", lambda v: v.tensor_tensor(var, var, scr,
                                                  op=ALU.subtract))
            E("vector", lambda v: v.drain())
            if mode == "A":
                E("scalar", lambda s: s.activation(
                    scr, var, AF.Ln, bias=epsv, scale=1.0), deps=["vector"])
                E("scalar", lambda s: s.activation(
                    rs, scr, AF.Exp, scale=-0.5))
            else:
                E("vector", lambda v: v.tensor_scalar(
                    var, var, 1.0, LN_EPS, op0=ALU.mult, op1=ALU.add))
                newton_rs(var, rs, seed, iters, tmp_base=24)

        def xlnT_quad(c0, tok0):
            a0 = c0 * 1040 + 4 + tok0
            return (arena[:, a0: a0 + 4 * 1040]
                    .rearrange("p (c r) -> p c r", r=1040)[:, :, 0:P])

        if w0_trunc is not None:
            del w0_steps[w0_trunc:]
        if w0_serial:
            drip(len(w0_steps))
            if w0_trunc is None:
                w0_post()
        def keepwarm(sg):
            pass

        def halo_ln(mode, gate_t=None):
            E("vector", lambda v: v.reduce_sum(
                stat[0:8, 44:45], xh, axis=AX.X))
            E("scalar", lambda s: s.activation(
                lnsq[0:8, :], xh[0:8, 0:512], AF.Square,
                accum_out=stat[0:8, 45:46]), deps=["vector"])
            E("scalar", lambda s: s.activation(
                lnsq[0:8, :], xh[0:8, 512:1024], AF.Square,
                accum_out=stat[0:8, 46:47]))
            E("vector", lambda v: v.tensor_tensor(
                stat[0:8, 45:46], stat[0:8, 45:46], stat[0:8, 46:47],
                op=ALU.add), deps=["scalar"])
            E("vector", lambda v: v.drain())
            E("vector", lambda v: v.tensor_scalar(
                stat[0:8, 44:45], stat[0:8, 44:45], 1.0 / D, None,
                op0=ALU.mult))
            E("vector", lambda v: v.tensor_scalar(
                stat[0:8, 45:46], stat[0:8, 45:46], 1.0 / D, None,
                op0=ALU.mult))
            E("vector", lambda v: v.drain())
            E("vector", lambda v: v.tensor_tensor(
                stat[0:8, 46:47], stat[0:8, 44:45], stat[0:8, 44:45],
                op=ALU.mult))
            E("vector", lambda v: v.drain())
            E("vector", lambda v: v.tensor_tensor(
                stat[0:8, 45:46], stat[0:8, 45:46], stat[0:8, 46:47],
                op=ALU.subtract))
            E("vector", lambda v: v.drain())
            if mode == "act":
                E("scalar", lambda s: s.activation(
                    stat[0:8, 47:48], stat[0:8, 45:46], AF.Ln,
                    bias=epsv[0:8, :], scale=1.0), deps=["vector"])
                E("scalar", lambda s: s.activation(
                    stat[0:8, 45:46], stat[0:8, 47:48], AF.Exp, scale=-0.5))
                rs_c = 45
            else:
                E("vector", lambda v: v.tensor_scalar(
                    stat[0:8, 45:46], stat[0:8, 45:46], 1.0, LN_EPS,
                    op0=ALU.mult, op1=ALU.add))
                E("vector", lambda v: v.drain())
                newton_rs(stat[0:8, 45:46], stat[0:8, 47:48], 0.6, 6,
                          tmp_base=24, rows=8)
                rs_c = 47
            E("vector", lambda v, rs_c=rs_c: v.tensor_scalar(
                xln_hm, xh, stat[0:8, 44:45], stat[0:8, rs_c:rs_c + 1],
                op0=ALU.subtract, op1=ALU.mult),
                deps=(["scalar"] if mode == "act" else None),
                at=({"tensor": gate_t} if gate_t else None))

        def a_prep(gate_t=None, dodrip=False):
            ee = "scalar" if gate_t else "vector"
            tprev = gate_t
            for i in range(NTIL):
                E("vector", lambda v, i=i: v.tensor_scalar(
                    xln_tm, xt[i], stat[:, i:i + 1], stat[:, 8 + i:9 + i],
                    op0=ALU.subtract, op1=ALU.mult),
                    deps=(["scalar"] if i == 0 else None),
                    at=({"tensor": tprev} if tprev else None))
                for c4 in range(2):
                    pst = ps_tAb if (c4 == 0) else ps_tBb
                    for cj in range(4):
                        c = c4 * 4 + cj
                        E("tensor", lambda t, c=c, cj=cj, p=pst: t.transpose(
                            p[:, cj * P:(cj + 1) * P],
                            xln_tm[:, c * P:(c + 1) * P], identb),
                            deps=((["vector"] if ee == "vector"
                                   else ["vector", "scalar"])
                                  if cj == 0 else None))
                    if ee == "vector":
                        E("vector", lambda v, c4=c4, i=i, p=pst:
                          v.tensor_copy(xlnT_quad(c4 * 4, i * P), p[:]),
                          deps=["tensor"])
                    else:
                        E("scalar", lambda sc, c4=c4, i=i, p=pst:
                          sc.copy(xlnT_quad(c4 * 4, i * P), p[:]),
                          deps=["tensor"])
                if ee == "scalar":
                    tprev = sch.cnt["tensor"]
                if dodrip:
                    drip(3)
            for c in range(DCH):
                pst = ps_tAb if (c % 2 == 0) else ps_tBb
                E("tensor", lambda t, c=c, p=pst: t.transpose(
                    p[:, 0:8], xln_hm[:, c * P:(c + 1) * P],
                    identb[0:8, 0:8]),
                    deps=(["vector"] if ee == "vector"
                          else ["vector", "scalar"]))
                if ee == "vector":
                    E("vector", lambda v, c=c, p=pst: v.tensor_copy(
                        xlnT[c][:, 0:4], p[:, 0:4]), deps=["tensor"])
                    E("vector", lambda v, c=c, p=pst: v.tensor_copy(
                        xlnT[c][:, SG + 4:SG + 8], p[:, 4:8]))
                else:
                    E("scalar", lambda sc, c=c, p=pst: sc.copy(
                        xlnT[c][:, 0:4], p[:, 0:4]), deps=["tensor"])
                    E("scalar", lambda sc, c=c, p=pst: sc.copy(
                        xlnT[c][:, SG + 4:SG + 8], p[:, 4:8]))
            E("vector", lambda v: v.memset(cklF, 0.0),
              at=({"tensor": gate_t} if gate_t else None))
            if dodrip:
                drip(3)

        ov = {}

        def ov_tile(j):
            g = ld_groups[WGRP[j]]
            g[1] += 1
            tgt_o = g[1] * 16
            vc = sch.cnt["vector"]

            def opst(h, j=j, g=g, vc=vc):
                if vc > 0:
                    h.wait_ge(sems["vector"], vc)
                h.dma_start(out=out[j * P:(j + 1) * P, :],
                            in_=xt[j]).then_inc(g[0], 16)
            sch.ops["sync"].append(opst)
            g[1] += 1
            tgt = g[1] * 16

            def opld(h, tgt_o=tgt_o, j=j, g=g):
                h.wait_ge(g[0], tgt_o)
                h.dma_start(out=xt[j],
                            in_=xin[SG + 4 + j * P: SG + 4 + (j + 1) * P, :]
                            ).then_inc(g[0], 16)
            sch.ops["sync"].append(opld)
            ov[("xt", j)] = tgt

        def ov_collect(j):
            wait_ld("vector", ov[("xt", j)], group=WGRP[j])
            ln_collect(j, xt[j])

        for sg in range(NSG):
            base = sg * SG
            if sg == 1:
                sch.barrier()
            if sg == 0:
                tloads = []
                for i in range(NTIL):
                    tloads.append(load(xt[i], xin[base + 4 + i * P:
                                                  base + 4 + (i + 1) * P, :],
                                       deps=["vector", "tensor", "scalar"]))
                th_l = load(xh[0:4, :], xin[base + 0: base + 4, :],
                            group="small")
                th_r = load(xh[4:8, :], xin[base + SG + 4: base + SG + 8, :],
                            group="small")

                wait_ld("vector", tloads[-1])
                for i in range(NTIL):
                    ln_collect(i, xt[i])
                    drip(3)
                    keepwarm(sg)
                ln_finish("A")
                drip(3)
                if debug_taps:
                    tls = dbg("ln1stat", [P, 64], F32)
                    dbg_store(tls[:, :], stat)
                wait_ld2("vector", th_l)
                wait_ld2("vector", th_r)
                halo_ln("act")
                a_prep(dodrip=True)
                sch.barrier()

            if debug_taps and sg == 0:
                tpx = dbg("xlnT0", [P, SG + 8], BF16)
                store(tpx[:, :], xlnT[0], deps=["vector"])

            for half in range(2):
                for g in range(3):
                    ps = ps_ckl[g]
                    ncc = 3 if g < 2 else 2
                    for cc in range(ncc):
                        c = 3 * g + cc
                        off = 32 * cc
                        for k in range(K9):
                            E("tensor", lambda t, c=c, k=k, off=off, ps=ps,
                              half=half: t.matmul(
                                  ps[off:off + 18, :],
                                  cwb[:, (c * K9 + k) * 18:
                                      (c * K9 + k) * 18 + 18],
                                  xlnT[c][:, half * 512 + k:
                                          half * 512 + k + 512],
                                  start=(k == 0), stop=(k == K9 - 1)),
                              deps=(["vector", "scalar"] if (cc == 0 and k == 0)
                                    else None))
                    for cc in range(ncc):
                        off = 32 * cc
                        E("scalar", lambda sc, ps=ps, off=off, g=g, half=half:
                          sc.copy(
                              cklF[off:off + 18,
                                   g * SG + half * 512: g * SG + half * 512 + 512],
                              ps[off:off + 18, :]),
                          deps=(["tensor"] if cc == 0 else None))
                    drip(3)
            sch.barrier()

            if debug_taps and sg == 0:
                tpk = dbg("cklF", [P, 3 * SG], F32)
                store(tpk[:, :], cklF, deps=["vector"])

            sm_ev = {}
            for i in range(NTIL):
                for g in range(3):
                    slot = (i * 3 + g) % 4
                    pst = ps_zt[slot][:, 0:256].bitcast(F32)
                    at_t = ({"vector": sm_ev[slot]} if slot in sm_ev
                            else None)
                    E("tensor", lambda t, i=i, g=g, pst=pst: t.transpose(
                        pst, cklF[:, g * SG + i * P: g * SG + (i + 1) * P],
                        identf), deps=(["vector"] if (i == 0 and g == 0)
                                       else None), at=at_t)
                    E("vector", lambda v, i=i, g=g, pst=pst: v.tensor_copy(
                        ckl_tm[i][:, g * P:(g + 1) * P], pst),
                        deps=["tensor"])
                    sm_ev[slot] = sch.cnt["vector"]
                    keepwarm(sg)
                for g in range(3):
                    ncc = 3 if g < 2 else 2
                    x4 = (ckl_tm[i][:, g * P: g * P + 32 * ncc]
                          .rearrange("p (cc r) -> p cc r", r=32)[:, :, 0:18]
                          .rearrange("p cc (hh j) -> p cc hh j", j=K9))
                    gm = gmax[:, 0:2 * ncc].rearrange("p (cc hh) -> p cc hh", hh=2)
                    E("vector", lambda v, x4=x4, gm=gm: v.tensor_reduce(
                        gm, x4, axis=AX.X, op=ALU.max))
                    E("vector", lambda v: v.drain())
                    E("vector", lambda v, x4=x4, gm=gm, ncc=ncc: v.tensor_tensor(
                        x4, x4, gm.unsqueeze(3).broadcast_to([P, ncc, 2, K9]),
                        op=ALU.subtract))
                E("scalar", lambda s, i=i: s.activation(
                    ckl_tm[i], ckl_tm[i], AF.Exp), deps=["vector"])
                for g in range(3):
                    ncc = 3 if g < 2 else 2
                    x4 = (ckl_tm[i][:, g * P: g * P + 32 * ncc]
                          .rearrange("p (cc r) -> p cc r", r=32)[:, :, 0:18]
                          .rearrange("p cc (hh j) -> p cc hh j", j=K9))
                    gs = gsum[:, 0:2 * ncc].rearrange("p (cc hh) -> p cc hh", hh=2)
                    gr = grec[:, 0:2 * ncc]
                    grv = gr.rearrange("p (cc hh) -> p cc hh", hh=2)
                    wout = (wsm_tm[:, 54 * g: 54 * g + 18 * ncc]
                            .rearrange("p (cc hh j) -> p cc hh j", hh=2, j=K9))
                    E("vector", lambda v, x4=x4, gs=gs: v.tensor_reduce(
                        gs, x4, axis=AX.X, op=ALU.add),
                        deps=(["scalar"] if g == 0 else None))
                    E("vector", lambda v: v.drain())
                    E("vector", lambda v, gr=gr, ncc=ncc: v.reciprocal(
                        gr, gsum[:, 0:2 * ncc]))
                    E("vector", lambda v: v.drain())
                    E("vector", lambda v, x4=x4, grv=grv, wout=wout, ncc=ncc:
                      v.tensor_tensor(
                          wout, x4,
                          grv.unsqueeze(3).broadcast_to([P, ncc, 2, K9]),
                          op=ALU.mult))
                    keepwarm(sg)
                wsm3 = wsm_tm.rearrange("p (h k) -> p h k", k=K9)
                for k in range(K9):
                    pst = ps_tAw if k < 8 else ps_tBb
                    off = (k % 8) * P
                    E("tensor", lambda t, k=k, wsm3=wsm3, p=pst, off=off:
                      t.transpose(p[0:16, off:off + P], wsm3[:, :, k], identb),
                      deps=(["vector"] if k in (0, 8) else None))
                wk8 = (arena[0:16, 22800: 22800 + 8 * SG]
                       .rearrange("p (k r) -> p k r", r=SG)
                       [:, :, i * P:(i + 1) * P])
                E("vector", lambda v, i=i, wk8=wk8: v.tensor_copy(
                    wk8, ps_tAw[0:16, :].rearrange("p (k r) -> p k r", r=P)),
                    deps=["tensor"])
                E("vector", lambda v, i=i: v.tensor_copy(
                    wsmK[8][:, i * P:(i + 1) * P], ps_tBb[0:16, 0:P]))
                drip(4)
            sch.barrier()

            if debug_taps and sg == 0:
                tpw = dbg("wsmK0", [16, SG], BF16)
                store(tpw[:, :], wsmK[0], deps=["vector"])

            wait_ld2("tensor", t_selp)
            ps_ap = [ps_big[:, j * 512:(j + 1) * 512] for j in range(4)]
            ap_mm = {}
            ap_cp = {}
            ap_mul = {}
            cq = []
            kidx = 0
            for pair in range(NPAIR):
                sp = selp[:, pair * P:(pair + 1) * P]
                for w in range(2):
                    for k in range(K9):
                        slot = kidx % 4
                        wslot = kidx % 4
                        at_mm = {}
                        if kidx >= 4:
                            at_mm["scalar"] = ap_cp[kidx - 4]
                        E("tensor", lambda t, k=k, w=w, sp=sp, slot=slot:
                          t.matmul(ps_ap[slot], sp,
                                   wsmK[k][:, w * 512:(w + 1) * 512],
                                   start=True, stop=True),
                          at=(at_mm or None),
                          deps=(["vector"] if kidx == 0 else None))
                        ap_mm[kidx] = sch.cnt["tensor"]
                        at_cp = {"tensor": ap_mm[kidx]}
                        if kidx >= 4:
                            at_cp["vector"] = ap_mul[kidx - 4]
                        E("scalar", lambda s, slot=slot, wslot=wslot: s.copy(
                            wsb[wslot], ps_ap[slot]), at=at_cp)
                        ap_cp[kidx] = sch.cnt["scalar"]
                        if cq and pair >= 4:
                            fn, at_q = cq.pop(0)
                            E("scalar", fn, at=at_q)
                        xsl = xlnT[pair][:, w * 512 + k: w * 512 + k + 512]
                        if k < 4:
                            E("vector", lambda v, xsl=xsl, k=k, wslot=wslot:
                              v.tensor_tensor(prods[k], xsl, wsb[wslot],
                                              op=ALU.mult),
                              at={"scalar": ap_cp[kidx]})
                        elif k < 8:
                            E("vector", lambda v, xsl=xsl, k=k, wslot=wslot:
                              v.tensor_tensor(prods[4], xsl, wsb[wslot],
                                              op=ALU.mult),
                              at={"scalar": ap_cp[kidx]})
                            E("vector", lambda v, k=k: v.tensor_tensor(
                                prods[k - 4], prods[k - 4], prods[4],
                                op=ALU.add))
                        else:
                            E("vector", lambda v, xsl=xsl, k=k, wslot=wslot:
                              v.tensor_tensor(prods[4], xsl, wsb[wslot],
                                              op=ALU.mult),
                              at={"scalar": ap_cp[kidx]})
                        ap_mul[kidx] = sch.cnt["vector"]
                        kidx += 1
                    E("vector", lambda v: v.tensor_tensor(
                        prods[0], prods[0], prods[1], op=ALU.add))
                    E("vector", lambda v: v.tensor_tensor(
                        prods[2], prods[2], prods[3], op=ALU.add))
                    E("vector", lambda v: v.tensor_tensor(
                        prods[0], prods[0], prods[2], op=ALU.add))
                    E("vector", lambda v, w=w: v.tensor_tensor(
                        apair[:, w * 512:(w + 1) * 512], prods[0], prods[4],
                        op=ALU.add))
                    drip(3)
                for i4 in range(2):
                    pst = ps_tAb if (i4 == 0) else ps_tBb
                    for j in range(4):
                        i = i4 * 4 + j
                        E("tensor", lambda t, i=i, j=j, p=pst: t.transpose(
                            p[:, j * P:(j + 1) * P],
                            apair[:, i * P:(i + 1) * P], identb),
                            deps=(["vector"] if j == 0 else None))
                    dstv = xall.rearrange("p (t r) -> p t r", r=D)[
                        :, i4 * 4:(i4 + 1) * 4, pair * P:(pair + 1) * P]
                    E("vector", lambda v, dstv=dstv, p=pst: v.tensor_tensor(
                        dstv, dstv, p[:].rearrange("p (t r) -> p t r", r=P),
                        op=ALU.add), deps=["tensor"])
                    if (pair == 5 and i4 == 1 and sg == 0
                            and not w0_serial and not skip_ffn
                            and not ov.get("w0done")):
                        drip(len(w0_steps))
                        w0_post()
                        ov["w0done"] = True
                    if pair == 3:
                        vg3 = sch.cnt["vector"]
                        cdst = x2lnT2.bitcast(F32)[:, 0:512]
                        for j in range(4):
                            i = i4 * 4 + j
                            at3 = {"vector": vg3} if (j == 0) else None
                            cq.append((lambda sc, i=i, cdst=cdst:
                                       sc.activation(
                                           cdst, xt[i][:, 0:512], AF.Copy,
                                           accum_out=stat[:, 16 + i:17 + i]),
                                       at3))
                            cq.append((lambda sc, i=i: sc.activation(
                                lnsq, xt[i][:, 0:512], AF.Square,
                                accum_out=stat[:, 24 + i:25 + i]), None))
                    if pair == NPAIR - 1:
                        vg7 = sch.cnt["vector"]
                        for j in range(4):
                            i = i4 * 4 + j
                            E("vector", lambda v, i=i: v.reduce_sum(
                                stat[:, 56 + i:57 + i], xt[i][:, 512:1024],
                                axis=AX.X))
                            E("scalar", lambda sc, i=i: sc.activation(
                                lnsq, xt[i][:, 512:1024], AF.Square,
                                accum_out=stat[:, 32 + i:33 + i]),
                                at=({"vector": vg7} if j == 0 else None))
                        E("vector", lambda v: v.drain())
            sch.barrier()

            if debug_taps and sg == 0:
                tpa = dbg("x1", [SG, D], F32)
                for i in range(NTIL):
                    store(tpa[i * P:(i + 1) * P, :], xt[i], deps=["vector"])

            if sg == 0:
                if not w0_serial and not ov.get("w0done"):
                    drip(len(w0_steps))
                    w0_post()
                    ov["w0done"] = True
                tgt_w = w0_state["last_st"]
                sch.ops["sync"].append(lambda h, t=tgt_w: h.wait_ge(s_st, t))
            if skip_ffn:
                for i in range(NTIL):
                    store(out[base + i * P: base + (i + 1) * P, :], xt[i],
                          deps=["vector"])
                continue
            sg_pages = []
            for q4 in range(4):
                grpq = WGRP[q4]
                for pg in range(4 * q4, 4 * q4 + 4):
                    tgtq = load(arena[:, pg * 2048:(pg + 1) * 2048],
                                wfct_scr[pg, :, :],
                                deps=(["tensor", "vector", "scalar"]
                                      if pg == 4 * q4 else None), group=grpq)
                sg_pages.append((grpq, tgtq))
            tw = None

            ln_finish("B", seed=0.6, iters=5, split_sums=True)
            if debug_taps and sg == 0:
                tls2 = dbg("ln2stat", [P, 64], F32)
                dbg_store(tls2[:, :], stat)

            prev = {}
            for i in range(NTIL):
                tm = x2ln_tm_b[i % 2]
                tT = x2lnT_b[i % 2]
                E("vector", lambda v, i=i, tm=tm: v.tensor_scalar(
                    tm, xt[i], stat[:, i:i + 1], stat[:, 8 + i:9 + i],
                    op0=ALU.subtract, op1=ALU.mult))
                va = sch.cnt["vector"]
                ev1 = []
                for c4 in range(2):
                    pst = ps_tAb if (c4 == 0) else ps_tBb
                    dep_at = {"vector": va if "t1ev" not in prev
                              else max(va, prev["t1ev"][c4])}
                    for cj in range(4):
                        c = c4 * 4 + cj
                        E("tensor", lambda t, c=c, cj=cj, p=pst, tm=tm:
                          t.transpose(p[:, cj * P:(cj + 1) * P],
                                      tm[:, c * P:(c + 1) * P], identb),
                          at=(dep_at if cj == 0 else None))
                    E("vector", lambda v, c4=c4, p=pst, tT=tT: v.tensor_copy(
                        tT[:, c4 * 512:(c4 + 1) * 512], p[:]),
                        deps=["tensor"])
                    ev1.append(sch.cnt["vector"])
                if i > 0:
                    E("vector", lambda v, j=i - 1: v.scalar_tensor_tensor(
                        xt[j], ps_proj, stat[:, 42:43], xt[j],
                        op0=ALU.mult, op1=ALU.add),
                        at={"tensor": prev["proj"], "scalar": prev["sq"]})
                    prev["evac"] = sch.cnt["vector"]
                    if sg == 0 and not skip_ffn:
                        ov_tile(i - 1)
                        if i >= 2:
                            ov_collect(i - 2)

                sqq_hist = prev.get("sqq_hist", [0, 0])
                for quart in range(4):
                    half, wq = quart // 2, (quart % 2) * 2
                    for c in range(DCH):
                        for w in (wq, wq + 1):
                            at_ = None
                            if c == 0 and w == wq:
                                at_ = {"vector": ev1[1],
                                       "scalar": sqq_hist[-2]}
                                if i == 0:
                                    grpq, tgtq = sg_pages[quart]
                                    wait_ld("tensor", tgtq, group=grpq)
                            j0 = half * 8 + 2 * w
                            rhsv = (arena.rearrange("p (j r) -> p j r", r=2048)
                                    [:, j0:j0 + 2, :]
                                    .rearrange("p j (a u) -> p j a u", u=1024)
                                    [:, :, :, c * P:(c + 1) * P])
                            E("tensor", lambda t, c=c, w=w, half=half, tT=tT,
                              rhsv=rhsv: t.matmul(
                                ps_fc[:, w * 512:(w + 1) * 512],
                                tT[:, c * P:(c + 1) * P], rhsv,
                                start=(c == 0), stop=(c == DCH - 1)),
                                at=at_)
                    gsl = gbuf[:, (quart % 2) * 1024:(quart % 2 + 1) * 1024]
                    at_g = {"tensor": sch.cnt["tensor"]}
                    if quart == 2:
                        at_g["vector"] = prev.get("gcp", 0)
                    E("scalar", lambda s, quart=quart, gsl=gsl, at_g=at_g:
                      s.activation(
                        gsl, ps_fc[:, (quart % 2) * 1024:
                                   (quart % 2 + 1) * 1024],
                        AF.Gelu, scale=0.5,
                        accum_out=accg[:, quart:quart + 1]), at=at_g)
                    if quart == 3:
                        prev["gq3"] = sch.cnt["scalar"]
                    E("scalar", lambda s, quart=quart, gsl=gsl: s.activation(
                        zscr[:, 0:2048][:, (quart % 2) * 1024:
                                        (quart % 2 + 1) * 1024], gsl,
                        AF.Square, accum_out=accg[:, 4 + quart:5 + quart]))
                    sqq_hist.append(sch.cnt["scalar"])
                    if quart == 1:
                        E("vector", lambda v: v.tensor_copy(
                            zscr[:, 2048:4096], gbuf), deps=["scalar"])
                        prev["gcp"] = sch.cnt["vector"]
                prev["sq"] = sch.cnt["scalar"]
                prev["sqq_hist"] = sqq_hist[-2:]
                if sg == 0 and i == NTIL - 1:
                    ov["fc_last"] = sch.cnt["tensor"]
                E("vector", lambda v: v.tensor_reduce(
                    stat[:, 40:41], accg[:, 0:4], axis=AX.X, op=ALU.add),
                    at={"scalar": prev["gq3"]})
                E("vector", lambda v: v.drain())
                E("vector", lambda v: v.tensor_scalar(
                    stat[:, 40:41], stat[:, 40:41], 1.0 / H, None,
                    op0=ALU.mult))
                E("vector", lambda v: v.drain())
                vzq = []
                for zq in range(4):
                    if zq < 2:
                        src_z = zscr[:, 2048 + zq * 1024: 3072 + zq * 1024]
                    else:
                        src_z = gbuf[:, (zq - 2) * 1024:(zq - 1) * 1024]
                    E("vector", lambda v, zq=zq, src_z=src_z: v.tensor_scalar(
                        zscr[:, zq * 1024:(zq + 1) * 1024], src_z,
                        stat[:, 40:41], 0.5,
                        op0=ALU.subtract, op1=ALU.mult))
                    vzq.append(sch.cnt["vector"])
                vz0, vz = vzq[1], vzq[3]
                E("vector", lambda v: v.tensor_reduce(
                    stat[:, 41:42], accg[:, 4:8], axis=AX.X, op=ALU.add),
                    at={"scalar": prev["sq"]})
                E("vector", lambda v: v.drain())
                E("vector", lambda v: v.tensor_scalar(
                    stat[:, 41:42], stat[:, 41:42], 1.0 / H, None,
                    op0=ALU.mult))
                E("vector", lambda v: v.tensor_tensor(
                    stat[:, 42:43], stat[:, 40:41], stat[:, 40:41],
                    op=ALU.mult))
                E("vector", lambda v: v.drain())
                E("vector", lambda v: v.tensor_tensor(
                    stat[:, 41:42], stat[:, 41:42], stat[:, 42:43],
                    op=ALU.subtract))
                E("vector", lambda v: v.drain())
                E("vector", lambda v: v.tensor_scalar(
                    stat[:, 41:42], stat[:, 41:42], 1.0, LN_EPS,
                    op0=ALU.mult, op1=ALU.add))
                E("vector", lambda v: v.drain())
                evac_cnt = []
                for b4 in range(8):
                    pst = ps_zt[b4 % 4][:, 0:512]
                    need_z = vzq[b4 // 2]
                    dep_at = {"vector": (need_z if b4 < 4
                                         else max(need_z, evac_cnt[b4 - 4])),
                              "scalar": prev["sq"]}
                    for j in range(4):
                        hc = b4 * 4 + j
                        E("tensor", lambda t, hc=hc, j=j, p=pst: t.transpose(
                            p[:, j * P:(j + 1) * P],
                            zscr[:, hc * P:(hc + 1) * P], identb),
                            at=(dep_at if j == 0 else None))
                    E("vector", lambda v, b4=b4, p=pst: v.tensor_copy(
                        zT[:, b4 * 512:(b4 + 1) * 512], p[:]), deps=["tensor"])
                    evac_cnt.append(sch.cnt["vector"])
                prev["t1ev"] = [0, ev1[1]]
                newton_rs(stat[:, 41:42], stat[:, 42:43], 0.064, 4)
                E("vector", lambda v: v.tensor_tensor(
                    scol, stat[:, 41:42], stat[:, 42:43], op=ALU.mult))
                at_p = {"vector": evac_cnt[-1]}
                if "evac" in prev:
                    at_p["vector"] = max(at_p["vector"], prev["evac"])
                for hc in range(HCH):
                    for w in range(2):
                        E("tensor", lambda t, hc=hc, w=w: t.matmul(
                            ps_proj[:, w * 512:(w + 1) * 512],
                            zT[:, hc * P:(hc + 1) * P],
                            wpT[hc][:, w * 512:(w + 1) * 512],
                            start=(hc == 0), stop=False),
                            at=(at_p if (hc == 0 and w == 0) else None))
                E("tensor", lambda t: t.transpose(
                    ps_tAb[0:1, 0:P], scol, identb), deps=["vector"])
                E("vector", lambda v: v.tensor_copy(
                    srow, ps_tAb[0:1, 0:P]), deps=["tensor"])
                prev["t1ev"][0] = sch.cnt["vector"]
                for w in range(2):
                    E("tensor", lambda t, w=w: t.matmul(
                        ps_proj[:, w * 512:(w + 1) * 512], srow,
                        cvec[:, w * 512:(w + 1) * 512], start=False,
                        stop=(w == 1)), deps=(["vector"] if w == 0 else None))
                prev["proj"] = sch.cnt["tensor"]
            E("vector", lambda v: v.scalar_tensor_tensor(
                xt[NTIL - 1], ps_proj, stat[:, 42:43], xt[NTIL - 1],
                op0=ALU.mult, op1=ALU.add),
                at={"tensor": prev["proj"]})
            if sg == 0:
                ov_tile(NTIL - 1)
                ov_collect(NTIL - 2)
                ov_collect(NTIL - 1)
                th_l = load(xh[0:4, :], xin[SG + 0: SG + 4, :],
                            group="small")
                th_r = load(xh[4:8, :], xin[2 * SG + 4: 2 * SG + 8, :],
                            group="small")
                ln_finish("B", seed=0.6, iters=5)
                wait_ld2("vector", th_l)
                wait_ld2("vector", th_r)
                halo_ln("newton", gate_t=ov["fc_last"])
                a_prep(gate_t=ov["fc_last"])
            else:
                sch.barrier()
                for i in range(NTIL):
                    store(out[base + i * P: base + (i + 1) * P, :], xt[i],
                          deps=["vector"])

        sch.ops["gpsimd"].append(lambda h: h.wait_ge(s_st, st_cnt[0] * 16))

        with nc.Block() as block:
            @block.sync
            def _(sync):
                for op in sch.ops["sync"]:
                    op(sync)

            @block.tensor
            def _(tensor):
                for op in sch.ops["tensor"]:
                    op(tensor)

            @block.vector
            def _(vector):
                for op in sch.ops["vector"]:
                    op(vector)

            @block.scalar
            def _(scalar):
                for op in sch.ops["scalar"]:
                    op(scalar)

            @block.gpsimd
            def _(gpsimd):
                for op in sch.ops["gpsimd"]:
                    op(gpsimd)

    return nc, taps



def _pack_inputs(L0_attn_query, L0_attn_conv_weight, L0_ffn_W_fc,
                 L0_ffn_ln_w, L0_ffn_ln_b, L0_ffn_W_proj):
    x = np.asarray(L0_attn_query, dtype=np.float32)
    cw = np.asarray(L0_attn_conv_weight, dtype=np.float32)
    wfc = np.ascontiguousarray(np.asarray(L0_ffn_W_fc, dtype=np.float32))
    wproj = np.ascontiguousarray(np.asarray(L0_ffn_W_proj, dtype=np.float32))
    lnw = np.asarray(L0_ffn_ln_w, dtype=np.float32)
    lnb = np.asarray(L0_ffn_ln_b, dtype=np.float32)

    B, S, _ = x.shape
    Q = S // 4
    cwp = np.zeros((P, DCH * K9 * 18), np.float32)
    for c in range(DCH):
        for k in range(K9):
            for hh in range(2):
                h = 2 * c + hh
                for j in range(K9):
                    cwp[hh * HS:(hh + 1) * HS, (c * K9 + k) * 18 + hh * K9 + j] = \
                        cw[h * K9 + j, :, k]
    lnw_d = np.ascontiguousarray(lnw.reshape(HCH, P).T)
    lnb_d = np.ascontiguousarray(lnb.reshape(HCH, P).T)
    identb = np.eye(P, dtype=ml_dtypes.bfloat16)
    identf = np.eye(P, dtype=np.float32)
    onesf = np.ones((P, P), np.float32)
    selp = np.zeros((16, NPAIR * P), ml_dtypes.bfloat16)
    for pair in range(NPAIR):
        selp[2 * pair, pair * P: pair * P + HS] = 1
        selp[2 * pair + 1, pair * P + HS: (pair + 1) * P] = 1

    in_maps = []
    for b in range(B):
        for q in range(4):
            lo, hi = q * Q, (q + 1) * Q
            sl = x[b, max(0, lo - 4):min(S, hi + 4), :]
            padl = np.zeros((4 - (lo - max(0, lo - 4)), D), np.float32)
            padr = np.zeros((4 - (min(S, hi + 4) - hi), D), np.float32)
            shard = np.ascontiguousarray(np.concatenate([padl, sl, padr], axis=0))
            in_maps.append({
                "xin": shard, "wfc": wfc, "wproj": wproj,
                "lnw_d": lnw_d, "lnb_d": lnb_d, "cwpack": cwp,
                "identb_d": identb, "identf_d": identf, "onesf_d": onesf,
                "selp_d": selp,
            })
    return in_maps


_cached = {}


def kernel(L0_attn_query, L0_attn_conv_weight, L0_ffn_W_fc,
           L0_ffn_ln_w, L0_ffn_ln_b, L0_ffn_W_proj):
    in_maps = _pack_inputs(L0_attn_query, L0_attn_conv_weight, L0_ffn_W_fc,
                           L0_ffn_ln_w, L0_ffn_ln_b, L0_ffn_W_proj)
    if "nc" not in _cached:
        _cached["nc"] = build_program(debug_taps=False)[0]
    res = run_bass_kernel_spmd(_cached["nc"], in_maps, core_ids=list(range(8)))
    B, S = 2, 8192
    Q = S // 4
    outp = np.empty((B, S, D), np.float32)
    ci = 0
    for b in range(B):
        for q in range(4):
            outp[b, q * Q:(q + 1) * Q, :] = res.results[ci]["out"]
            ci += 1
    return outp


# version 36
# speedup vs baseline: 1.0105x; 1.0075x over previous
import contextlib
import numpy as np
import ml_dtypes
import concourse.bass as bass
import concourse.mybir as mybir
from concourse.bass_utils import run_bass_kernel_spmd

F32 = mybir.dt.float32
BF16 = mybir.dt.bfloat16
AF = mybir.ActivationFunctionType
ALU = mybir.AluOpType
AX = mybir.AxisListType

P = 128
SG = 1024
NSG = 2
TCORE = SG * NSG
D = 1024
DCH = D // P
H = 4096
HCH = H // P
HEADS, HS, K9 = 16, 64, 9
NPAIR = HEADS // 2
NTIL = SG // P
LN_EPS = 1e-5
ABS_EPS = 1e-6

ENGS = ("sync", "tensor", "vector", "scalar", "gpsimd")


class Sched:
    def __init__(self):
        self.ops = {e: [] for e in ENGS}
        self.cnt = {e: 0 for e in ENGS}
        self.sems = None

    def emit(self, eng, fn, deps=None, inc=True, at=None):
        waits = []
        if deps:
            for d in deps:
                if d == eng:
                    continue
                c = self.cnt[d]
                if c > 0:
                    waits.append((d, c))
        if at:
            for d, c in at.items():
                if d != eng and c > 0:
                    waits.append((d, c))
        my_inc = None
        if inc:
            self.cnt[eng] += 1
            my_inc = self.cnt[eng]
        sems = lambda: self.sems

        def op(h):
            for (d, c) in waits:
                h.wait_ge(sems()[d], c)
            last = fn(h)
            if my_inc is not None:
                if last is None:
                    last = h.nop()
                last.then_inc(sems()[eng], 1)
        self.ops[eng].append(op)

    def barrier(self, engs=("tensor", "vector", "scalar")):
        snap = dict(self.cnt)
        sems = lambda: self.sems
        for e in engs:
            waits = [(d, snap[d]) for d in engs if d != e and snap[d] > 0]

            def op(h, waits=waits):
                for (d, c) in waits:
                    h.wait_ge(sems()[d], c)
            self.ops[e].append(op)


def build_program(debug_taps=False, skip_ffn=False, paranoid=(), oldrs=False, w0_serial=False, w0_trunc=None):
    nc = bass.Bass("TRN2", target_bir_lowering=False, debug=False)

    xin = nc.dram_tensor("xin", [TCORE + 8, D], F32, kind="ExternalInput").ap()
    wfc = nc.dram_tensor("wfc", [H, D], F32, kind="ExternalInput").ap()
    wproj = nc.dram_tensor("wproj", [D, H], F32, kind="ExternalInput").ap()
    lnw_in = nc.dram_tensor("lnw_d", [P, HCH], F32, kind="ExternalInput").ap()
    lnb_in = nc.dram_tensor("lnb_d", [P, HCH], F32, kind="ExternalInput").ap()
    cw_in = nc.dram_tensor("cwpack", [P, DCH * K9 * 18], F32, kind="ExternalInput").ap()
    identb_in = nc.dram_tensor("identb_d", [P, P], BF16, kind="ExternalInput").ap()
    identf_in = nc.dram_tensor("identf_d", [P, P], F32, kind="ExternalInput").ap()
    onesf_in = nc.dram_tensor("onesf_d", [P, P], F32, kind="ExternalInput").ap()
    selp_in = nc.dram_tensor("selp_d", [16, NPAIR * P], BF16, kind="ExternalInput").ap()
    out = nc.dram_tensor("out", [TCORE, D], F32, kind="ExternalOutput").ap()
    wfct_scr = nc.dram_tensor("wfct_scr", [16, P, 2048], BF16,
                              kind="Internal").ap()

    taps = {}

    def dbg(name, shape, dtype=F32):
        taps[name] = nc.dram_tensor("tap_" + name, shape, dtype,
                                    kind="ExternalOutput").ap()
        return taps[name]

    sch = Sched()
    ctx = contextlib.ExitStack()
    with ctx:
        sb = lambda name, shape, dt: ctx.enter_context(
            nc.sbuf_tensor(name, shape, dt)).ap()

        xall = sb("xall", [P, NTIL * D], F32)
        xt = [xall[:, i * D:(i + 1) * D] for i in range(NTIL)]
        xh = sb("xh", [8, D], F32)
        wpT_all = sb("wpT_all", [P, HCH * D], BF16)
        wpT = [wpT_all[:, i * D:(i + 1) * D] for i in range(HCH)]
        wsum = sb("wsum", [P, 64], F32)
        wfstg = sb("wfstg", [P, D], BF16)
        arena = sb("arena", [P, 32768], BF16)
        gbuf = sb("gbuf", [P, 2048], F32)
        zscr = sb("zscr", [P, H], BF16)
        zT = sb("zT", [P, HCH * P], BF16)
        x2ln_tm2 = sb("x2ln_tm2", [P, 2 * D], BF16)
        x2ln_tm_b = [x2ln_tm2[:, 0:D], x2ln_tm2[:, D:2 * D]]
        x2lnT2 = sb("x2lnT2", [P, 2 * DCH * P], BF16)
        x2lnT_b = [x2lnT2[:, 0:DCH * P], x2lnT2[:, DCH * P:2 * DCH * P]]
        stat = sb("stat", [P, 64], F32)
        accg = sb("accg", [P, 8], F32)
        lnw_t = sb("lnw_t", [P, HCH], F32)
        lnb_t = sb("lnb_t", [P, HCH], F32)
        lnb_b = sb("lnb_b", [P, HCH], BF16)
        cwb = sb("cwb", [P, DCH * K9 * 18], BF16)
        identb = sb("sb_identb", [P, P], BF16)
        identf = sb("sb_identf", [P, P], F32)
        onesf = sb("sb_onesf", [P, P], F32)
        onesb = sb("onesb", [1, P], BF16)
        selp = sb("sb_selp", [16, NPAIR * P], BF16)
        gam = sb("gam", [P, 4], F32)
        epsv = sb("epsv", [P, 1], F32)
        cvec = sb("cvec", [1, D], BF16)
        scol = sb("scol", [P, 1], BF16)
        srow = sb("srow", [1, P], BF16)

        wfT = [arena[:, c * 4096:(c + 1) * 4096] for c in range(DCH)]
        xlnT = [arena[:, c * 1040: c * 1040 + SG + 8] for c in range(DCH)]
        xln_tm = arena[:, 8320:9344]
        xln_hm = arena[0:8, 9344:10368]
        cklF = arena[:, 10368:16512].bitcast(F32)
        ckl_tm = [arena[:, 16512 + i * 768: 16512 + (i + 1) * 768].bitcast(F32)
                  for i in range(NTIL)]
        wsm_tm = arena[:, 22656:22800]
        wsmK = [arena[0:16, 22800 + k * SG: 22800 + (k + 1) * SG]
                for k in range(K9)]
        gmax = arena[:, 32016:32048].bitcast(F32)
        gsum = arena[:, 32048:32080].bitcast(F32)
        grec = arena[:, 32080:32112].bitcast(F32)
        prods = [arena[:, 10368 + j * 512: 10368 + (j + 1) * 512]
                 for j in range(5)]
        wsb = [arena[:, 10368 + (5 + m) * 512: 10368 + (6 + m) * 512]
               for m in range(4)]
        apair = arena[:, 14976:16000]
        wtile2b = [gbuf, zT.bitcast(F32),
                   wpT_all[:, 0:4096].bitcast(F32),
                   wpT_all[:, 4096:8192].bitcast(F32),
                   wpT_all[:, 8192:12288].bitcast(F32),
                   wpT_all[:, 12288:16384].bitcast(F32),
                   wpT_all[:, 16384:20480].bitcast(F32),
                   wpT_all[:, 20480:24576].bitcast(F32)]
        sgnA2 = zscr.bitcast(F32)
        wfstg2 = x2lnT2
        wqn = x2ln_tm2
        lnsq = wfstg.bitcast(F32)

        psv = lambda name, shape, dt: ctx.enter_context(
            nc.psum_tensor(name, shape, dt)).ap()
        ps_big = psv("ps_big", [P, 3072], F32)
        ps_tA = psv("ps_tA", [P, 512], F32)
        ps_tB = psv("ps_tB", [P, 512], F32)

        ps_fc = ps_big[:, 0:2048]
        ps_proj = ps_big[:, 2048:3072]
        ps_ckl = [ps_big[:, 0:512], ps_big[:, 512:1024], ps_big[:, 1024:1536]]
        ps_wb = [ps_big[:, 1536:2048], ps_big[:, 2048:2560]]
        ps_c = ps_big[0:1, 2048:3072]
        ps_bc = ps_big[:, 2560:2561]
        ps_tAb = ps_tA[:, 0:256].bitcast(BF16)
        ps_w45 = [ps_big[:, 2048:2560].bitcast(BF16),
                  ps_big[:, 2560:3072].bitcast(BF16)]
        ps_tAw = ps_tA.bitcast(BF16)
        ps_zt = [ps_big[:, b * 512:(b + 1) * 512].bitcast(BF16)
                 for b in range(4)]
        ps_tBb = ps_tB[:, 0:256].bitcast(BF16)
        ps_tm = ps_tB[:, 0:128]

        sems = {e: ctx.enter_context(nc.semaphore("s_" + e)) for e in ENGS}
        sch.sems = sems
        s_ld = ctx.enter_context(nc.semaphore("s_ld"))
        s_ld2 = ctx.enter_context(nc.semaphore("s_ld2"))
        s_lda = ctx.enter_context(nc.semaphore("s_lda"))
        s_ldb = ctx.enter_context(nc.semaphore("s_ldb"))
        s_st = ctx.enter_context(nc.semaphore("s_st"))
        s_w0v = ctx.enter_context(nc.semaphore("s_w0v"))
        s_w0s = ctx.enter_context(nc.semaphore("s_w0s"))
        s_w0t = ctx.enter_context(nc.semaphore("s_w0t"))
        s_w0e = ctx.enter_context(nc.semaphore("s_w0e"))
        s_ldc = ctx.enter_context(nc.semaphore("s_ldc"))
        s_ldd = ctx.enter_context(nc.semaphore("s_ldd"))
        s_lde = ctx.enter_context(nc.semaphore("s_lde"))
        s_ldf = ctx.enter_context(nc.semaphore("s_ldf"))
        s_ldg = ctx.enter_context(nc.semaphore("s_ldg"))
        s_ldh = ctx.enter_context(nc.semaphore("s_ldh"))
        ld_groups = {"main": [s_ld, 0], "small": [s_ld2, 0],
                     "wa": [s_lda, 0], "wb": [s_ldb, 0],
                     "wc": [s_ldc, 0], "wd": [s_ldd, 0],
                     "we": [s_lde, 0], "wf": [s_ldf, 0],
                     "wg": [s_ldg, 0], "wh": [s_ldh, 0]}
        WGRP = ("wa", "wb", "wc", "wd", "we", "wf", "wg", "wh")
        st_cnt = [0]

        def load(dst, src, deps=None, group="main"):
            g = ld_groups[group]
            g[1] += 1
            tgt = g[1] * 16
            sem = g[0]
            waits = [(d, sch.cnt[d]) for d in (deps or []) if sch.cnt[d] > 0]

            def op(h):
                for (d, c) in waits:
                    h.wait_ge(sems[d], c)
                h.dma_start(out=dst, in_=src).then_inc(sem, 16)
            sch.ops["sync"].append(op)
            return tgt

        def store(dst, src, deps=None):
            st_cnt[0] += 1
            tgt = st_cnt[0] * 16
            waits = [(d, sch.cnt[d]) for d in (deps or []) if sch.cnt[d] > 0]

            def op(h):
                for (d, c) in waits:
                    h.wait_ge(sems[d], c)
                h.dma_start(out=dst, in_=src).then_inc(s_st, 16)
            sch.ops["sync"].append(op)
            return tgt

        def wait_ld(eng, tgt, group="main"):
            sem = ld_groups[group][0]
            sch.ops[eng].append(lambda h: h.wait_ge(sem, tgt))

        def wait_ld2(eng, tgt):
            wait_ld(eng, tgt, group="small")

        def wait_st(eng, tgt):
            sch.ops[eng].append(lambda h: h.wait_ge(s_st, tgt))

        def dbg_store(dst, src):
            tgt = store(dst, src, deps=["vector", "scalar", "tensor"])
            for e in ("vector", "scalar", "tensor"):
                wait_st(e, tgt)

        E = sch.emit

        load(identb, identb_in[:, :])
        load(identf, identf_in[:, :])
        load(onesf, onesf_in[:, :])
        t_selp = load(selp, selp_in[:, :], group="small")
        load(lnw_t, lnw_in[:, :])
        load(lnb_t, lnb_in[:, :])
        cw_f = gbuf[:, 0:DCH * K9 * 18]
        t1 = load(cw_f, cw_in[:, :])
        wait_ld("vector", t1)
        E("vector", lambda v: v.tensor_copy(cwb, cw_f))
        E("vector", lambda v: v.tensor_copy(lnb_b, lnb_t))
        E("vector", lambda v: v.tensor_copy(onesb, onesf[0:1, :]))
        E("vector", lambda v: v.memset(epsv, LN_EPS))

        w0_steps = []
        w0_state = {"last_st": 0}
        gscr = zscr.bitcast(F32)[:, 0:1024]
        w0c = {"v": 0, "s": 0, "t": 0, "sgn": {}, "tt": {}, "tp": {},
               "ev": {0: 0, 1: 0}, "gamma_v0": 0, "gamma_v1": 0}

        def w0_chunks():
            out = []
            for i in range(16):
                src = wfc.rearrange("(a p) d -> p (a d)", p=P)[
                    :, 0:2048] if False else None
                out.append(("fc", i))
            for i in range(16):
                out.append(("pj", i))
            return out

        def w0_src(kind, i):
            if kind == "fc":
                return (wfc[256 * i: 256 * (i + 1), :]
                        .rearrange("(a p) d -> p a d", p=P))
            q0 = 2 * i
            rb = q0 // 4
            qa, qb = q0 % 4, (q0 + 1) % 4
            return (wproj[rb * P:(rb + 1) * P, qa * D:(qb + 1) * D]
                    .rearrange("p (a d) -> p a d", d=D))

        def w0_load(dst, src, grp, s_gate=0, v_gate=0, queue="sync"):
            g = ld_groups[grp]
            g[1] += 1
            tgt = g[1] * 16

            def op(h, tgt=tgt, s_gate=s_gate, v_gate=v_gate, g=g):
                if s_gate > 0:
                    h.wait_ge(s_w0s, s_gate)
                if v_gate > 0:
                    h.wait_ge(s_w0v, v_gate)
                h.dma_start(out=dst, in_=src).then_inc(g[0], 16)
            sch.ops[queue].append(op)
            return tgt

        FCB = (0, 1, 6, 7)
        PJB = (2, 3, 4, 5)

        def gamma_step(j, kind, i, bsel):
            def go():
                bmap = FCB if bsel == 0 else PJB
                bi = bmap[j % 4]
                buf = wtile2b[bi]
                grp = (WGRP[{0: 0, 1: 1, 6: 2, 7: 3}[bi]] if bsel == 0
                       else WGRP[4 + PJB.index(bi)])
                tgt = w0_load(buf.rearrange("p (a d) -> p a d", d=D),
                              w0_src(kind, i), grp,
                              s_gate=w0c["sgn"].get(("g", bsel, j - 4), 0),
                              queue=("gpsimd" if bsel else "sync"))
                w0c["s"] += 1
                w0c["sgn"][("g", bsel, j)] = w0c["s"]

                def op(h, tgt=tgt, j=j, grp=grp, bi=bi):
                    h.wait_ge(ld_groups[grp][0], tgt)
                    h.activation(wtile2b[bi][:, 0:1024],
                                 wtile2b[bi][:, 0:1024], AF.Abs,
                                 accum_out=wsum[:, 2 * j:2 * j + 1])
                    h.activation(wtile2b[bi][:, 1024:2048],
                                 wtile2b[bi][:, 1024:2048], AF.Abs,
                                 accum_out=wsum[:, 2 * j + 1:2 * j + 2]
                                 ).then_inc(s_w0s, 1)
                sch.ops["scalar"].append(op)
            return go

        def gamma_final(which, sgate_needed):
            def go():
                red = stat[:, 60 + which:61 + which]
                w0c["v"] += 0
                def opv(h, sg_=sgate_needed(), which=which, red=red):
                    h.wait_ge(s_w0s, sg_)
                    h.reduce_sum(red, wsum[:, 32 * which:32 * which + 32],
                                 axis=AX.X)
                    h.drain().then_inc(s_w0v, 1)
                w0c["v"] += 1
                vred = w0c["v"]
                sch.ops["vector"].append(opv)
                nelem = float(H * D)
                w0c["t"] += 1
                tmark = w0c["t"]

                evg = w0c["ev"][0]

                def opt(h, red=red, vred=vred, evg=evg):
                    h.wait_ge(s_w0v, vred)
                    if evg > 0:
                        h.wait_ge(s_w0s, evg)
                    h.matmul(ps_bc, onesf, red, start=True,
                             stop=True).then_inc(s_w0t, 1)
                sch.ops["tensor"].append(opt)
                w0c["v"] += 1

                def opv2(h, tmark=tmark, which=which):
                    h.wait_ge(s_w0t, tmark)
                    c0 = 2 * which
                    h.tensor_scalar(gam[:, c0:c0 + 1], ps_bc,
                                    0.5 / nelem, 0.5 * ABS_EPS,
                                    op0=ALU.mult, op1=ALU.add)
                    h.drain()
                    h.tensor_scalar(gam[:, c0 + 1:c0 + 2],
                                    gam[:, c0:c0 + 1], -1.0, None,
                                    op0=ALU.mult)
                    h.drain().then_inc(s_w0v, 1)
                sch.ops["vector"].append(opv2)
                w0c["gamma_v%d" % which] = w0c["v"]
            return go

        def tern_step(jj, kind, i):
            def go():
                j = jj
                if kind == "fc":
                    bi = FCB[jj % 4]
                    pv = jj - 4
                else:
                    bi = (0, 1)[jj % 2]
                    pv = jj - 4 if jj < 18 else jj - 2
                grp = WGRP[{0: 0, 1: 1, 6: 2, 7: 3}[bi]]
                buf = wtile2b[bi]
                sg_ = w0c["sgn"].get(("t", pv))
                if sg_ is None:
                    sg_ = w0c["sgn"][("g", 0, 12 + jj)]
                tgt = w0_load(buf.rearrange("p (a d) -> p a d", d=D),
                              w0_src(kind, i), grp, s_gate=sg_,
                              v_gate=w0c["tt"].get(pv, 0))
                c0 = 0 if kind == "fc" else 2
                gv = "gamma_v%d" % (0 if kind == "fc" else 1)
                w0c["s"] += 2
                w0c["sgn"][("t", jj)] = w0c["s"]
                vgate_sgnA = w0c["tt"].get(jj - 1, 0)

                def ops_act(h, tgt=tgt, j=j, c0=c0, grp=grp, vg=vgate_sgnA,
                            gv=gv, bi=bi):
                    h.wait_ge(ld_groups[grp][0], tgt)
                    h.wait_ge(s_w0v, max(vg, w0c[gv]))
                    h.activation(sgnA2, wtile2b[bi], AF.Sign,
                                 bias=gam[:, c0 + 1:c0 + 2],
                                 scale=1.0).then_inc(s_w0s, 1)
                    h.activation(wtile2b[bi], wtile2b[bi], AF.Sign,
                                 bias=gam[:, c0:c0 + 1],
                                 scale=1.0).then_inc(s_w0s, 1)
                sch.ops["scalar"].append(ops_act)
                if w0c.get("pend_st") is not None:
                    sch.ops["scalar"].append(w0c.pop("pend_st"))
                w0c["v"] += 1
                w0c["tt"][jj] = w0c["v"]
                smark = w0c["s"]
                tpgate = w0c["tp"].get(jj - 1, 0)

                def opv(h, j=j, smark=smark, tg=tpgate, bi=bi):
                    h.wait_ge(s_w0s, smark)
                    if tg > 0:
                        h.wait_ge(s_w0t, tg)
                    h.tensor_tensor(wqn[:, 0:2048], sgnA2, wtile2b[bi],
                                    op=ALU.add).then_inc(s_w0v, 1)
                sch.ops["vector"].append(opv)
                vmark = w0c["v"]
                w0c["t"] += 1
                evgate = w0c["ev"][0]

                def opt(h, vmark=vmark, evgate=evgate):
                    h.wait_ge(s_w0v, vmark)
                    if evgate > 0:
                        h.wait_ge(s_w0s, evgate)
                    for a in range(2):
                        for c in range(DCH):
                            ins = h.transpose(
                                ps_w45[a][:, c * P:(c + 1) * P],
                                wqn[:, a * 1024 + c * P:
                                    a * 1024 + (c + 1) * P], identb)
                    ins.then_inc(s_w0t, 1)
                sch.ops["tensor"].append(opt)
                tmark = w0c["t"]
                stg = w0c.get("last_fc_st", 0) if kind == "fc" else 0
                for a in range(2):
                    if kind == "fc":
                        edst = wfstg2[:, a * 1024:(a + 1) * 1024]
                        esrc = ps_w45[a]
                    else:
                        qq = (2 * i + a) % 4
                        hc0 = qq * 8
                        rb = (2 * i + a) // 4
                        edst = wpT_all.rearrange("p (h r) -> p h r", r=D)[
                            :, hc0:hc0 + 8, rb * P: rb * P + P]
                        esrc = ps_w45[a].rearrange("p (x b) -> p x b", b=P)
                    w0c["s"] += 1

                    def ope(h, d=edst, sr=esrc, tmark=tmark, stg=stg,
                            a=a):
                        h.wait_ge(s_w0t, tmark)
                        if a == 0 and stg > 0:
                            h.wait_ge(s_st, stg)
                        h.tensor_copy(d, sr).then_inc(s_w0s, 1)
                    sch.ops["vector"].append(ope)
                w0c["ev"][0] = w0c["s"]
                w0c["tp"][jj] = w0c["t"]
                if kind == "fc":
                    w0c["fcev"] = w0c.get("fcev", 0) + 1
                    fcev = w0c["fcev"]
                    sch.ops["vector"].append(
                        lambda h: h.nop().then_inc(s_w0e, 1))
                    st_cnt[0] += 1

                    def opst(h, i=i, fcev=fcev):
                        h.wait_ge(s_w0e, fcev)
                        h.dma_start(out=wfct_scr[i, :, :],
                                    in_=wfstg2).then_inc(s_st, 16)
                    w0c["pend_st"] = opst
                    w0c["last_fc_st"] = st_cnt[0] * 16
                    w0_state["last_st"] = st_cnt[0] * 16
            return go

        for j in range(16):
            w0_steps.append(gamma_step(j, "fc", j, 0))
        w0_steps.append(gamma_final(0, lambda: w0c["sgn"][("g", 0, 15)]))
        for j in range(16):
            w0_steps.append(gamma_step(16 + j, "pj", j, 2))
            w0_steps.append(tern_step(j, "fc", j))
        w0_steps.append(gamma_final(1, lambda: w0c["sgn"][("g", 2, 31)]))
        for j in range(16):
            w0_steps.append(tern_step(16 + j, "pj", j))

        def w0_post():
            if w0c.get("pend_st") is not None:
                sch.ops["scalar"].append(w0c.pop("pend_st"))
            sv, ss, st_ = w0c["v"], w0c["s"], w0c["t"]
            for e, sem, c in (("tensor", s_w0s, ss), ("vector", s_w0s, ss),
                              ("scalar", s_w0v, sv)):
                sch.ops[e].append(lambda h, sem=sem, c=c: h.wait_ge(sem, c))
            for w in range(2):
                for hc in range(HCH):
                    E("tensor", lambda t, w=w, hc=hc: t.matmul(
                        ps_c[:, w * 512:(w + 1) * 512], lnb_b[:, hc:hc + 1],
                        wpT[hc][:, w * 512:(w + 1) * 512],
                        start=(hc == 0), stop=(hc == HCH - 1)),
                        deps=(["vector", "scalar"] if hc == 0 else None))
            E("vector", lambda v: v.tensor_scalar(
                cvec, ps_c, 0.5, None, op0=ALU.mult), deps=["tensor"])
            for hc in range(HCH):
                E("vector", lambda v, hc=hc: v.tensor_scalar(
                    wpT[hc], wpT[hc], lnw_t[:, hc:hc + 1], None, op0=ALU.mult))

        def drip(n=4):
            for _ in range(n):
                if w0_steps:
                    w0_steps.pop(0)()

        def newton_rs(v_ap, rs_ap, seed, iters, tmp_base=56, rows=P):
            tmp = stat[0:rows, tmp_base:tmp_base + (v_ap.shape[1])]
            E("vector", lambda v: v.memset(rs_ap, seed))
            for _ in range(iters):
                E("vector", lambda v: v.drain())
                E("vector", lambda v: v.tensor_tensor(tmp, rs_ap, rs_ap,
                                                      op=ALU.mult))
                E("vector", lambda v: v.drain())
                E("vector", lambda v: v.tensor_tensor(tmp, tmp, v_ap,
                                                      op=ALU.mult))
                E("vector", lambda v: v.drain())
                E("vector", lambda v: v.tensor_scalar(
                    tmp, tmp, -0.5, 1.5, op0=ALU.mult, op1=ALU.add))
                E("vector", lambda v: v.drain())
                E("vector", lambda v: v.tensor_tensor(rs_ap, rs_ap, tmp,
                                                      op=ALU.mult))
            E("vector", lambda v: v.drain())

        def ln_collect(i, x_ap, deps=None):
            E("vector", lambda v, i=i, x_ap=x_ap: v.reduce_sum(
                stat[:, 16 + i:17 + i], x_ap, axis=AX.X), deps=deps)
            E("vector", lambda v: v.drain())
            E("scalar", lambda s, i=i, x_ap=x_ap: s.activation(
                lnsq, x_ap[:, 0:512], AF.Square,
                accum_out=stat[:, 24 + i:25 + i]), deps=["vector"])
            E("scalar", lambda s, i=i, x_ap=x_ap: s.activation(
                lnsq, x_ap[:, 512:1024], AF.Square,
                accum_out=stat[:, 32 + i:33 + i]))

        def ln_finish(mode, seed=0.6, iters=8, split_sums=False):
            n = NTIL
            mu = stat[:, 0:n]
            rs = stat[:, 8:8 + n]
            var = stat[:, 48:48 + n]
            scr = stat[:, 16:16 + n]
            E("vector", lambda v: v.drain())
            if split_sums:
                E("vector", lambda v: v.tensor_tensor(
                    stat[:, 16:16 + n], stat[:, 16:16 + n],
                    stat[:, 56:56 + n], op=ALU.add), deps=["scalar"])
                E("vector", lambda v: v.drain())
            E("vector", lambda v: v.tensor_scalar(
                mu, stat[:, 16:16 + n], 1.0 / D, None, op0=ALU.mult),
                deps=(None if split_sums else ["scalar"]))
            E("vector", lambda v: v.tensor_tensor(
                var, stat[:, 24:24 + n], stat[:, 32:32 + n], op=ALU.add))
            E("vector", lambda v: v.drain())
            E("vector", lambda v: v.tensor_scalar(
                var, var, 1.0 / D, None, op0=ALU.mult))
            E("vector", lambda v: v.tensor_tensor(scr, mu, mu, op=ALU.mult))
            E("vector", lambda v: v.drain())
            E("vector", lambda v: v.tensor_tensor(var, var, scr,
                                                  op=ALU.subtract))
            E("vector", lambda v: v.drain())
            if mode == "A":
                E("scalar", lambda s: s.activation(
                    scr, var, AF.Ln, bias=epsv, scale=1.0), deps=["vector"])
                E("scalar", lambda s: s.activation(
                    rs, scr, AF.Exp, scale=-0.5))
            else:
                E("vector", lambda v: v.tensor_scalar(
                    var, var, 1.0, LN_EPS, op0=ALU.mult, op1=ALU.add))
                newton_rs(var, rs, seed, iters, tmp_base=24)

        def xlnT_quad(c0, tok0):
            a0 = c0 * 1040 + 4 + tok0
            return (arena[:, a0: a0 + 4 * 1040]
                    .rearrange("p (c r) -> p c r", r=1040)[:, :, 0:P])

        if w0_trunc is not None:
            del w0_steps[w0_trunc:]
        if w0_serial:
            drip(len(w0_steps))
            if w0_trunc is None:
                w0_post()
        def keepwarm(sg):
            pass

        def halo_ln(mode, gate_t=None):
            E("vector", lambda v: v.reduce_sum(
                stat[0:8, 44:45], xh, axis=AX.X))
            E("scalar", lambda s: s.activation(
                lnsq[0:8, :], xh[0:8, 0:512], AF.Square,
                accum_out=stat[0:8, 45:46]), deps=["vector"])
            E("scalar", lambda s: s.activation(
                lnsq[0:8, :], xh[0:8, 512:1024], AF.Square,
                accum_out=stat[0:8, 46:47]))
            E("vector", lambda v: v.tensor_tensor(
                stat[0:8, 45:46], stat[0:8, 45:46], stat[0:8, 46:47],
                op=ALU.add), deps=["scalar"])
            E("vector", lambda v: v.drain())
            E("vector", lambda v: v.tensor_scalar(
                stat[0:8, 44:45], stat[0:8, 44:45], 1.0 / D, None,
                op0=ALU.mult))
            E("vector", lambda v: v.tensor_scalar(
                stat[0:8, 45:46], stat[0:8, 45:46], 1.0 / D, None,
                op0=ALU.mult))
            E("vector", lambda v: v.drain())
            E("vector", lambda v: v.tensor_tensor(
                stat[0:8, 46:47], stat[0:8, 44:45], stat[0:8, 44:45],
                op=ALU.mult))
            E("vector", lambda v: v.drain())
            E("vector", lambda v: v.tensor_tensor(
                stat[0:8, 45:46], stat[0:8, 45:46], stat[0:8, 46:47],
                op=ALU.subtract))
            E("vector", lambda v: v.drain())
            if mode == "act":
                E("scalar", lambda s: s.activation(
                    stat[0:8, 47:48], stat[0:8, 45:46], AF.Ln,
                    bias=epsv[0:8, :], scale=1.0), deps=["vector"])
                E("scalar", lambda s: s.activation(
                    stat[0:8, 45:46], stat[0:8, 47:48], AF.Exp, scale=-0.5))
                rs_c = 45
            else:
                E("vector", lambda v: v.tensor_scalar(
                    stat[0:8, 45:46], stat[0:8, 45:46], 1.0, LN_EPS,
                    op0=ALU.mult, op1=ALU.add))
                E("vector", lambda v: v.drain())
                newton_rs(stat[0:8, 45:46], stat[0:8, 47:48], 0.6, 6,
                          tmp_base=24, rows=8)
                rs_c = 47
            E("vector", lambda v, rs_c=rs_c: v.tensor_scalar(
                xln_hm, xh, stat[0:8, 44:45], stat[0:8, rs_c:rs_c + 1],
                op0=ALU.subtract, op1=ALU.mult),
                deps=(["scalar"] if mode == "act" else None),
                at=({"tensor": gate_t} if gate_t else None))

        def a_prep(gate_t=None, dodrip=False):
            ee = "scalar" if gate_t else "vector"
            tprev = gate_t
            for i in range(NTIL):
                E("vector", lambda v, i=i: v.tensor_scalar(
                    xln_tm, xt[i], stat[:, i:i + 1], stat[:, 8 + i:9 + i],
                    op0=ALU.subtract, op1=ALU.mult),
                    deps=(["scalar"] if i == 0 else None),
                    at=({"tensor": tprev} if tprev else None))
                for c4 in range(2):
                    pst = ps_tAb if (c4 == 0) else ps_tBb
                    for cj in range(4):
                        c = c4 * 4 + cj
                        E("tensor", lambda t, c=c, cj=cj, p=pst: t.transpose(
                            p[:, cj * P:(cj + 1) * P],
                            xln_tm[:, c * P:(c + 1) * P], identb),
                            deps=((["vector"] if ee == "vector"
                                   else ["vector", "scalar"])
                                  if cj == 0 else None))
                    if ee == "vector":
                        E("vector", lambda v, c4=c4, i=i, p=pst:
                          v.tensor_copy(xlnT_quad(c4 * 4, i * P), p[:]),
                          deps=["tensor"])
                    else:
                        E("scalar", lambda sc, c4=c4, i=i, p=pst:
                          sc.copy(xlnT_quad(c4 * 4, i * P), p[:]),
                          deps=["tensor"])
                if ee == "scalar":
                    tprev = sch.cnt["tensor"]
                if dodrip:
                    drip(3)
            for c in range(DCH):
                pst = ps_tAb if (c % 2 == 0) else ps_tBb
                E("tensor", lambda t, c=c, p=pst: t.transpose(
                    p[:, 0:8], xln_hm[:, c * P:(c + 1) * P],
                    identb[0:8, 0:8]),
                    deps=(["vector"] if ee == "vector"
                          else ["vector", "scalar"]))
                if ee == "vector":
                    E("vector", lambda v, c=c, p=pst: v.tensor_copy(
                        xlnT[c][:, 0:4], p[:, 0:4]), deps=["tensor"])
                    E("vector", lambda v, c=c, p=pst: v.tensor_copy(
                        xlnT[c][:, SG + 4:SG + 8], p[:, 4:8]))
                else:
                    E("scalar", lambda sc, c=c, p=pst: sc.copy(
                        xlnT[c][:, 0:4], p[:, 0:4]), deps=["tensor"])
                    E("scalar", lambda sc, c=c, p=pst: sc.copy(
                        xlnT[c][:, SG + 4:SG + 8], p[:, 4:8]))
            E("vector", lambda v: v.memset(cklF, 0.0),
              at=({"tensor": gate_t} if gate_t else None))
            if dodrip:
                drip(3)

        ov = {}

        def ov_tile(j):
            g = ld_groups[WGRP[j]]
            g[1] += 1
            tgt_o = g[1] * 16
            vc = sch.cnt["vector"]

            def opst(h, j=j, g=g, vc=vc):
                if vc > 0:
                    h.wait_ge(sems["vector"], vc)
                h.dma_start(out=out[j * P:(j + 1) * P, :],
                            in_=xt[j]).then_inc(g[0], 16)
            sch.ops["sync"].append(opst)
            g[1] += 1
            tgt = g[1] * 16

            def opld(h, tgt_o=tgt_o, j=j, g=g):
                h.wait_ge(g[0], tgt_o)
                h.dma_start(out=xt[j],
                            in_=xin[SG + 4 + j * P: SG + 4 + (j + 1) * P, :]
                            ).then_inc(g[0], 16)
            sch.ops["sync"].append(opld)
            ov[("xt", j)] = tgt

        def ov_collect(j):
            wait_ld("vector", ov[("xt", j)], group=WGRP[j])
            ln_collect(j, xt[j])

        for sg in range(NSG):
            base = sg * SG
            if sg == 1:
                sch.barrier()
            if sg == 0:
                tloads = []
                for i in range(NTIL):
                    tloads.append(load(xt[i], xin[base + 4 + i * P:
                                                  base + 4 + (i + 1) * P, :],
                                       deps=["vector", "tensor", "scalar"]))
                th_l = load(xh[0:4, :], xin[base + 0: base + 4, :],
                            group="small")
                th_r = load(xh[4:8, :], xin[base + SG + 4: base + SG + 8, :],
                            group="small")

                wait_ld("vector", tloads[-1])
                for i in range(NTIL):
                    ln_collect(i, xt[i])
                    drip(3)
                    keepwarm(sg)
                ln_finish("A")
                drip(3)
                if debug_taps:
                    tls = dbg("ln1stat", [P, 64], F32)
                    dbg_store(tls[:, :], stat)
                wait_ld2("vector", th_l)
                wait_ld2("vector", th_r)
                halo_ln("act")
                a_prep(dodrip=True)
                sch.barrier()

            if debug_taps and sg == 0:
                tpx = dbg("xlnT0", [P, SG + 8], BF16)
                store(tpx[:, :], xlnT[0], deps=["vector"])

            for half in range(2):
                for g in range(3):
                    ps = ps_ckl[g]
                    ncc = 3 if g < 2 else 2
                    for cc in range(ncc):
                        c = 3 * g + cc
                        off = 32 * cc
                        for k in range(K9):
                            E("tensor", lambda t, c=c, k=k, off=off, ps=ps,
                              half=half: t.matmul(
                                  ps[off:off + 18, :],
                                  cwb[:, (c * K9 + k) * 18:
                                      (c * K9 + k) * 18 + 18],
                                  xlnT[c][:, half * 512 + k:
                                          half * 512 + k + 512],
                                  start=(k == 0), stop=(k == K9 - 1)),
                              deps=(["vector", "scalar"] if (cc == 0 and k == 0)
                                    else None))
                    for cc in range(ncc):
                        off = 32 * cc
                        E("scalar", lambda sc, ps=ps, off=off, g=g, half=half:
                          sc.copy(
                              cklF[off:off + 18,
                                   g * SG + half * 512: g * SG + half * 512 + 512],
                              ps[off:off + 18, :]),
                          deps=(["tensor"] if cc == 0 else None))
                    drip(3)
            sch.barrier()

            if debug_taps and sg == 0:
                tpk = dbg("cklF", [P, 3 * SG], F32)
                store(tpk[:, :], cklF, deps=["vector"])

            sm_ev = {}
            for i in range(NTIL):
                for g in range(3):
                    slot = (i * 3 + g) % 4
                    pst = ps_zt[slot][:, 0:256].bitcast(F32)
                    at_t = ({"vector": sm_ev[slot]} if slot in sm_ev
                            else None)
                    E("tensor", lambda t, i=i, g=g, pst=pst: t.transpose(
                        pst, cklF[:, g * SG + i * P: g * SG + (i + 1) * P],
                        identf), deps=(["vector"] if (i == 0 and g == 0)
                                       else None), at=at_t)
                    E("vector", lambda v, i=i, g=g, pst=pst: v.tensor_copy(
                        ckl_tm[i][:, g * P:(g + 1) * P], pst),
                        deps=["tensor"])
                    sm_ev[slot] = sch.cnt["vector"]
                    keepwarm(sg)
                x12 = (ckl_tm[i]
                       .rearrange("p (m r) -> p m r", r=32)[:, :, 0:18]
                       .rearrange("p m (hh j) -> p m hh j", j=K9))
                gm24 = stat[:, 16:40].rearrange("p (m hh) -> p m hh", hh=2)
                gs24 = stat[:, 40:64].rearrange("p (m hh) -> p m hh", hh=2)
                E("vector", lambda v, x12=x12, gm24=gm24: v.tensor_reduce(
                    gm24, x12, axis=AX.X, op=ALU.max))
                E("vector", lambda v: v.drain())
                E("vector", lambda v, x12=x12, gm24=gm24: v.tensor_tensor(
                    x12, x12,
                    gm24.unsqueeze(3).broadcast_to([P, 12, 2, K9]),
                    op=ALU.subtract))
                E("scalar", lambda s, i=i: s.activation(
                    ckl_tm[i], ckl_tm[i], AF.Exp), deps=["vector"])
                E("vector", lambda v, x12=x12, gs24=gs24: v.tensor_reduce(
                    gs24, x12, axis=AX.X, op=ALU.add), deps=["scalar"])
                E("vector", lambda v: v.drain())
                E("vector", lambda v: v.reciprocal(
                    stat[:, 16:40], stat[:, 40:64]))
                E("vector", lambda v: v.drain())
                for g in range(3):
                    ncc = 3 if g < 2 else 2
                    x4 = (ckl_tm[i][:, g * P: g * P + 32 * ncc]
                          .rearrange("p (cc r) -> p cc r", r=32)[:, :, 0:18]
                          .rearrange("p cc (hh j) -> p cc hh j", j=K9))
                    grv = (stat[:, 16 + 8 * g: 16 + 8 * g + 2 * ncc]
                           .rearrange("p (cc hh) -> p cc hh", hh=2))
                    wout = (wsm_tm[:, 54 * g: 54 * g + 18 * ncc]
                            .rearrange("p (cc hh j) -> p cc hh j", hh=2, j=K9))
                    E("vector", lambda v, x4=x4, grv=grv, wout=wout, ncc=ncc:
                      v.tensor_tensor(
                          wout, x4,
                          grv.unsqueeze(3).broadcast_to([P, ncc, 2, K9]),
                          op=ALU.mult))
                wsm3 = wsm_tm.rearrange("p (h k) -> p h k", k=K9)
                for k in range(K9):
                    pst = ps_tAw if k < 8 else ps_tBb
                    off = (k % 8) * P
                    E("tensor", lambda t, k=k, wsm3=wsm3, p=pst, off=off:
                      t.transpose(p[0:16, off:off + P], wsm3[:, :, k], identb),
                      deps=(["vector"] if k in (0, 8) else None))
                wk8 = (arena[0:16, 22800: 22800 + 8 * SG]
                       .rearrange("p (k r) -> p k r", r=SG)
                       [:, :, i * P:(i + 1) * P])
                E("vector", lambda v, i=i, wk8=wk8: v.tensor_copy(
                    wk8, ps_tAw[0:16, :].rearrange("p (k r) -> p k r", r=P)),
                    deps=["tensor"])
                E("vector", lambda v, i=i: v.tensor_copy(
                    wsmK[8][:, i * P:(i + 1) * P], ps_tBb[0:16, 0:P]))
                drip(4)
            sch.barrier()

            if debug_taps and sg == 0:
                tpw = dbg("wsmK0", [16, SG], BF16)
                store(tpw[:, :], wsmK[0], deps=["vector"])

            wait_ld2("tensor", t_selp)
            ps_ap = [ps_big[:, j * 512:(j + 1) * 512] for j in range(4)]
            ap_mm = {}
            ap_cp = {}
            ap_mul = {}
            cq = []
            kidx = 0
            for pair in range(NPAIR):
                sp = selp[:, pair * P:(pair + 1) * P]
                for w in range(2):
                    for k in range(K9):
                        slot = kidx % 4
                        wslot = kidx % 4
                        at_mm = {}
                        if kidx >= 4:
                            at_mm["scalar"] = ap_cp[kidx - 4]
                        E("tensor", lambda t, k=k, w=w, sp=sp, slot=slot:
                          t.matmul(ps_ap[slot], sp,
                                   wsmK[k][:, w * 512:(w + 1) * 512],
                                   start=True, stop=True),
                          at=(at_mm or None),
                          deps=(["vector"] if kidx == 0 else None))
                        ap_mm[kidx] = sch.cnt["tensor"]
                        at_cp = {"tensor": ap_mm[kidx]}
                        if kidx >= 4:
                            at_cp["vector"] = ap_mul[kidx - 4]
                        E("scalar", lambda s, slot=slot, wslot=wslot: s.copy(
                            wsb[wslot], ps_ap[slot]), at=at_cp)
                        ap_cp[kidx] = sch.cnt["scalar"]
                        if cq and pair >= 4:
                            fn, at_q = cq.pop(0)
                            E("scalar", fn, at=at_q)
                        xsl = xlnT[pair][:, w * 512 + k: w * 512 + k + 512]
                        if k < 4:
                            E("vector", lambda v, xsl=xsl, k=k, wslot=wslot:
                              v.tensor_tensor(prods[k], xsl, wsb[wslot],
                                              op=ALU.mult),
                              at={"scalar": ap_cp[kidx]})
                        elif k < 8:
                            E("vector", lambda v, xsl=xsl, k=k, wslot=wslot:
                              v.tensor_tensor(prods[4], xsl, wsb[wslot],
                                              op=ALU.mult),
                              at={"scalar": ap_cp[kidx]})
                            E("vector", lambda v, k=k: v.tensor_tensor(
                                prods[k - 4], prods[k - 4], prods[4],
                                op=ALU.add))
                        else:
                            E("vector", lambda v, xsl=xsl, k=k, wslot=wslot:
                              v.tensor_tensor(prods[4], xsl, wsb[wslot],
                                              op=ALU.mult),
                              at={"scalar": ap_cp[kidx]})
                        ap_mul[kidx] = sch.cnt["vector"]
                        kidx += 1
                    E("vector", lambda v: v.tensor_tensor(
                        prods[0], prods[0], prods[1], op=ALU.add))
                    E("vector", lambda v: v.tensor_tensor(
                        prods[2], prods[2], prods[3], op=ALU.add))
                    E("vector", lambda v: v.tensor_tensor(
                        prods[0], prods[0], prods[2], op=ALU.add))
                    E("vector", lambda v, w=w: v.tensor_tensor(
                        apair[:, w * 512:(w + 1) * 512], prods[0], prods[4],
                        op=ALU.add))
                    drip(3)
                for i4 in range(2):
                    pst = ps_tAb if (i4 == 0) else ps_tBb
                    for j in range(4):
                        i = i4 * 4 + j
                        E("tensor", lambda t, i=i, j=j, p=pst: t.transpose(
                            p[:, j * P:(j + 1) * P],
                            apair[:, i * P:(i + 1) * P], identb),
                            deps=(["vector"] if j == 0 else None))
                    dstv = xall.rearrange("p (t r) -> p t r", r=D)[
                        :, i4 * 4:(i4 + 1) * 4, pair * P:(pair + 1) * P]
                    E("vector", lambda v, dstv=dstv, p=pst: v.tensor_tensor(
                        dstv, dstv, p[:].rearrange("p (t r) -> p t r", r=P),
                        op=ALU.add), deps=["tensor"])
                    if (pair == 5 and i4 == 1 and sg == 0
                            and not w0_serial and not skip_ffn
                            and not ov.get("w0done")):
                        drip(len(w0_steps))
                        w0_post()
                        ov["w0done"] = True
                    if pair == 3:
                        vg3 = sch.cnt["vector"]
                        cdst = x2lnT2.bitcast(F32)[:, 0:512]
                        for j in range(4):
                            i = i4 * 4 + j
                            at3 = {"vector": vg3} if (j == 0) else None
                            cq.append((lambda sc, i=i, cdst=cdst:
                                       sc.activation(
                                           cdst, xt[i][:, 0:512], AF.Copy,
                                           accum_out=stat[:, 16 + i:17 + i]),
                                       at3))
                            cq.append((lambda sc, i=i: sc.activation(
                                lnsq, xt[i][:, 0:512], AF.Square,
                                accum_out=stat[:, 24 + i:25 + i]), None))
                    if pair == NPAIR - 1:
                        vg7 = sch.cnt["vector"]
                        for j in range(4):
                            i = i4 * 4 + j
                            E("vector", lambda v, i=i: v.reduce_sum(
                                stat[:, 56 + i:57 + i], xt[i][:, 512:1024],
                                axis=AX.X))
                            E("scalar", lambda sc, i=i: sc.activation(
                                lnsq, xt[i][:, 512:1024], AF.Square,
                                accum_out=stat[:, 32 + i:33 + i]),
                                at=({"vector": vg7} if j == 0 else None))
                        E("vector", lambda v: v.drain())
            sch.barrier()

            if debug_taps and sg == 0:
                tpa = dbg("x1", [SG, D], F32)
                for i in range(NTIL):
                    store(tpa[i * P:(i + 1) * P, :], xt[i], deps=["vector"])

            if sg == 0:
                if not w0_serial and not ov.get("w0done"):
                    drip(len(w0_steps))
                    w0_post()
                    ov["w0done"] = True
                tgt_w = w0_state["last_st"]
                sch.ops["sync"].append(lambda h, t=tgt_w: h.wait_ge(s_st, t))
            if skip_ffn:
                for i in range(NTIL):
                    store(out[base + i * P: base + (i + 1) * P, :], xt[i],
                          deps=["vector"])
                continue
            sg_pages = []
            for q4 in range(4):
                grpq = WGRP[q4]
                for pg in range(4 * q4, 4 * q4 + 4):
                    tgtq = load(arena[:, pg * 2048:(pg + 1) * 2048],
                                wfct_scr[pg, :, :],
                                deps=(["tensor", "vector", "scalar"]
                                      if pg == 4 * q4 else None), group=grpq)
                sg_pages.append((grpq, tgtq))
            tw = None

            ln_finish("B", seed=0.6, iters=5, split_sums=True)
            if debug_taps and sg == 0:
                tls2 = dbg("ln2stat", [P, 64], F32)
                dbg_store(tls2[:, :], stat)

            prev = {}
            for i in range(NTIL):
                tm = x2ln_tm_b[i % 2]
                tT = x2lnT_b[i % 2]
                E("vector", lambda v, i=i, tm=tm: v.tensor_scalar(
                    tm, xt[i], stat[:, i:i + 1], stat[:, 8 + i:9 + i],
                    op0=ALU.subtract, op1=ALU.mult))
                va = sch.cnt["vector"]
                ev1 = []
                for c4 in range(2):
                    pst = ps_tAb if (c4 == 0) else ps_tBb
                    dep_at = {"vector": va if "t1ev" not in prev
                              else max(va, prev["t1ev"][c4])}
                    for cj in range(4):
                        c = c4 * 4 + cj
                        E("tensor", lambda t, c=c, cj=cj, p=pst, tm=tm:
                          t.transpose(p[:, cj * P:(cj + 1) * P],
                                      tm[:, c * P:(c + 1) * P], identb),
                          at=(dep_at if cj == 0 else None))
                    E("vector", lambda v, c4=c4, p=pst, tT=tT: v.tensor_copy(
                        tT[:, c4 * 512:(c4 + 1) * 512], p[:]),
                        deps=["tensor"])
                    ev1.append(sch.cnt["vector"])
                if i > 0:
                    E("vector", lambda v, j=i - 1: v.scalar_tensor_tensor(
                        xt[j], ps_proj, stat[:, 42:43], xt[j],
                        op0=ALU.mult, op1=ALU.add),
                        at={"tensor": prev["proj"], "scalar": prev["sq"]})
                    prev["evac"] = sch.cnt["vector"]
                    if sg == 0 and not skip_ffn:
                        ov_tile(i - 1)
                        if i >= 2:
                            ov_collect(i - 2)

                sqq_hist = prev.get("sqq_hist", [0, 0])
                for quart in range(4):
                    half, wq = quart // 2, (quart % 2) * 2
                    for c in range(DCH):
                        for w in (wq, wq + 1):
                            at_ = None
                            if c == 0 and w == wq:
                                at_ = {"vector": ev1[1],
                                       "scalar": sqq_hist[-2]}
                                if i == 0:
                                    grpq, tgtq = sg_pages[quart]
                                    wait_ld("tensor", tgtq, group=grpq)
                            j0 = half * 8 + 2 * w
                            rhsv = (arena.rearrange("p (j r) -> p j r", r=2048)
                                    [:, j0:j0 + 2, :]
                                    .rearrange("p j (a u) -> p j a u", u=1024)
                                    [:, :, :, c * P:(c + 1) * P])
                            E("tensor", lambda t, c=c, w=w, half=half, tT=tT,
                              rhsv=rhsv: t.matmul(
                                ps_fc[:, w * 512:(w + 1) * 512],
                                tT[:, c * P:(c + 1) * P], rhsv,
                                start=(c == 0), stop=(c == DCH - 1)),
                                at=at_)
                    gsl = gbuf[:, (quart % 2) * 1024:(quart % 2 + 1) * 1024]
                    at_g = {"tensor": sch.cnt["tensor"]}
                    if quart == 2:
                        at_g["vector"] = prev.get("gcp", 0)
                    E("scalar", lambda s, quart=quart, gsl=gsl, at_g=at_g:
                      s.activation(
                        gsl, ps_fc[:, (quart % 2) * 1024:
                                   (quart % 2 + 1) * 1024],
                        AF.Gelu, scale=0.5,
                        accum_out=accg[:, quart:quart + 1]), at=at_g)
                    if quart == 3:
                        prev["gq3"] = sch.cnt["scalar"]
                    E("scalar", lambda s, quart=quart, gsl=gsl: s.activation(
                        zscr[:, 0:2048][:, (quart % 2) * 1024:
                                        (quart % 2 + 1) * 1024], gsl,
                        AF.Square, accum_out=accg[:, 4 + quart:5 + quart]))
                    sqq_hist.append(sch.cnt["scalar"])
                    if quart == 1:
                        E("vector", lambda v: v.tensor_copy(
                            zscr[:, 2048:4096], gbuf), deps=["scalar"])
                        prev["gcp"] = sch.cnt["vector"]
                prev["sq"] = sch.cnt["scalar"]
                prev["sqq_hist"] = sqq_hist[-2:]
                if sg == 0 and i == NTIL - 1:
                    ov["fc_last"] = sch.cnt["tensor"]
                E("vector", lambda v: v.tensor_reduce(
                    stat[:, 40:41], accg[:, 0:4], axis=AX.X, op=ALU.add),
                    at={"scalar": prev["gq3"]})
                E("vector", lambda v: v.drain())
                E("vector", lambda v: v.tensor_scalar(
                    stat[:, 40:41], stat[:, 40:41], 1.0 / H, None,
                    op0=ALU.mult))
                E("vector", lambda v: v.drain())
                vzq = []
                for zq in range(4):
                    if zq < 2:
                        src_z = zscr[:, 2048 + zq * 1024: 3072 + zq * 1024]
                    else:
                        src_z = gbuf[:, (zq - 2) * 1024:(zq - 1) * 1024]
                    E("vector", lambda v, zq=zq, src_z=src_z: v.tensor_scalar(
                        zscr[:, zq * 1024:(zq + 1) * 1024], src_z,
                        stat[:, 40:41], 0.5,
                        op0=ALU.subtract, op1=ALU.mult))
                    vzq.append(sch.cnt["vector"])
                vz0, vz = vzq[1], vzq[3]
                E("vector", lambda v: v.tensor_reduce(
                    stat[:, 41:42], accg[:, 4:8], axis=AX.X, op=ALU.add),
                    at={"scalar": prev["sq"]})
                E("vector", lambda v: v.drain())
                E("vector", lambda v: v.tensor_scalar(
                    stat[:, 41:42], stat[:, 41:42], 1.0 / H, None,
                    op0=ALU.mult))
                E("vector", lambda v: v.tensor_tensor(
                    stat[:, 42:43], stat[:, 40:41], stat[:, 40:41],
                    op=ALU.mult))
                E("vector", lambda v: v.drain())
                E("vector", lambda v: v.tensor_tensor(
                    stat[:, 41:42], stat[:, 41:42], stat[:, 42:43],
                    op=ALU.subtract))
                E("vector", lambda v: v.drain())
                E("vector", lambda v: v.tensor_scalar(
                    stat[:, 41:42], stat[:, 41:42], 1.0, LN_EPS,
                    op0=ALU.mult, op1=ALU.add))
                E("vector", lambda v: v.drain())
                evac_cnt = []
                for b4 in range(8):
                    pst = ps_zt[b4 % 4][:, 0:512]
                    need_z = vzq[b4 // 2]
                    dep_at = {"vector": (need_z if b4 < 4
                                         else max(need_z, evac_cnt[b4 - 4])),
                              "scalar": prev["sq"]}
                    for j in range(4):
                        hc = b4 * 4 + j
                        E("tensor", lambda t, hc=hc, j=j, p=pst: t.transpose(
                            p[:, j * P:(j + 1) * P],
                            zscr[:, hc * P:(hc + 1) * P], identb),
                            at=(dep_at if j == 0 else None))
                    E("vector", lambda v, b4=b4, p=pst: v.tensor_copy(
                        zT[:, b4 * 512:(b4 + 1) * 512], p[:]), deps=["tensor"])
                    evac_cnt.append(sch.cnt["vector"])
                prev["t1ev"] = [0, ev1[1]]
                newton_rs(stat[:, 41:42], stat[:, 42:43], 0.064, 4)
                E("vector", lambda v: v.tensor_tensor(
                    scol, stat[:, 41:42], stat[:, 42:43], op=ALU.mult))
                at_p = {"vector": evac_cnt[-1]}
                if "evac" in prev:
                    at_p["vector"] = max(at_p["vector"], prev["evac"])
                for hc in range(HCH):
                    for w in range(2):
                        E("tensor", lambda t, hc=hc, w=w: t.matmul(
                            ps_proj[:, w * 512:(w + 1) * 512],
                            zT[:, hc * P:(hc + 1) * P],
                            wpT[hc][:, w * 512:(w + 1) * 512],
                            start=(hc == 0), stop=False),
                            at=(at_p if (hc == 0 and w == 0) else None))
                E("tensor", lambda t: t.transpose(
                    ps_tAb[0:1, 0:P], scol, identb), deps=["vector"])
                E("vector", lambda v: v.tensor_copy(
                    srow, ps_tAb[0:1, 0:P]), deps=["tensor"])
                prev["t1ev"][0] = sch.cnt["vector"]
                for w in range(2):
                    E("tensor", lambda t, w=w: t.matmul(
                        ps_proj[:, w * 512:(w + 1) * 512], srow,
                        cvec[:, w * 512:(w + 1) * 512], start=False,
                        stop=(w == 1)), deps=(["vector"] if w == 0 else None))
                prev["proj"] = sch.cnt["tensor"]
            E("vector", lambda v: v.scalar_tensor_tensor(
                xt[NTIL - 1], ps_proj, stat[:, 42:43], xt[NTIL - 1],
                op0=ALU.mult, op1=ALU.add),
                at={"tensor": prev["proj"]})
            if sg == 0:
                ov_tile(NTIL - 1)
                ov_collect(NTIL - 2)
                ov_collect(NTIL - 1)
                th_l = load(xh[0:4, :], xin[SG + 0: SG + 4, :],
                            group="small")
                th_r = load(xh[4:8, :], xin[2 * SG + 4: 2 * SG + 8, :],
                            group="small")
                ln_finish("B", seed=0.6, iters=5)
                wait_ld2("vector", th_l)
                wait_ld2("vector", th_r)
                halo_ln("newton", gate_t=ov["fc_last"])
                a_prep(gate_t=ov["fc_last"])
            else:
                sch.barrier()
                for i in range(NTIL):
                    store(out[base + i * P: base + (i + 1) * P, :], xt[i],
                          deps=["vector"])

        sch.ops["gpsimd"].append(lambda h: h.wait_ge(s_st, st_cnt[0] * 16))

        with nc.Block() as block:
            @block.sync
            def _(sync):
                for op in sch.ops["sync"]:
                    op(sync)

            @block.tensor
            def _(tensor):
                for op in sch.ops["tensor"]:
                    op(tensor)

            @block.vector
            def _(vector):
                for op in sch.ops["vector"]:
                    op(vector)

            @block.scalar
            def _(scalar):
                for op in sch.ops["scalar"]:
                    op(scalar)

            @block.gpsimd
            def _(gpsimd):
                for op in sch.ops["gpsimd"]:
                    op(gpsimd)

    return nc, taps



def _pack_inputs(L0_attn_query, L0_attn_conv_weight, L0_ffn_W_fc,
                 L0_ffn_ln_w, L0_ffn_ln_b, L0_ffn_W_proj):
    x = np.asarray(L0_attn_query, dtype=np.float32)
    cw = np.asarray(L0_attn_conv_weight, dtype=np.float32)
    wfc = np.ascontiguousarray(np.asarray(L0_ffn_W_fc, dtype=np.float32))
    wproj = np.ascontiguousarray(np.asarray(L0_ffn_W_proj, dtype=np.float32))
    lnw = np.asarray(L0_ffn_ln_w, dtype=np.float32)
    lnb = np.asarray(L0_ffn_ln_b, dtype=np.float32)

    B, S, _ = x.shape
    Q = S // 4
    cwp = np.zeros((P, DCH * K9 * 18), np.float32)
    for c in range(DCH):
        for k in range(K9):
            for hh in range(2):
                h = 2 * c + hh
                for j in range(K9):
                    cwp[hh * HS:(hh + 1) * HS, (c * K9 + k) * 18 + hh * K9 + j] = \
                        cw[h * K9 + j, :, k]
    lnw_d = np.ascontiguousarray(lnw.reshape(HCH, P).T)
    lnb_d = np.ascontiguousarray(lnb.reshape(HCH, P).T)
    identb = np.eye(P, dtype=ml_dtypes.bfloat16)
    identf = np.eye(P, dtype=np.float32)
    onesf = np.ones((P, P), np.float32)
    selp = np.zeros((16, NPAIR * P), ml_dtypes.bfloat16)
    for pair in range(NPAIR):
        selp[2 * pair, pair * P: pair * P + HS] = 1
        selp[2 * pair + 1, pair * P + HS: (pair + 1) * P] = 1

    in_maps = []
    for b in range(B):
        for q in range(4):
            lo, hi = q * Q, (q + 1) * Q
            sl = x[b, max(0, lo - 4):min(S, hi + 4), :]
            padl = np.zeros((4 - (lo - max(0, lo - 4)), D), np.float32)
            padr = np.zeros((4 - (min(S, hi + 4) - hi), D), np.float32)
            shard = np.ascontiguousarray(np.concatenate([padl, sl, padr], axis=0))
            in_maps.append({
                "xin": shard, "wfc": wfc, "wproj": wproj,
                "lnw_d": lnw_d, "lnb_d": lnb_d, "cwpack": cwp,
                "identb_d": identb, "identf_d": identf, "onesf_d": onesf,
                "selp_d": selp,
            })
    return in_maps


_cached = {}


def kernel(L0_attn_query, L0_attn_conv_weight, L0_ffn_W_fc,
           L0_ffn_ln_w, L0_ffn_ln_b, L0_ffn_W_proj):
    in_maps = _pack_inputs(L0_attn_query, L0_attn_conv_weight, L0_ffn_W_fc,
                           L0_ffn_ln_w, L0_ffn_ln_b, L0_ffn_W_proj)
    if "nc" not in _cached:
        _cached["nc"] = build_program(debug_taps=False)[0]
    res = run_bass_kernel_spmd(_cached["nc"], in_maps, core_ids=list(range(8)))
    B, S = 2, 8192
    Q = S // 4
    outp = np.empty((B, S, D), np.float32)
    ci = 0
    for b in range(B):
        for q in range(4):
            outp[b, q * Q:(q + 1) * Q, :] = res.results[ci]["out"]
            ci += 1
    return outp


# version 37
# speedup vs baseline: 1.0302x; 1.0195x over previous
import contextlib
import numpy as np
import ml_dtypes
import concourse.bass as bass
import concourse.mybir as mybir
from concourse.bass_utils import run_bass_kernel_spmd

F32 = mybir.dt.float32
BF16 = mybir.dt.bfloat16
AF = mybir.ActivationFunctionType
ALU = mybir.AluOpType
AX = mybir.AxisListType

P = 128
SG = 1024
NSG = 2
TCORE = SG * NSG
D = 1024
DCH = D // P
H = 4096
HCH = H // P
HEADS, HS, K9 = 16, 64, 9
NPAIR = HEADS // 2
NTIL = SG // P
LN_EPS = 1e-5
ABS_EPS = 1e-6

ENGS = ("sync", "tensor", "vector", "scalar", "gpsimd")


class Sched:
    def __init__(self):
        self.ops = {e: [] for e in ENGS}
        self.cnt = {e: 0 for e in ENGS}
        self.sems = None

    def emit(self, eng, fn, deps=None, inc=True, at=None):
        waits = []
        if deps:
            for d in deps:
                if d == eng:
                    continue
                c = self.cnt[d]
                if c > 0:
                    waits.append((d, c))
        if at:
            for d, c in at.items():
                if d != eng and c > 0:
                    waits.append((d, c))
        my_inc = None
        if inc:
            self.cnt[eng] += 1
            my_inc = self.cnt[eng]
        sems = lambda: self.sems

        def op(h):
            for (d, c) in waits:
                h.wait_ge(sems()[d], c)
            last = fn(h)
            if my_inc is not None:
                if last is None:
                    last = h.nop()
                last.then_inc(sems()[eng], 1)
        self.ops[eng].append(op)

    def barrier(self, engs=("tensor", "vector", "scalar")):
        snap = dict(self.cnt)
        sems = lambda: self.sems
        for e in engs:
            waits = [(d, snap[d]) for d in engs if d != e and snap[d] > 0]

            def op(h, waits=waits):
                for (d, c) in waits:
                    h.wait_ge(sems()[d], c)
            self.ops[e].append(op)


def build_program(debug_taps=False, skip_ffn=False, paranoid=(), oldrs=False, w0_serial=False, w0_trunc=None):
    nc = bass.Bass("TRN2", target_bir_lowering=False, debug=False)

    xin = nc.dram_tensor("xin", [TCORE + 8, D], F32, kind="ExternalInput").ap()
    wfc = nc.dram_tensor("wfc", [H, D], F32, kind="ExternalInput").ap()
    wproj = nc.dram_tensor("wproj", [D, H], F32, kind="ExternalInput").ap()
    lnw_in = nc.dram_tensor("lnw_d", [P, HCH], F32, kind="ExternalInput").ap()
    lnb_in = nc.dram_tensor("lnb_d", [P, HCH], F32, kind="ExternalInput").ap()
    cw_in = nc.dram_tensor("cwpack", [P, DCH * K9 * 18], F32, kind="ExternalInput").ap()
    identb_in = nc.dram_tensor("identb_d", [P, P], BF16, kind="ExternalInput").ap()
    identf_in = nc.dram_tensor("identf_d", [P, P], F32, kind="ExternalInput").ap()
    onesf_in = nc.dram_tensor("onesf_d", [P, P], F32, kind="ExternalInput").ap()
    selp_in = nc.dram_tensor("selp_d", [16, NPAIR * P], BF16, kind="ExternalInput").ap()
    out = nc.dram_tensor("out", [TCORE, D], F32, kind="ExternalOutput").ap()
    wfct_scr = nc.dram_tensor("wfct_scr", [16, P, 2048], BF16,
                              kind="Internal").ap()

    taps = {}

    def dbg(name, shape, dtype=F32):
        taps[name] = nc.dram_tensor("tap_" + name, shape, dtype,
                                    kind="ExternalOutput").ap()
        return taps[name]

    sch = Sched()
    ctx = contextlib.ExitStack()
    with ctx:
        sb = lambda name, shape, dt: ctx.enter_context(
            nc.sbuf_tensor(name, shape, dt)).ap()

        xall = sb("xall", [P, NTIL * D], F32)
        xt = [xall[:, i * D:(i + 1) * D] for i in range(NTIL)]
        xh = sb("xh", [8, D], F32)
        wpT_all = sb("wpT_all", [P, HCH * D], BF16)
        wpT = [wpT_all[:, i * D:(i + 1) * D] for i in range(HCH)]
        wsum = sb("wsum", [P, 64], F32)
        wfstg = sb("wfstg", [P, D], BF16)
        arena = sb("arena", [P, 32768], BF16)
        gbuf = sb("gbuf", [P, 2048], F32)
        zscr = sb("zscr", [P, H], BF16)
        zT = sb("zT", [P, HCH * P], BF16)
        x2ln_tm2 = sb("x2ln_tm2", [P, 2 * D], BF16)
        x2ln_tm_b = [x2ln_tm2[:, 0:D], x2ln_tm2[:, D:2 * D]]
        x2lnT2 = sb("x2lnT2", [P, 2 * DCH * P], BF16)
        x2lnT_b = [x2lnT2[:, 0:DCH * P], x2lnT2[:, DCH * P:2 * DCH * P]]
        stat = sb("stat", [P, 64], F32)
        accg = sb("accg", [P, 8], F32)
        lnw_t = sb("lnw_t", [P, HCH], F32)
        lnb_t = sb("lnb_t", [P, HCH], F32)
        lnb_b = sb("lnb_b", [P, HCH], BF16)
        cwb = sb("cwb", [P, DCH * K9 * 18], BF16)
        identb = sb("sb_identb", [P, P], BF16)
        identf = sb("sb_identf", [P, P], F32)
        onesf = sb("sb_onesf", [P, P], F32)
        onesb = sb("onesb", [1, P], BF16)
        selp = sb("sb_selp", [16, NPAIR * P], BF16)
        gam = sb("gam", [P, 4], F32)
        epsv = sb("epsv", [P, 1], F32)
        cvec = sb("cvec", [1, D], BF16)
        scol = sb("scol", [P, 1], BF16)
        srow = sb("srow", [1, P], BF16)

        wfT = [arena[:, c * 4096:(c + 1) * 4096] for c in range(DCH)]
        xlnT = [arena[:, c * 1040: c * 1040 + SG + 8] for c in range(DCH)]
        xln_tm = arena[:, 8320:9344]
        xln_hm = arena[0:8, 9344:10368]
        cklF = arena[:, 10368:16512].bitcast(F32)
        ckl_tm = [arena[:, 16512 + i * 768: 16512 + (i + 1) * 768].bitcast(F32)
                  for i in range(NTIL)]
        wsm_tm = arena[:, 22656:22800]
        wsmK = [arena[0:16, 22800 + k * SG: 22800 + (k + 1) * SG]
                for k in range(K9)]
        gmax = arena[:, 32016:32048].bitcast(F32)
        gsum = arena[:, 32048:32080].bitcast(F32)
        grec = arena[:, 32080:32112].bitcast(F32)
        prods = [arena[:, 10368 + j * 512: 10368 + (j + 1) * 512]
                 for j in range(5)]
        wsb = [arena[:, 10368 + (5 + m) * 512: 10368 + (6 + m) * 512]
               for m in range(4)]
        apair = arena[:, 14976:16000]
        wtile2b = [gbuf, zT.bitcast(F32),
                   wpT_all[:, 0:4096].bitcast(F32),
                   wpT_all[:, 4096:8192].bitcast(F32),
                   wpT_all[:, 8192:12288].bitcast(F32),
                   wpT_all[:, 12288:16384].bitcast(F32),
                   wpT_all[:, 16384:20480].bitcast(F32),
                   wpT_all[:, 20480:24576].bitcast(F32)]
        sgnA2 = zscr.bitcast(F32)
        wfstg2 = x2lnT2
        wqn = x2ln_tm2
        lnsq = wfstg.bitcast(F32)

        psv = lambda name, shape, dt: ctx.enter_context(
            nc.psum_tensor(name, shape, dt)).ap()
        ps_big = psv("ps_big", [P, 3072], F32)
        ps_tA = psv("ps_tA", [P, 512], F32)
        ps_tB = psv("ps_tB", [P, 512], F32)

        ps_fc = ps_big[:, 0:2048]
        ps_proj = ps_big[:, 2048:3072]
        ps_ckl = [ps_big[:, 0:512], ps_big[:, 512:1024], ps_big[:, 1024:1536]]
        ps_wb = [ps_big[:, 1536:2048], ps_big[:, 2048:2560]]
        ps_c = ps_big[0:1, 2048:3072]
        ps_bc = ps_big[:, 2560:2561]
        ps_tAb = ps_tA[:, 0:256].bitcast(BF16)
        ps_w45 = [ps_big[:, 2048:2560].bitcast(BF16),
                  ps_big[:, 2560:3072].bitcast(BF16)]
        ps_tAw = ps_tA.bitcast(BF16)
        ps_zt = [ps_big[:, b * 512:(b + 1) * 512].bitcast(BF16)
                 for b in range(4)]
        ps_tBb = ps_tB[:, 0:256].bitcast(BF16)
        ps_tm = ps_tB[:, 0:128]

        sems = {e: ctx.enter_context(nc.semaphore("s_" + e)) for e in ENGS}
        sch.sems = sems
        s_ld = ctx.enter_context(nc.semaphore("s_ld"))
        s_ld2 = ctx.enter_context(nc.semaphore("s_ld2"))
        s_lda = ctx.enter_context(nc.semaphore("s_lda"))
        s_ldb = ctx.enter_context(nc.semaphore("s_ldb"))
        s_st = ctx.enter_context(nc.semaphore("s_st"))
        s_w0v = ctx.enter_context(nc.semaphore("s_w0v"))
        s_w0s = ctx.enter_context(nc.semaphore("s_w0s"))
        s_w0t = ctx.enter_context(nc.semaphore("s_w0t"))
        s_w0e = ctx.enter_context(nc.semaphore("s_w0e"))
        s_ldc = ctx.enter_context(nc.semaphore("s_ldc"))
        s_ldd = ctx.enter_context(nc.semaphore("s_ldd"))
        s_lde = ctx.enter_context(nc.semaphore("s_lde"))
        s_ldf = ctx.enter_context(nc.semaphore("s_ldf"))
        s_ldg = ctx.enter_context(nc.semaphore("s_ldg"))
        s_ldh = ctx.enter_context(nc.semaphore("s_ldh"))
        ld_groups = {"main": [s_ld, 0], "small": [s_ld2, 0],
                     "wa": [s_lda, 0], "wb": [s_ldb, 0],
                     "wc": [s_ldc, 0], "wd": [s_ldd, 0],
                     "we": [s_lde, 0], "wf": [s_ldf, 0],
                     "wg": [s_ldg, 0], "wh": [s_ldh, 0]}
        WGRP = ("wa", "wb", "wc", "wd", "we", "wf", "wg", "wh")
        st_cnt = [0]

        def load(dst, src, deps=None, group="main"):
            g = ld_groups[group]
            g[1] += 1
            tgt = g[1] * 16
            sem = g[0]
            waits = [(d, sch.cnt[d]) for d in (deps or []) if sch.cnt[d] > 0]

            def op(h):
                for (d, c) in waits:
                    h.wait_ge(sems[d], c)
                h.dma_start(out=dst, in_=src).then_inc(sem, 16)
            sch.ops["sync"].append(op)
            return tgt

        def store(dst, src, deps=None):
            st_cnt[0] += 1
            tgt = st_cnt[0] * 16
            waits = [(d, sch.cnt[d]) for d in (deps or []) if sch.cnt[d] > 0]

            def op(h):
                for (d, c) in waits:
                    h.wait_ge(sems[d], c)
                h.dma_start(out=dst, in_=src).then_inc(s_st, 16)
            sch.ops["sync"].append(op)
            return tgt

        def wait_ld(eng, tgt, group="main"):
            sem = ld_groups[group][0]
            sch.ops[eng].append(lambda h: h.wait_ge(sem, tgt))

        def wait_ld2(eng, tgt):
            wait_ld(eng, tgt, group="small")

        def wait_st(eng, tgt):
            sch.ops[eng].append(lambda h: h.wait_ge(s_st, tgt))

        def dbg_store(dst, src):
            tgt = store(dst, src, deps=["vector", "scalar", "tensor"])
            for e in ("vector", "scalar", "tensor"):
                wait_st(e, tgt)

        E = sch.emit

        load(identb, identb_in[:, :])
        load(identf, identf_in[:, :])
        load(onesf, onesf_in[:, :])
        t_selp = load(selp, selp_in[:, :], group="small")
        load(lnw_t, lnw_in[:, :])
        load(lnb_t, lnb_in[:, :])
        cw_f = gbuf[:, 0:DCH * K9 * 18]
        t1 = load(cw_f, cw_in[:, :])
        wait_ld("vector", t1)
        E("vector", lambda v: v.tensor_copy(cwb, cw_f))
        E("vector", lambda v: v.tensor_copy(lnb_b, lnb_t))
        E("vector", lambda v: v.tensor_copy(onesb, onesf[0:1, :]))
        E("vector", lambda v: v.memset(epsv, LN_EPS))

        w0_steps = []
        w0_state = {"last_st": 0}
        gscr = zscr.bitcast(F32)[:, 0:1024]
        w0c = {"v": 0, "s": 0, "t": 0, "sgn": {}, "tt": {}, "tp": {},
               "ev": {0: 0, 1: 0}, "gamma_v0": 0, "gamma_v1": 0}

        def w0_chunks():
            out = []
            for i in range(16):
                src = wfc.rearrange("(a p) d -> p (a d)", p=P)[
                    :, 0:2048] if False else None
                out.append(("fc", i))
            for i in range(16):
                out.append(("pj", i))
            return out

        def w0_src(kind, i):
            if kind == "fc":
                return (wfc[256 * i: 256 * (i + 1), :]
                        .rearrange("(a p) d -> p a d", p=P))
            q0 = 2 * i
            rb = q0 // 4
            qa, qb = q0 % 4, (q0 + 1) % 4
            return (wproj[rb * P:(rb + 1) * P, qa * D:(qb + 1) * D]
                    .rearrange("p (a d) -> p a d", d=D))

        def w0_load(dst, src, grp, s_gate=0, v_gate=0, queue="sync"):
            g = ld_groups[grp]
            g[1] += 1
            tgt = g[1] * 16

            def op(h, tgt=tgt, s_gate=s_gate, v_gate=v_gate, g=g):
                if s_gate > 0:
                    h.wait_ge(s_w0s, s_gate)
                if v_gate > 0:
                    h.wait_ge(s_w0v, v_gate)
                h.dma_start(out=dst, in_=src).then_inc(g[0], 16)
            sch.ops[queue].append(op)
            return tgt

        FCB = (0, 1, 6, 7)
        PJB = (2, 3, 4, 5)

        def gamma_step(j, kind, i, bsel):
            def go():
                bmap = FCB if bsel == 0 else PJB
                bi = bmap[j % 4]
                buf = wtile2b[bi]
                grp = (WGRP[{0: 0, 1: 1, 6: 2, 7: 3}[bi]] if bsel == 0
                       else WGRP[4 + PJB.index(bi)])
                tgt = w0_load(buf.rearrange("p (a d) -> p a d", d=D),
                              w0_src(kind, i), grp,
                              s_gate=w0c["sgn"].get(("g", bsel, j - 4), 0),
                              queue=("gpsimd" if bsel else "sync"))
                w0c["s"] += 1
                w0c["sgn"][("g", bsel, j)] = w0c["s"]

                def op(h, tgt=tgt, j=j, grp=grp, bi=bi):
                    h.wait_ge(ld_groups[grp][0], tgt)
                    h.activation(wtile2b[bi][:, 0:1024],
                                 wtile2b[bi][:, 0:1024], AF.Abs,
                                 accum_out=wsum[:, 2 * j:2 * j + 1])
                    h.activation(wtile2b[bi][:, 1024:2048],
                                 wtile2b[bi][:, 1024:2048], AF.Abs,
                                 accum_out=wsum[:, 2 * j + 1:2 * j + 2]
                                 ).then_inc(s_w0s, 1)
                sch.ops["scalar"].append(op)
            return go

        def gamma_final(which, sgate_needed):
            def go():
                red = stat[:, 60 + which:61 + which]
                w0c["v"] += 0
                def opv(h, sg_=sgate_needed(), which=which, red=red):
                    h.wait_ge(s_w0s, sg_)
                    h.reduce_sum(red, wsum[:, 32 * which:32 * which + 32],
                                 axis=AX.X)
                    h.drain().then_inc(s_w0v, 1)
                w0c["v"] += 1
                vred = w0c["v"]
                sch.ops["vector"].append(opv)
                nelem = float(H * D)
                w0c["t"] += 1
                tmark = w0c["t"]

                evg = w0c["ev"][0]

                def opt(h, red=red, vred=vred, evg=evg):
                    h.wait_ge(s_w0v, vred)
                    if evg > 0:
                        h.wait_ge(s_w0s, evg)
                    h.matmul(ps_bc, onesf, red, start=True,
                             stop=True).then_inc(s_w0t, 1)
                sch.ops["tensor"].append(opt)
                w0c["v"] += 1

                def opv2(h, tmark=tmark, which=which):
                    h.wait_ge(s_w0t, tmark)
                    c0 = 2 * which
                    h.tensor_scalar(gam[:, c0:c0 + 1], ps_bc,
                                    0.5 / nelem, 0.5 * ABS_EPS,
                                    op0=ALU.mult, op1=ALU.add)
                    h.drain()
                    h.tensor_scalar(gam[:, c0 + 1:c0 + 2],
                                    gam[:, c0:c0 + 1], -1.0, None,
                                    op0=ALU.mult)
                    h.drain().then_inc(s_w0v, 1)
                sch.ops["vector"].append(opv2)
                w0c["gamma_v%d" % which] = w0c["v"]
            return go

        def tern_step(jj, kind, i):
            def go():
                j = jj
                if kind == "fc":
                    bi = FCB[jj % 4]
                    pv = jj - 4
                else:
                    bi = (0, 1)[jj % 2]
                    pv = jj - 4 if jj < 18 else jj - 2
                grp = WGRP[{0: 0, 1: 1, 6: 2, 7: 3}[bi]]
                buf = wtile2b[bi]
                sg_ = w0c["sgn"].get(("t", pv))
                if sg_ is None:
                    sg_ = w0c["sgn"][("g", 0, 12 + jj)]
                tgt = w0_load(buf.rearrange("p (a d) -> p a d", d=D),
                              w0_src(kind, i), grp, s_gate=sg_,
                              v_gate=w0c["tt"].get(pv, 0))
                c0 = 0 if kind == "fc" else 2
                gv = "gamma_v%d" % (0 if kind == "fc" else 1)
                w0c["s"] += 2
                w0c["sgn"][("t", jj)] = w0c["s"]
                vgate_sgnA = w0c["tt"].get(jj - 1, 0)

                def ops_act(h, tgt=tgt, j=j, c0=c0, grp=grp, vg=vgate_sgnA,
                            gv=gv, bi=bi):
                    h.wait_ge(ld_groups[grp][0], tgt)
                    h.wait_ge(s_w0v, max(vg, w0c[gv]))
                    h.activation(sgnA2, wtile2b[bi], AF.Sign,
                                 bias=gam[:, c0 + 1:c0 + 2],
                                 scale=1.0).then_inc(s_w0s, 1)
                    h.activation(wtile2b[bi], wtile2b[bi], AF.Sign,
                                 bias=gam[:, c0:c0 + 1],
                                 scale=1.0).then_inc(s_w0s, 1)
                sch.ops["scalar"].append(ops_act)
                if w0c.get("pend_st") is not None:
                    sch.ops["scalar"].append(w0c.pop("pend_st"))
                w0c["v"] += 1
                w0c["tt"][jj] = w0c["v"]
                smark = w0c["s"]
                tpgate = w0c["tp"].get(jj - 1, 0)

                def opv(h, j=j, smark=smark, tg=tpgate, bi=bi):
                    h.wait_ge(s_w0s, smark)
                    if tg > 0:
                        h.wait_ge(s_w0t, tg)
                    h.tensor_tensor(wqn[:, 0:2048], sgnA2, wtile2b[bi],
                                    op=ALU.add).then_inc(s_w0v, 1)
                sch.ops["vector"].append(opv)
                vmark = w0c["v"]
                w0c["t"] += 1
                evgate = w0c["ev"][0]

                def opt(h, vmark=vmark, evgate=evgate):
                    h.wait_ge(s_w0v, vmark)
                    if evgate > 0:
                        h.wait_ge(s_w0s, evgate)
                    for a in range(2):
                        for c in range(DCH):
                            ins = h.transpose(
                                ps_w45[a][:, c * P:(c + 1) * P],
                                wqn[:, a * 1024 + c * P:
                                    a * 1024 + (c + 1) * P], identb)
                    ins.then_inc(s_w0t, 1)
                sch.ops["tensor"].append(opt)
                tmark = w0c["t"]
                stg = w0c.get("last_fc_st", 0) if kind == "fc" else 0
                for a in range(2):
                    if kind == "fc":
                        edst = wfstg2[:, a * 1024:(a + 1) * 1024]
                        esrc = ps_w45[a]
                    else:
                        qq = (2 * i + a) % 4
                        hc0 = qq * 8
                        rb = (2 * i + a) // 4
                        edst = wpT_all.rearrange("p (h r) -> p h r", r=D)[
                            :, hc0:hc0 + 8, rb * P: rb * P + P]
                        esrc = ps_w45[a].rearrange("p (x b) -> p x b", b=P)
                    w0c["s"] += 1

                    def ope(h, d=edst, sr=esrc, tmark=tmark, stg=stg,
                            a=a):
                        h.wait_ge(s_w0t, tmark)
                        if a == 0 and stg > 0:
                            h.wait_ge(s_st, stg)
                        h.tensor_copy(d, sr).then_inc(s_w0s, 1)
                    sch.ops["vector"].append(ope)
                w0c["ev"][0] = w0c["s"]
                w0c["tp"][jj] = w0c["t"]
                if kind == "fc":
                    w0c["fcev"] = w0c.get("fcev", 0) + 1
                    fcev = w0c["fcev"]
                    sch.ops["vector"].append(
                        lambda h: h.nop().then_inc(s_w0e, 1))
                    st_cnt[0] += 1

                    def opst(h, i=i, fcev=fcev):
                        h.wait_ge(s_w0e, fcev)
                        h.dma_start(out=wfct_scr[i, :, :],
                                    in_=wfstg2).then_inc(s_st, 16)
                    w0c["pend_st"] = opst
                    w0c["last_fc_st"] = st_cnt[0] * 16
                    w0_state["last_st"] = st_cnt[0] * 16
            return go

        for j in range(16):
            w0_steps.append(gamma_step(j, "fc", j, 0))
        w0_steps.append(gamma_final(0, lambda: w0c["sgn"][("g", 0, 15)]))
        for j in range(16):
            w0_steps.append(gamma_step(16 + j, "pj", j, 2))
            w0_steps.append(tern_step(j, "fc", j))
        w0_steps.append(gamma_final(1, lambda: w0c["sgn"][("g", 2, 31)]))
        for j in range(16):
            w0_steps.append(tern_step(16 + j, "pj", j))

        def w0_post():
            if w0c.get("pend_st") is not None:
                sch.ops["scalar"].append(w0c.pop("pend_st"))
            sv, ss, st_ = w0c["v"], w0c["s"], w0c["t"]
            for e, sem, c in (("tensor", s_w0s, ss), ("vector", s_w0s, ss),
                              ("scalar", s_w0v, sv)):
                sch.ops[e].append(lambda h, sem=sem, c=c: h.wait_ge(sem, c))
            for w in range(2):
                for hc in range(HCH):
                    E("tensor", lambda t, w=w, hc=hc: t.matmul(
                        ps_c[:, w * 512:(w + 1) * 512], lnb_b[:, hc:hc + 1],
                        wpT[hc][:, w * 512:(w + 1) * 512],
                        start=(hc == 0), stop=(hc == HCH - 1)),
                        deps=(["vector", "scalar"] if hc == 0 else None))
            E("vector", lambda v: v.tensor_scalar(
                cvec, ps_c, 0.5, None, op0=ALU.mult), deps=["tensor"])
            for hc in range(HCH):
                E("vector", lambda v, hc=hc: v.tensor_scalar(
                    wpT[hc], wpT[hc], lnw_t[:, hc:hc + 1], None, op0=ALU.mult))

        def drip(n=4):
            for _ in range(n):
                if w0_steps:
                    w0_steps.pop(0)()

        def newton_rs(v_ap, rs_ap, seed, iters, tmp_base=56, rows=P):
            tmp = stat[0:rows, tmp_base:tmp_base + (v_ap.shape[1])]
            E("vector", lambda v: v.memset(rs_ap, seed))
            for _ in range(iters):
                E("vector", lambda v: v.drain())
                E("vector", lambda v: v.tensor_tensor(tmp, rs_ap, rs_ap,
                                                      op=ALU.mult))
                E("vector", lambda v: v.drain())
                E("vector", lambda v: v.tensor_tensor(tmp, tmp, v_ap,
                                                      op=ALU.mult))
                E("vector", lambda v: v.drain())
                E("vector", lambda v: v.tensor_scalar(
                    tmp, tmp, -0.5, 1.5, op0=ALU.mult, op1=ALU.add))
                E("vector", lambda v: v.drain())
                E("vector", lambda v: v.tensor_tensor(rs_ap, rs_ap, tmp,
                                                      op=ALU.mult))
            E("vector", lambda v: v.drain())

        def ln_collect(i, x_ap, deps=None, sq_free=False):
            E("vector", lambda v, i=i, x_ap=x_ap: v.reduce_sum(
                stat[:, 16 + i:17 + i], x_ap, axis=AX.X), deps=deps)
            E("vector", lambda v: v.drain())
            E("scalar", lambda s, i=i, x_ap=x_ap: s.activation(
                lnsq, x_ap[:, 0:512], AF.Square,
                accum_out=stat[:, 24 + i:25 + i]),
                deps=(None if sq_free else ["vector"]))
            E("scalar", lambda s, i=i, x_ap=x_ap: s.activation(
                lnsq, x_ap[:, 512:1024], AF.Square,
                accum_out=stat[:, 32 + i:33 + i]))

        def ln_finish(mode, seed=0.6, iters=8, split_sums=False):
            n = NTIL
            mu = stat[:, 0:n]
            rs = stat[:, 8:8 + n]
            var = stat[:, 48:48 + n]
            scr = stat[:, 16:16 + n]
            E("vector", lambda v: v.drain())
            if split_sums:
                E("vector", lambda v: v.tensor_tensor(
                    stat[:, 16:16 + n], stat[:, 16:16 + n],
                    stat[:, 56:56 + n], op=ALU.add), deps=["scalar"])
                E("vector", lambda v: v.drain())
            E("vector", lambda v: v.tensor_scalar(
                mu, stat[:, 16:16 + n], 1.0 / D, None, op0=ALU.mult),
                deps=(None if split_sums else ["scalar"]))
            E("vector", lambda v: v.tensor_tensor(
                var, stat[:, 24:24 + n], stat[:, 32:32 + n], op=ALU.add))
            E("vector", lambda v: v.drain())
            E("vector", lambda v: v.tensor_scalar(
                var, var, 1.0 / D, None, op0=ALU.mult))
            E("vector", lambda v: v.tensor_tensor(scr, mu, mu, op=ALU.mult))
            E("vector", lambda v: v.drain())
            E("vector", lambda v: v.tensor_tensor(var, var, scr,
                                                  op=ALU.subtract))
            E("vector", lambda v: v.drain())
            if mode == "A":
                E("scalar", lambda s: s.activation(
                    scr, var, AF.Ln, bias=epsv, scale=1.0), deps=["vector"])
                E("scalar", lambda s: s.activation(
                    rs, scr, AF.Exp, scale=-0.5))
            else:
                E("vector", lambda v: v.tensor_scalar(
                    var, var, 1.0, LN_EPS, op0=ALU.mult, op1=ALU.add))
                newton_rs(var, rs, seed, iters, tmp_base=24)

        def xlnT_quad(c0, tok0):
            a0 = c0 * 1040 + 4 + tok0
            return (arena[:, a0: a0 + 4 * 1040]
                    .rearrange("p (c r) -> p c r", r=1040)[:, :, 0:P])

        if w0_trunc is not None:
            del w0_steps[w0_trunc:]
        if w0_serial:
            drip(len(w0_steps))
            if w0_trunc is None:
                w0_post()
        def keepwarm(sg):
            pass

        def halo_ln(mode, gate_t=None):
            E("vector", lambda v: v.reduce_sum(
                stat[0:8, 44:45], xh, axis=AX.X))
            E("scalar", lambda s: s.activation(
                lnsq[0:8, :], xh[0:8, 0:512], AF.Square,
                accum_out=stat[0:8, 45:46]), deps=["vector"])
            E("scalar", lambda s: s.activation(
                lnsq[0:8, :], xh[0:8, 512:1024], AF.Square,
                accum_out=stat[0:8, 46:47]))
            E("vector", lambda v: v.tensor_tensor(
                stat[0:8, 45:46], stat[0:8, 45:46], stat[0:8, 46:47],
                op=ALU.add), deps=["scalar"])
            E("vector", lambda v: v.drain())
            E("vector", lambda v: v.tensor_scalar(
                stat[0:8, 44:45], stat[0:8, 44:45], 1.0 / D, None,
                op0=ALU.mult))
            E("vector", lambda v: v.tensor_scalar(
                stat[0:8, 45:46], stat[0:8, 45:46], 1.0 / D, None,
                op0=ALU.mult))
            E("vector", lambda v: v.drain())
            E("vector", lambda v: v.tensor_tensor(
                stat[0:8, 46:47], stat[0:8, 44:45], stat[0:8, 44:45],
                op=ALU.mult))
            E("vector", lambda v: v.drain())
            E("vector", lambda v: v.tensor_tensor(
                stat[0:8, 45:46], stat[0:8, 45:46], stat[0:8, 46:47],
                op=ALU.subtract))
            E("vector", lambda v: v.drain())
            if mode == "act":
                E("scalar", lambda s: s.activation(
                    stat[0:8, 47:48], stat[0:8, 45:46], AF.Ln,
                    bias=epsv[0:8, :], scale=1.0), deps=["vector"])
                E("scalar", lambda s: s.activation(
                    stat[0:8, 45:46], stat[0:8, 47:48], AF.Exp, scale=-0.5))
                rs_c = 45
            else:
                E("vector", lambda v: v.tensor_scalar(
                    stat[0:8, 45:46], stat[0:8, 45:46], 1.0, LN_EPS,
                    op0=ALU.mult, op1=ALU.add))
                E("vector", lambda v: v.drain())
                newton_rs(stat[0:8, 45:46], stat[0:8, 47:48], 0.6, 6,
                          tmp_base=24, rows=8)
                rs_c = 47
            E("vector", lambda v, rs_c=rs_c: v.tensor_scalar(
                xln_hm, xh, stat[0:8, 44:45], stat[0:8, rs_c:rs_c + 1],
                op0=ALU.subtract, op1=ALU.mult),
                deps=(["scalar"] if mode == "act" else None),
                at=({"tensor": gate_t} if gate_t else None))

        def a_prep(gate_t=None, dodrip=False):
            ee = "scalar" if gate_t else "vector"
            tprev = gate_t
            for i in range(NTIL):
                E("vector", lambda v, i=i: v.tensor_scalar(
                    xln_tm, xt[i], stat[:, i:i + 1], stat[:, 8 + i:9 + i],
                    op0=ALU.subtract, op1=ALU.mult),
                    deps=(["scalar"] if i == 0 else None),
                    at=({"tensor": tprev} if tprev else None))
                for c4 in range(2):
                    pst = ps_tAb if (c4 == 0) else ps_tBb
                    for cj in range(4):
                        c = c4 * 4 + cj
                        E("tensor", lambda t, c=c, cj=cj, p=pst: t.transpose(
                            p[:, cj * P:(cj + 1) * P],
                            xln_tm[:, c * P:(c + 1) * P], identb),
                            deps=((["vector"] if ee == "vector"
                                   else ["vector", "scalar"])
                                  if cj == 0 else None))
                    if ee == "vector":
                        E("vector", lambda v, c4=c4, i=i, p=pst:
                          v.tensor_copy(xlnT_quad(c4 * 4, i * P), p[:]),
                          deps=["tensor"])
                    else:
                        E("scalar", lambda sc, c4=c4, i=i, p=pst:
                          sc.copy(xlnT_quad(c4 * 4, i * P), p[:]),
                          deps=["tensor"])
                if ee == "scalar":
                    tprev = sch.cnt["tensor"]
                if dodrip:
                    drip(3)
            for c in range(DCH):
                pst = ps_tAb if (c % 2 == 0) else ps_tBb
                E("tensor", lambda t, c=c, p=pst: t.transpose(
                    p[:, 0:8], xln_hm[:, c * P:(c + 1) * P],
                    identb[0:8, 0:8]),
                    deps=(["vector"] if ee == "vector"
                          else ["vector", "scalar"]))
                if ee == "vector":
                    E("vector", lambda v, c=c, p=pst: v.tensor_copy(
                        xlnT[c][:, 0:4], p[:, 0:4]), deps=["tensor"])
                    E("vector", lambda v, c=c, p=pst: v.tensor_copy(
                        xlnT[c][:, SG + 4:SG + 8], p[:, 4:8]))
                else:
                    E("scalar", lambda sc, c=c, p=pst: sc.copy(
                        xlnT[c][:, 0:4], p[:, 0:4]), deps=["tensor"])
                    E("scalar", lambda sc, c=c, p=pst: sc.copy(
                        xlnT[c][:, SG + 4:SG + 8], p[:, 4:8]))
            E("vector", lambda v: v.memset(cklF, 0.0),
              at=({"tensor": gate_t} if gate_t else None))
            if dodrip:
                drip(3)

        ov = {}

        def ov_tile(j):
            g = ld_groups[WGRP[j]]
            g[1] += 1
            tgt_o = g[1] * 16
            vc = sch.cnt["vector"]

            def opst(h, j=j, g=g, vc=vc):
                if vc > 0:
                    h.wait_ge(sems["vector"], vc)
                h.dma_start(out=out[j * P:(j + 1) * P, :],
                            in_=xt[j]).then_inc(g[0], 16)
            sch.ops["sync"].append(opst)
            g[1] += 1
            tgt = g[1] * 16

            def opld(h, tgt_o=tgt_o, j=j, g=g):
                h.wait_ge(g[0], tgt_o)
                h.dma_start(out=xt[j],
                            in_=xin[SG + 4 + j * P: SG + 4 + (j + 1) * P, :]
                            ).then_inc(g[0], 16)
            sch.ops["sync"].append(opld)
            ov[("xt", j)] = tgt

        def ov_collect(j):
            wait_ld("vector", ov[("xt", j)], group=WGRP[j])
            ln_collect(j, xt[j])

        for sg in range(NSG):
            base = sg * SG
            if sg == 1:
                sch.barrier()
            if sg == 0:
                tloads = []
                for i in range(NTIL):
                    tloads.append(load(xt[i], xin[base + 4 + i * P:
                                                  base + 4 + (i + 1) * P, :],
                                       deps=["vector", "tensor", "scalar"]))
                th_l = load(xh[0:4, :], xin[base + 0: base + 4, :],
                            group="small")
                th_r = load(xh[4:8, :], xin[base + SG + 4: base + SG + 8, :],
                            group="small")

                wait_ld("vector", tloads[-1])
                wait_ld("scalar", tloads[-1])
                for i in range(NTIL):
                    ln_collect(i, xt[i], sq_free=True)
                    drip(3)
                    keepwarm(sg)
                ln_finish("A")
                drip(3)
                if debug_taps:
                    tls = dbg("ln1stat", [P, 64], F32)
                    dbg_store(tls[:, :], stat)
                wait_ld2("vector", th_l)
                wait_ld2("vector", th_r)
                halo_ln("act")
                a_prep(dodrip=True)
                sch.barrier()

            if debug_taps and sg == 0:
                tpx = dbg("xlnT0", [P, SG + 8], BF16)
                store(tpx[:, :], xlnT[0], deps=["vector"])

            for half in range(2):
                for g in range(3):
                    ps = ps_ckl[g]
                    ncc = 3 if g < 2 else 2
                    for cc in range(ncc):
                        c = 3 * g + cc
                        off = 32 * cc
                        for k in range(K9):
                            E("tensor", lambda t, c=c, k=k, off=off, ps=ps,
                              half=half: t.matmul(
                                  ps[off:off + 18, :],
                                  cwb[:, (c * K9 + k) * 18:
                                      (c * K9 + k) * 18 + 18],
                                  xlnT[c][:, half * 512 + k:
                                          half * 512 + k + 512],
                                  start=(k == 0), stop=(k == K9 - 1)),
                              deps=(["vector", "scalar"] if (cc == 0 and k == 0)
                                    else None))
                    for cc in range(ncc):
                        off = 32 * cc
                        E("scalar", lambda sc, ps=ps, off=off, g=g, half=half:
                          sc.copy(
                              cklF[off:off + 18,
                                   g * SG + half * 512: g * SG + half * 512 + 512],
                              ps[off:off + 18, :]),
                          deps=(["tensor"] if cc == 0 else None))
                    drip(3)
            sch.barrier()

            if debug_taps and sg == 0:
                tpk = dbg("cklF", [P, 3 * SG], F32)
                store(tpk[:, :], cklF, deps=["vector"])

            sm_ev = {}
            for i in range(NTIL):
                for g in range(3):
                    slot = (i * 3 + g) % 4
                    pst = ps_zt[slot][:, 0:256].bitcast(F32)
                    at_t = ({"vector": sm_ev[slot]} if slot in sm_ev
                            else None)
                    E("tensor", lambda t, i=i, g=g, pst=pst: t.transpose(
                        pst, cklF[:, g * SG + i * P: g * SG + (i + 1) * P],
                        identf), deps=(["vector"] if (i == 0 and g == 0)
                                       else None), at=at_t)
                    E("vector", lambda v, i=i, g=g, pst=pst: v.tensor_copy(
                        ckl_tm[i][:, g * P:(g + 1) * P], pst),
                        deps=["tensor"])
                    sm_ev[slot] = sch.cnt["vector"]
                    keepwarm(sg)
                x12 = (ckl_tm[i]
                       .rearrange("p (m r) -> p m r", r=32)[:, :, 0:18]
                       .rearrange("p m (hh j) -> p m hh j", j=K9))
                gm24 = stat[:, 16:40].rearrange("p (m hh) -> p m hh", hh=2)
                gs24 = stat[:, 40:64].rearrange("p (m hh) -> p m hh", hh=2)
                E("vector", lambda v, x12=x12, gm24=gm24: v.tensor_reduce(
                    gm24, x12, axis=AX.X, op=ALU.max))
                E("vector", lambda v: v.drain())
                E("vector", lambda v, x12=x12, gm24=gm24: v.tensor_tensor(
                    x12, x12,
                    gm24.unsqueeze(3).broadcast_to([P, 12, 2, K9]),
                    op=ALU.subtract))
                E("scalar", lambda s, i=i: s.activation(
                    ckl_tm[i], ckl_tm[i], AF.Exp), deps=["vector"])
                E("vector", lambda v, x12=x12, gs24=gs24: v.tensor_reduce(
                    gs24, x12, axis=AX.X, op=ALU.add), deps=["scalar"])
                E("vector", lambda v: v.drain())
                E("vector", lambda v: v.reciprocal(
                    stat[:, 16:40], stat[:, 40:64]))
                E("vector", lambda v: v.drain())
                for g in range(3):
                    ncc = 3 if g < 2 else 2
                    x4 = (ckl_tm[i][:, g * P: g * P + 32 * ncc]
                          .rearrange("p (cc r) -> p cc r", r=32)[:, :, 0:18]
                          .rearrange("p cc (hh j) -> p cc hh j", j=K9))
                    grv = (stat[:, 16 + 8 * g: 16 + 8 * g + 2 * ncc]
                           .rearrange("p (cc hh) -> p cc hh", hh=2))
                    wout = (wsm_tm[:, 54 * g: 54 * g + 18 * ncc]
                            .rearrange("p (cc hh j) -> p cc hh j", hh=2, j=K9))
                    E("vector", lambda v, x4=x4, grv=grv, wout=wout, ncc=ncc:
                      v.tensor_tensor(
                          wout, x4,
                          grv.unsqueeze(3).broadcast_to([P, ncc, 2, K9]),
                          op=ALU.mult))
                wsm3 = wsm_tm.rearrange("p (h k) -> p h k", k=K9)
                for k in range(K9):
                    pst = ps_tAw if k < 8 else ps_tBb
                    off = (k % 8) * P
                    E("tensor", lambda t, k=k, wsm3=wsm3, p=pst, off=off:
                      t.transpose(p[0:16, off:off + P], wsm3[:, :, k], identb),
                      deps=(["vector"] if k in (0, 8) else None))
                wk8 = (arena[0:16, 22800: 22800 + 8 * SG]
                       .rearrange("p (k r) -> p k r", r=SG)
                       [:, :, i * P:(i + 1) * P])
                E("vector", lambda v, i=i, wk8=wk8: v.tensor_copy(
                    wk8, ps_tAw[0:16, :].rearrange("p (k r) -> p k r", r=P)),
                    deps=["tensor"])
                E("vector", lambda v, i=i: v.tensor_copy(
                    wsmK[8][:, i * P:(i + 1) * P], ps_tBb[0:16, 0:P]))
                drip(4)
            sch.barrier()

            if debug_taps and sg == 0:
                tpw = dbg("wsmK0", [16, SG], BF16)
                store(tpw[:, :], wsmK[0], deps=["vector"])

            wait_ld2("tensor", t_selp)
            ps_ap = [ps_big[:, j * 512:(j + 1) * 512] for j in range(4)]
            ap_mm = {}
            ap_cp = {}
            ap_mul = {}
            cq = []
            kidx = 0
            for pair in range(NPAIR):
                sp = selp[:, pair * P:(pair + 1) * P]
                for w in range(2):
                    for k in range(K9):
                        slot = kidx % 4
                        wslot = kidx % 4
                        at_mm = {}
                        if kidx >= 4:
                            at_mm["scalar"] = ap_cp[kidx - 4]
                        E("tensor", lambda t, k=k, w=w, sp=sp, slot=slot:
                          t.matmul(ps_ap[slot], sp,
                                   wsmK[k][:, w * 512:(w + 1) * 512],
                                   start=True, stop=True),
                          at=(at_mm or None),
                          deps=(["vector"] if kidx == 0 else None))
                        ap_mm[kidx] = sch.cnt["tensor"]
                        at_cp = {"tensor": ap_mm[kidx]}
                        if kidx >= 4:
                            at_cp["vector"] = ap_mul[kidx - 4]
                        E("scalar", lambda s, slot=slot, wslot=wslot: s.copy(
                            wsb[wslot], ps_ap[slot]), at=at_cp)
                        ap_cp[kidx] = sch.cnt["scalar"]
                        if cq and pair >= 4:
                            fn, at_q = cq.pop(0)
                            E("scalar", fn, at=at_q)
                        xsl = xlnT[pair][:, w * 512 + k: w * 512 + k + 512]
                        if k < 4:
                            E("vector", lambda v, xsl=xsl, k=k, wslot=wslot:
                              v.tensor_tensor(prods[k], xsl, wsb[wslot],
                                              op=ALU.mult),
                              at={"scalar": ap_cp[kidx]})
                        elif k < 8:
                            E("vector", lambda v, xsl=xsl, k=k, wslot=wslot:
                              v.tensor_tensor(prods[4], xsl, wsb[wslot],
                                              op=ALU.mult),
                              at={"scalar": ap_cp[kidx]})
                            E("vector", lambda v, k=k: v.tensor_tensor(
                                prods[k - 4], prods[k - 4], prods[4],
                                op=ALU.add))
                        else:
                            E("vector", lambda v, xsl=xsl, k=k, wslot=wslot:
                              v.tensor_tensor(prods[4], xsl, wsb[wslot],
                                              op=ALU.mult),
                              at={"scalar": ap_cp[kidx]})
                        ap_mul[kidx] = sch.cnt["vector"]
                        kidx += 1
                    E("vector", lambda v: v.tensor_tensor(
                        prods[0], prods[0], prods[1], op=ALU.add))
                    E("vector", lambda v: v.tensor_tensor(
                        prods[2], prods[2], prods[3], op=ALU.add))
                    E("vector", lambda v: v.tensor_tensor(
                        prods[0], prods[0], prods[2], op=ALU.add))
                    E("vector", lambda v, w=w: v.tensor_tensor(
                        apair[:, w * 512:(w + 1) * 512], prods[0], prods[4],
                        op=ALU.add))
                    drip(3)
                for i4 in range(2):
                    pst = ps_tAb if (i4 == 0) else ps_tBb
                    for j in range(4):
                        i = i4 * 4 + j
                        E("tensor", lambda t, i=i, j=j, p=pst: t.transpose(
                            p[:, j * P:(j + 1) * P],
                            apair[:, i * P:(i + 1) * P], identb),
                            deps=(["vector"] if j == 0 else None))
                    dstv = xall.rearrange("p (t r) -> p t r", r=D)[
                        :, i4 * 4:(i4 + 1) * 4, pair * P:(pair + 1) * P]
                    E("vector", lambda v, dstv=dstv, p=pst: v.tensor_tensor(
                        dstv, dstv, p[:].rearrange("p (t r) -> p t r", r=P),
                        op=ALU.add), deps=["tensor"])
                    if (pair == 5 and i4 == 1 and sg == 0
                            and not w0_serial and not skip_ffn
                            and not ov.get("w0done")):
                        drip(len(w0_steps))
                        w0_post()
                        ov["w0done"] = True
                    if pair == 3:
                        vg3 = sch.cnt["vector"]
                        cdst = x2lnT2.bitcast(F32)[:, 0:512]
                        for j in range(4):
                            i = i4 * 4 + j
                            at3 = {"vector": vg3} if (j == 0) else None
                            cq.append((lambda sc, i=i, cdst=cdst:
                                       sc.activation(
                                           cdst, xt[i][:, 0:512], AF.Copy,
                                           accum_out=stat[:, 16 + i:17 + i]),
                                       at3))
                            cq.append((lambda sc, i=i: sc.activation(
                                lnsq, xt[i][:, 0:512], AF.Square,
                                accum_out=stat[:, 24 + i:25 + i]), None))
                    if pair == NPAIR - 1:
                        vg7 = sch.cnt["vector"]
                        for j in range(4):
                            i = i4 * 4 + j
                            E("vector", lambda v, i=i: v.reduce_sum(
                                stat[:, 56 + i:57 + i], xt[i][:, 512:1024],
                                axis=AX.X))
                            E("scalar", lambda sc, i=i: sc.activation(
                                lnsq, xt[i][:, 512:1024], AF.Square,
                                accum_out=stat[:, 32 + i:33 + i]),
                                at=({"vector": vg7} if j == 0 else None))
                        E("vector", lambda v: v.drain())
            sch.barrier()

            if debug_taps and sg == 0:
                tpa = dbg("x1", [SG, D], F32)
                for i in range(NTIL):
                    store(tpa[i * P:(i + 1) * P, :], xt[i], deps=["vector"])

            if sg == 0:
                if not w0_serial and not ov.get("w0done"):
                    drip(len(w0_steps))
                    w0_post()
                    ov["w0done"] = True
                tgt_w = w0_state["last_st"]
                sch.ops["sync"].append(lambda h, t=tgt_w: h.wait_ge(s_st, t))
            if skip_ffn:
                for i in range(NTIL):
                    store(out[base + i * P: base + (i + 1) * P, :], xt[i],
                          deps=["vector"])
                continue
            sg_pages = []
            for q4 in range(4):
                grpq = WGRP[q4]
                for pg in range(4 * q4, 4 * q4 + 4):
                    tgtq = load(arena[:, pg * 2048:(pg + 1) * 2048],
                                wfct_scr[pg, :, :],
                                deps=(["tensor", "vector", "scalar"]
                                      if pg == 4 * q4 else None), group=grpq)
                sg_pages.append((grpq, tgtq))
            tw = None

            ln_finish("B", seed=0.6, iters=5, split_sums=True)
            if debug_taps and sg == 0:
                tls2 = dbg("ln2stat", [P, 64], F32)
                dbg_store(tls2[:, :], stat)

            prev = {}
            for i in range(NTIL):
                tm = x2ln_tm_b[i % 2]
                tT = x2lnT_b[i % 2]
                E("vector", lambda v, i=i, tm=tm: v.tensor_scalar(
                    tm, xt[i], stat[:, i:i + 1], stat[:, 8 + i:9 + i],
                    op0=ALU.subtract, op1=ALU.mult))
                va = sch.cnt["vector"]
                ev1 = []
                for c4 in range(2):
                    pst = ps_tAb if (c4 == 0) else ps_tBb
                    dep_at = {"vector": va if "t1ev" not in prev
                              else max(va, prev["t1ev"][c4])}
                    for cj in range(4):
                        c = c4 * 4 + cj
                        E("tensor", lambda t, c=c, cj=cj, p=pst, tm=tm:
                          t.transpose(p[:, cj * P:(cj + 1) * P],
                                      tm[:, c * P:(c + 1) * P], identb),
                          at=(dep_at if cj == 0 else None))
                    E("vector", lambda v, c4=c4, p=pst, tT=tT: v.tensor_copy(
                        tT[:, c4 * 512:(c4 + 1) * 512], p[:]),
                        deps=["tensor"])
                    ev1.append(sch.cnt["vector"])
                if i > 0:
                    E("vector", lambda v, j=i - 1: v.scalar_tensor_tensor(
                        xt[j], ps_proj, stat[:, 42:43], xt[j],
                        op0=ALU.mult, op1=ALU.add),
                        at={"tensor": prev["proj"], "scalar": prev["sq"]})
                    prev["evac"] = sch.cnt["vector"]
                    if sg == 0 and not skip_ffn:
                        ov_tile(i - 1)
                        if i >= 2:
                            ov_collect(i - 2)

                sqq_hist = prev.get("sqq_hist", [0, 0])
                for quart in range(4):
                    half, wq = quart // 2, (quart % 2) * 2
                    for c in range(DCH):
                        for w in (wq, wq + 1):
                            at_ = None
                            if c == 0 and w == wq:
                                at_ = {"vector": ev1[1],
                                       "scalar": sqq_hist[-2]}
                                if i == 0:
                                    grpq, tgtq = sg_pages[quart]
                                    wait_ld("tensor", tgtq, group=grpq)
                            j0 = half * 8 + 2 * w
                            rhsv = (arena.rearrange("p (j r) -> p j r", r=2048)
                                    [:, j0:j0 + 2, :]
                                    .rearrange("p j (a u) -> p j a u", u=1024)
                                    [:, :, :, c * P:(c + 1) * P])
                            E("tensor", lambda t, c=c, w=w, half=half, tT=tT,
                              rhsv=rhsv: t.matmul(
                                ps_fc[:, w * 512:(w + 1) * 512],
                                tT[:, c * P:(c + 1) * P], rhsv,
                                start=(c == 0), stop=(c == DCH - 1)),
                                at=at_)
                    gsl = gbuf[:, (quart % 2) * 1024:(quart % 2 + 1) * 1024]
                    at_g = {"tensor": sch.cnt["tensor"]}
                    if quart == 2:
                        at_g["vector"] = prev.get("gcp", 0)
                    E("scalar", lambda s, quart=quart, gsl=gsl, at_g=at_g:
                      s.activation(
                        gsl, ps_fc[:, (quart % 2) * 1024:
                                   (quart % 2 + 1) * 1024],
                        AF.Gelu, scale=0.5,
                        accum_out=accg[:, quart:quart + 1]), at=at_g)
                    if quart == 3:
                        prev["gq3"] = sch.cnt["scalar"]
                    E("scalar", lambda s, quart=quart, gsl=gsl: s.activation(
                        zscr[:, 0:2048][:, (quart % 2) * 1024:
                                        (quart % 2 + 1) * 1024], gsl,
                        AF.Square, accum_out=accg[:, 4 + quart:5 + quart]))
                    sqq_hist.append(sch.cnt["scalar"])
                    if quart == 1:
                        E("vector", lambda v: v.tensor_copy(
                            zscr[:, 2048:4096], gbuf), deps=["scalar"])
                        prev["gcp"] = sch.cnt["vector"]
                prev["sq"] = sch.cnt["scalar"]
                prev["sqq_hist"] = sqq_hist[-2:]
                if sg == 0 and i == NTIL - 1:
                    ov["fc_last"] = sch.cnt["tensor"]
                for wg in (prev["gq3"], prev["sq"]):
                    E("tensor", lambda t, wg=wg: t.matmul(
                        ps_tA[0:32, 320:352], identb[0:32, 0:32],
                        identb[0:32, 0:32], start=True, stop=True),
                        at={"scalar": wg})
                E("vector", lambda v: v.tensor_reduce(
                    stat[:, 40:41], accg[:, 0:4], axis=AX.X, op=ALU.add),
                    at={"scalar": prev["gq3"]})
                E("vector", lambda v: v.drain())
                E("vector", lambda v: v.tensor_scalar(
                    stat[:, 40:41], stat[:, 40:41], 1.0 / H, None,
                    op0=ALU.mult))
                E("vector", lambda v: v.drain())
                vzq = []
                for zq in range(4):
                    if zq < 2:
                        src_z = zscr[:, 2048 + zq * 1024: 3072 + zq * 1024]
                    else:
                        src_z = gbuf[:, (zq - 2) * 1024:(zq - 1) * 1024]
                    E("vector", lambda v, zq=zq, src_z=src_z: v.tensor_scalar(
                        zscr[:, zq * 1024:(zq + 1) * 1024], src_z,
                        stat[:, 40:41], 0.5,
                        op0=ALU.subtract, op1=ALU.mult))
                    vzq.append(sch.cnt["vector"])
                vz0, vz = vzq[1], vzq[3]
                E("vector", lambda v: v.tensor_reduce(
                    stat[:, 41:42], accg[:, 4:8], axis=AX.X, op=ALU.add),
                    at={"scalar": prev["sq"]})
                E("vector", lambda v: v.drain())
                E("vector", lambda v: v.tensor_scalar(
                    stat[:, 41:42], stat[:, 41:42], 1.0 / H, None,
                    op0=ALU.mult))
                E("vector", lambda v: v.tensor_tensor(
                    stat[:, 42:43], stat[:, 40:41], stat[:, 40:41],
                    op=ALU.mult))
                E("vector", lambda v: v.drain())
                E("vector", lambda v: v.tensor_tensor(
                    stat[:, 41:42], stat[:, 41:42], stat[:, 42:43],
                    op=ALU.subtract))
                E("vector", lambda v: v.drain())
                E("vector", lambda v: v.tensor_scalar(
                    stat[:, 41:42], stat[:, 41:42], 1.0, LN_EPS,
                    op0=ALU.mult, op1=ALU.add))
                E("vector", lambda v: v.drain())
                evac_cnt = []
                for b4 in range(8):
                    pst = ps_zt[b4 % 4][:, 0:512]
                    need_z = vzq[b4 // 2]
                    dep_at = {"vector": (need_z if b4 < 4
                                         else max(need_z, evac_cnt[b4 - 4])),
                              "scalar": prev["sq"]}
                    for j in range(4):
                        hc = b4 * 4 + j
                        E("tensor", lambda t, hc=hc, j=j, p=pst: t.transpose(
                            p[:, j * P:(j + 1) * P],
                            zscr[:, hc * P:(hc + 1) * P], identb),
                            at=(dep_at if j == 0 else None))
                    E("vector", lambda v, b4=b4, p=pst: v.tensor_copy(
                        zT[:, b4 * 512:(b4 + 1) * 512], p[:]), deps=["tensor"])
                    evac_cnt.append(sch.cnt["vector"])
                prev["t1ev"] = [0, ev1[1]]
                newton_rs(stat[:, 41:42], stat[:, 42:43], 0.064, 4)
                E("vector", lambda v: v.tensor_tensor(
                    scol, stat[:, 41:42], stat[:, 42:43], op=ALU.mult))
                at_p = {"vector": evac_cnt[-1]}
                if "evac" in prev:
                    at_p["vector"] = max(at_p["vector"], prev["evac"])
                for hc in range(HCH):
                    for w in range(2):
                        E("tensor", lambda t, hc=hc, w=w: t.matmul(
                            ps_proj[:, w * 512:(w + 1) * 512],
                            zT[:, hc * P:(hc + 1) * P],
                            wpT[hc][:, w * 512:(w + 1) * 512],
                            start=(hc == 0), stop=False),
                            at=(at_p if (hc == 0 and w == 0) else None))
                E("tensor", lambda t: t.transpose(
                    ps_tAb[0:1, 0:P], scol, identb), deps=["vector"])
                E("vector", lambda v: v.tensor_copy(
                    srow, ps_tAb[0:1, 0:P]), deps=["tensor"])
                prev["t1ev"][0] = sch.cnt["vector"]
                for w in range(2):
                    E("tensor", lambda t, w=w: t.matmul(
                        ps_proj[:, w * 512:(w + 1) * 512], srow,
                        cvec[:, w * 512:(w + 1) * 512], start=False,
                        stop=(w == 1)), deps=(["vector"] if w == 0 else None))
                prev["proj"] = sch.cnt["tensor"]
            E("vector", lambda v: v.scalar_tensor_tensor(
                xt[NTIL - 1], ps_proj, stat[:, 42:43], xt[NTIL - 1],
                op0=ALU.mult, op1=ALU.add),
                at={"tensor": prev["proj"]})
            if sg == 0:
                ov_tile(NTIL - 1)
                ov_collect(NTIL - 2)
                ov_collect(NTIL - 1)
                th_l = load(xh[0:4, :], xin[SG + 0: SG + 4, :],
                            group="small")
                th_r = load(xh[4:8, :], xin[2 * SG + 4: 2 * SG + 8, :],
                            group="small")
                ln_finish("B", seed=0.6, iters=5)
                wait_ld2("vector", th_l)
                wait_ld2("vector", th_r)
                halo_ln("newton", gate_t=ov["fc_last"])
                a_prep(gate_t=ov["fc_last"])
            else:
                sch.barrier()
                for i in range(NTIL):
                    store(out[base + i * P: base + (i + 1) * P, :], xt[i],
                          deps=["vector"])

        sch.ops["gpsimd"].append(lambda h: h.wait_ge(s_st, st_cnt[0] * 16))

        with nc.Block() as block:
            @block.sync
            def _(sync):
                for op in sch.ops["sync"]:
                    op(sync)

            @block.tensor
            def _(tensor):
                for op in sch.ops["tensor"]:
                    op(tensor)

            @block.vector
            def _(vector):
                for op in sch.ops["vector"]:
                    op(vector)

            @block.scalar
            def _(scalar):
                for op in sch.ops["scalar"]:
                    op(scalar)

            @block.gpsimd
            def _(gpsimd):
                for op in sch.ops["gpsimd"]:
                    op(gpsimd)

    return nc, taps



def _pack_inputs(L0_attn_query, L0_attn_conv_weight, L0_ffn_W_fc,
                 L0_ffn_ln_w, L0_ffn_ln_b, L0_ffn_W_proj):
    x = np.asarray(L0_attn_query, dtype=np.float32)
    cw = np.asarray(L0_attn_conv_weight, dtype=np.float32)
    wfc = np.ascontiguousarray(np.asarray(L0_ffn_W_fc, dtype=np.float32))
    wproj = np.ascontiguousarray(np.asarray(L0_ffn_W_proj, dtype=np.float32))
    lnw = np.asarray(L0_ffn_ln_w, dtype=np.float32)
    lnb = np.asarray(L0_ffn_ln_b, dtype=np.float32)

    B, S, _ = x.shape
    Q = S // 4
    cwp = np.zeros((P, DCH * K9 * 18), np.float32)
    for c in range(DCH):
        for k in range(K9):
            for hh in range(2):
                h = 2 * c + hh
                for j in range(K9):
                    cwp[hh * HS:(hh + 1) * HS, (c * K9 + k) * 18 + hh * K9 + j] = \
                        cw[h * K9 + j, :, k]
    lnw_d = np.ascontiguousarray(lnw.reshape(HCH, P).T)
    lnb_d = np.ascontiguousarray(lnb.reshape(HCH, P).T)
    identb = np.eye(P, dtype=ml_dtypes.bfloat16)
    identf = np.eye(P, dtype=np.float32)
    onesf = np.ones((P, P), np.float32)
    selp = np.zeros((16, NPAIR * P), ml_dtypes.bfloat16)
    for pair in range(NPAIR):
        selp[2 * pair, pair * P: pair * P + HS] = 1
        selp[2 * pair + 1, pair * P + HS: (pair + 1) * P] = 1

    in_maps = []
    for b in range(B):
        for q in range(4):
            lo, hi = q * Q, (q + 1) * Q
            sl = x[b, max(0, lo - 4):min(S, hi + 4), :]
            padl = np.zeros((4 - (lo - max(0, lo - 4)), D), np.float32)
            padr = np.zeros((4 - (min(S, hi + 4) - hi), D), np.float32)
            shard = np.ascontiguousarray(np.concatenate([padl, sl, padr], axis=0))
            in_maps.append({
                "xin": shard, "wfc": wfc, "wproj": wproj,
                "lnw_d": lnw_d, "lnb_d": lnb_d, "cwpack": cwp,
                "identb_d": identb, "identf_d": identf, "onesf_d": onesf,
                "selp_d": selp,
            })
    return in_maps


_cached = {}


def kernel(L0_attn_query, L0_attn_conv_weight, L0_ffn_W_fc,
           L0_ffn_ln_w, L0_ffn_ln_b, L0_ffn_W_proj):
    in_maps = _pack_inputs(L0_attn_query, L0_attn_conv_weight, L0_ffn_W_fc,
                           L0_ffn_ln_w, L0_ffn_ln_b, L0_ffn_W_proj)
    if "nc" not in _cached:
        _cached["nc"] = build_program(debug_taps=False)[0]
    res = run_bass_kernel_spmd(_cached["nc"], in_maps, core_ids=list(range(8)))
    B, S = 2, 8192
    Q = S // 4
    outp = np.empty((B, S, D), np.float32)
    ci = 0
    for b in range(B):
        for q in range(4):
            outp[b, q * Q:(q + 1) * Q, :] = res.results[ci]["out"]
            ci += 1
    return outp
